# version 18
# baseline (speedup 1.0000x reference)
import numpy as np
import concourse.bass as bass
import concourse.mybir as mybir
from concourse.bass_utils import run_bass_kernel_spmd

F32 = mybir.dt.float32
BF16 = mybir.dt.bfloat16
I32 = mybir.dt.int32
AF = mybir.ActivationFunctionType
ALU = mybir.AluOpType
AX = mybir.AxisListType

D = 1024
NCH = 8
TILE = 512
NROLE = 4
QCH = 128
NEG = -1.0e38

ENGINES = ("sync", "tensor", "vector", "scalar", "gpsimd")


class Sched:

    def __init__(self):
        self.ops = {e: [] for e in ENGINES}
        self.cnt = {}

    def do(self, eng, fn, wait=(), inc=None, amt=1, drain=False):
        tok = None
        if inc is not None:
            self.cnt[inc] = self.cnt.get(inc, 0) + amt
            tok = self.cnt[inc]
        self.ops[eng].append((tuple(wait), fn, inc, amt, drain))
        return tok

    def dma(self, eng, out, in_, wait=(), inc="ld", slow=False):
        if slow:
            return self.do(eng, (lambda e, o=out, i=in_:
                                 e.dma_start(out=o, in_=i, allow_slow_non_contiguous=True)),
                           wait=wait, inc=inc, amt=16)
        return self.do(eng, (lambda e, o=out, i=in_: e.dma_start(out=o, in_=i)),
                       wait=wait, inc=inc, amt=16)

    def val(self, sem):
        return self.cnt.get(sem, 0)

    def replay(self, eng_name, eng, sems, guard=False):
        chain = guard and eng_name in ("vector", "scalar")
        osem = "o" + eng_name if chain else None
        own = "VE" if eng_name == "vector" else ("SE" if eng_name == "scalar" else None)
        n = 0
        nown = 0
        for wait, fn, inc, amt, drain in self.ops[eng_name]:
            for (s, v) in wait:
                if v > 0:
                    eng.wait_ge(sems[s], v)
            if chain:
                if n > 0:
                    eng.wait_ge(sems[osem], n)
                if nown > 0:
                    eng.wait_ge(sems[own], nown)
            instr = fn(eng)
            if drain:
                instr = eng.drain()
            if inc is not None:
                assert instr is not None, f"inc {inc} on op with no instruction"
                instr.then_inc(sems[inc], amt)
                if inc == own:
                    nown += amt
            if chain and instr is not None and inc is None:
                instr.then_inc(sems[osem], 1)
                n += 1


def build_core_program(nt, debug=False, guard=False):
    nc = bass.Bass("TRN2", target_bir_lowering=False,
                   detect_race_conditions=guard)
    T = nt * TILE
    NOWN = nt * QCH

    def din(name, shape, dt=F32):
        return nc.dram_tensor(name, shape, dt, kind="ExternalInput").ap()

    def dout_(name, shape, dt=F32):
        return nc.dram_tensor(name, shape, dt, kind="ExternalOutput").ap()

    def dtmp(name, shape, dt=F32):
        return nc.dram_tensor(name, shape, dt).ap()

    xin = din("xin", [NOWN, D])
    wq_d = din("wq", [D, D]); wk_d = din("wk", [D, D]); wv_d = din("wv", [D, D])
    wo_d = din("wo", [D, D]); frhs = din("frhs", [D, D])
    bq_d = din("bq", [D]); bk_d = din("bk", [D]); bv_d = din("bv", [D]); bo_d = din("bo", [D])
    maskin = din("maskin", [QCH, TILE])
    idf = din("idf", [128, 128])
    idb = din("idb", [128, 128], BF16)
    ones_in = din("ones_col", [1, 128])

    xout = dout_("xout", [NOWN, D])

    NS = nt // 4
    contrib = dtmp("contrib", [NS, 512, NCH * TILE], BF16)
    gath = dtmp("gath", [NS, 4 * 512, NCH * TILE], BF16)
    qx_hi = dtmp("qx_hi", [NS, 128, NCH * TILE], BF16)
    qx_lo = dtmp("qx_lo", [NS, 128, NCH * TILE], BF16)
    x1s = dtmp("x1s", [nt, 128, D])
    ctxs = dtmp("ctxs", [nt, 128, D])
    rs_bounce = dtmp("rs_bounce", [8, 128])
    wo_scr = dtmp("wo_scr", [2, 128, NCH * D], BF16)
    bo_bounce = dtmp("bo_bounce", [D])

    dbg = {}
    if debug:
        dbg["d_x1"] = dout_("d_x1", [nt, 128, D])
        dbg["d_k"] = dout_("d_k", [nt, 128, NCH * TILE], BF16)
        dbg["d_v"] = dout_("d_v", [nt, 128, 4 * D], BF16)
        dbg["d_q"] = dout_("d_q", [nt, 128, NCH * QCH], BF16)
        dbg["d_ctx"] = dout_("d_ctx", [nt, 128, D])
        dbg["d_l"] = dout_("d_l", [128, nt])
        dbg["d_rsbc"] = dout_("d_rsbc", [128, D])
        dbg["d_qrhs"] = dout_("d_qrhs", [128, NCH * D], BF16)
        dbg["d_qtm"] = dout_("d_qtm", [128, D], BF16)
        dbg["d_y"] = dout_("d_y", [128, D])
        dbg["d_wt2"] = dout_("d_wt2", [128, NCH * D])
        dbg["d_qrT"] = dout_("d_qrT", [128, NCH * D], BF16)
        dbg["d_cg"] = dout_("d_cg", [128, 8])
        dbg["d_ch"] = dout_("d_ch", [128, 8])

    def sb(name, shape, dt=F32):
        return nc.alloc_sbuf_tensor(name, shape, dt)

    q_rhs = sb("q_rhs", [128, NCH * D], BF16)
    rs_bc = sb("rs_bc", [128, D])
    mask = sb("mask", [QCH, TILE])
    s_idf = sb("s_idf", [128, 128])
    s_idb = sb("s_idb", [128, 128], BF16)
    s_ones = sb("s_ones", [1, 128])
    bq32c = sb("bq32c", [128, 8]); bkc = sb("bkc", [128, 8])
    bvc = sb("bvc", [128, 8]); boc = sb("boc", [128, 8]); bopc = sb("bopc", [128, 8])
    epsc = sb("epsc", [128, 1])
    l_col = sb("l_col", [128, max(nt, 2)])
    nm_col = sb("nm_col", [128, max(nt, 2)])
    li_col = sb("li_col", [128, max(nt, 2)])
    c_a = sb("c_a", [128, 2]); c_b = sb("c_b", [128, 2])
    c_c = sb("c_c", [128, 4]); c_d = sb("c_d", [128, 4])
    c_e = sb("c_e", [128, 4]); c_f = sb("c_f", [128, 4])
    c_g = sb("c_g", [128, 8]); c_h = sb("c_h", [128, 8])

    base_R = nc.sbuf_base

    wq_hi = sb("wq_hi", [128, NCH * D], BF16); wq_lo = sb("wq_lo", [128, NCH * D], BF16)
    wk_hi = sb("wk_hi", [128, NCH * D], BF16); wk_lo = sb("wk_lo", [128, NCH * D], BF16)
    wv_hi = sb("wv_hi", [128, NCH * D], BF16); wv_lo = sb("wv_lo", [128, NCH * D], BF16)
    base_A2 = nc.sbuf_base
    wtmp = sb("wtmp", [128, NCH * D])
    wtmp2 = sb("wtmp2", [128, NCH * D])
    qrT = sb("qrT", [128, NCH * D], BF16)
    nc.sbuf_base = base_A2
    a_x = sb("a_x", [128, 2 * D])
    a_y = sb("a_y", [128, D])
    a_i = sb("a_i", [128, D], I32)
    a_qtm = sb("a_qtm", [128, D], BF16)
    a_qfm = sb("a_qfm", [128, D], BF16)
    a_qf = sb("a_qf", [128, TILE])
    a_x1 = sb("a_x1", [128, D])
    z2buf = sb("z2buf", [128, 2 * 4 * D])
    a_z2 = sb("a_z2", [128, D])
    a_nhi = sb("a_nhi", [128, NCH * TILE], BF16)
    a_nlo = sb("a_nlo", [128, NCH * TILE], BF16)
    a_kvhi = sb("a_kvhi", [128, 2 * TILE], BF16)
    a_kvlo = sb("a_kvlo", [128, 2 * TILE], BF16)
    endA = nc.sbuf_base
    assert endA - base_R <= 186 * 1024, f"A over: {(endA - base_R)/1024}KB"

    nc.sbuf_base = base_R
    b_ctx = sb("b_ctx", [128, nt * D])
    b_khi = sb("b_khi", [128, 3 * NCH * TILE], BF16)
    b_klo = sb("b_klo", [128, 3 * NCH * TILE], BF16)
    b_vhi = sb("b_vhi", [128, 3 * 4 * D], BF16)
    b_vlo = sb("b_vlo", [128, 3 * 4 * D], BF16)
    b_qhi = sb("b_qhi", [128, 2 * NCH * QCH], BF16)
    b_qlo = sb("b_qlo", [128, 2 * NCH * QCH], BF16)
    b_p = sb("b_p", [128, 2 * TILE])
    b_pthi = sb("b_pthi", [128, 4 * QCH], BF16)
    b_ptlo = sb("b_ptlo", [128, 4 * QCH], BF16)
    endB = nc.sbuf_base
    assert endB - base_R <= 186 * 1024, f"B over: {(endB - base_R)/1024}KB"

    nc.sbuf_base = base_R
    wo_hi = sb("wo_hi", [128, NCH * D], BF16); wo_lo = sb("wo_lo", [128, NCH * D], BF16)
    c_ctx = sb("c_ctx", [128, 2 * 4 * D])
    c_ctxn = sb("c_ctxn", [128, D])
    c_fhi = sb("c_fhi", [128, NCH * TILE], BF16)
    c_flo = sb("c_flo", [128, NCH * TILE], BF16)
    c_y = sb("c_y", [128, NCH * TILE])
    c_x1g = sb("c_x1g", [128, 2 * 4 * D])
    c_x2 = sb("c_x2", [128, 2 * 4 * D])
    c_z = sb("c_z", [128, D]); c_yq = sb("c_yq", [128, D])
    c_ii = sb("c_ii", [128, D], I32)
    c_qtm = sb("c_qtm", [128, 2 * D], BF16); c_qfm = sb("c_qfm", [128, D], BF16)
    endC = nc.sbuf_base
    assert endC - base_R <= 186 * 1024, f"C over: {(endC - base_R)/1024}KB"

    def psn(name, shape, dt=F32):
        return nc.alloc_psum_tensor(name, shape, dt)

    ps0 = nc.psum_base
    pF0 = psn("pF0", [128, 512]); pF1 = psn("pF1", [128, 512])
    pTb = psn("pTb", [128, 1024], BF16)
    pTf0 = psn("pTf0", [128, 512]); pTf1 = psn("pTf1", [128, 512])
    pA0 = psn("pA0", [128, 512]); pA1 = psn("pA1", [128, 512])
    pX = psn("pX", [128, 512])
    nc.psum_base = ps0
    pS0 = psn("pS0", [128, 512]); pS1 = psn("pS1", [128, 512])
    pPT0 = psn("pPT0", [128, 512]); pPT1 = psn("pPT1", [128, 512])
    pC0 = psn("pC0", [128, 1024]); pC1 = psn("pC1", [128, 1024])
    nc.psum_base = ps0
    pO0 = psn("pO0", [128, 512]); pO1 = psn("pO1", [128, 512])
    pCT0 = psn("pCT0", [128, 512]); pCT1 = psn("pCT1", [128, 512])
    pCF0 = psn("pCF0", [128, 1024]); pCb = psn("pCb", [128, 2048], BF16)

    S = Sched()
    A = lambda h: h.ap()

    def mm3(psum_ap, pairs, wait_first=(), mark=False):
        n = len(pairs)
        tok = None
        for i, (lh, ll, rh, rl) in enumerate(pairs):
            for j, (a_, b_) in enumerate(((lh, rh), (lh, rl), (ll, rh))):
                first = (i == 0 and j == 0)
                last = (i == n - 1 and j == 2)
                tok = S.do("tensor",
                           (lambda e, a_=a_, b_=b_, first=first, last=last, p=psum_ap:
                            e.matmul(p, a_, b_, start=first, stop=last)),
                           wait=(wait_first if first else ()),
                           inc=("TE" if (last and mark) else None))
        return tok

    def emit_ln(x_ap, z_ap, wait_x_v=(), wait_x_s=(), sq=None):
        S.do("vector", lambda e, x=x_ap: e.tensor_reduce(A(c_a)[:, 0:1], x, axis=AX.X, op=ALU.add),
             wait=wait_x_v, drain=True)
        S.do("vector", lambda e: e.tensor_scalar(A(c_a)[:, 0:1], A(c_a)[:, 0:1], -1.0 / D, None, ALU.mult),
             inc="VE", drain=True)
        vm = S.val("VE")
        S.do("scalar", lambda e, x=x_ap, q=sq: e.activation(q, x, AF.Square, bias=A(c_a)[:, 0:1],
                                                      scale=1.0, accum_out=A(c_b)[:, 0:1]),
             wait=tuple(wait_x_s) + (("VE", vm),), inc="SE", drain=True)
        S.do("scalar", lambda e: e.activation(A(c_c)[:, 0:1], A(c_b)[:, 0:1], AF.Ln,
                                              bias=A(epsc), scale=1.0 / D), drain=True)
        S.do("scalar", lambda e: e.activation(A(c_c)[:, 0:1], A(c_c)[:, 0:1], AF.Exp,
                                              bias=0.0, scale=-0.5), inc="SE")
        sm = S.val("SE")
        S.do("vector", lambda e: e.tensor_tensor(A(c_d)[:, 0:1], A(c_a)[:, 0:1], A(c_c)[:, 0:1], ALU.mult),
             wait=[("SE", sm)], inc="VE", drain=True)
        vm2 = S.val("VE")
        tok = S.do("scalar", lambda e, x=x_ap, z=z_ap: e.activation(z, x, AF.Identity,
                                                                    bias=A(c_d)[:, 0:1], scale=A(c_c)[:, 0:1]),
                   wait=[("VE", vm2)], inc="SE")
        return tok

    def emit_lnq(x_ap, qtm_ap, ls_col, st, wait_x_v=(), wait_x_s=(),
                 ybuf=None, ibuf=None, dbuf=None, prev_tok=None):
        G = lambda i: A(c_g)[:, st * 4 + i: st * 4 + i + 1]
        H = lambda i: A(c_h)[:, st * 4 + i: st * 4 + i + 1]
        w0 = list(wait_x_v)
        if prev_tok is not None:
            w0.append(("SE", prev_tok))
        S.do("vector", lambda e, x=x_ap: e.tensor_reduce(G(0), x, axis=AX.X, op=ALU.add),
             wait=w0, drain=True)
        S.do("vector", lambda e, x=x_ap: e.tensor_reduce(G(1), x, axis=AX.X, op=ALU.max),
             drain=True)
        S.do("vector", lambda e, x=x_ap: e.tensor_reduce(G(2), x, axis=AX.X, op=ALU.min),
             drain=True)
        S.do("vector", lambda e: e.tensor_scalar(G(3), G(0), -1.0 / D, None, ALU.mult), drain=True)
        S.do("vector", lambda e: e.tensor_tensor(G(1), G(1), G(3), ALU.add), drain=True)
        S.do("vector", lambda e: e.tensor_tensor(G(2), G(2), G(3), ALU.add), drain=True)
        S.do("vector", lambda e: e.scalar_tensor_tensor(H(0), G(2), -1.0, G(1), ALU.mult, ALU.max),
             drain=True)
        S.do("vector", lambda e: e.tensor_scalar(H(1), H(0), 1e-12, None, ALU.max), drain=True)
        S.do("vector", lambda e: e.reciprocal(H(2), H(1)), drain=True)
        S.do("vector", lambda e: e.tensor_scalar(H(1), H(2), 127.0, None, ALU.mult), drain=True)
        vf = S.do("vector", lambda e: e.tensor_tensor(H(2), G(3), H(1), ALU.mult),
                  inc="VE", drain=True)
        sy = S.do("scalar", lambda e, x=x_ap, y=ybuf: e.activation(y, x, AF.Identity,
                                                                   bias=H(2), scale=H(1)),
                  wait=tuple(wait_x_s) + (("VE", vf),), inc="SE", drain=True)
        S.do("vector", lambda e, y=ybuf, ib=ibuf: e.tensor_copy(ib, y), wait=[("SE", sy)])
        S.do("vector", lambda e, y=ybuf, ib=ibuf, db=dbuf: e.tensor_tensor(db, y, ib, ALU.subtract))
        S.do("vector", lambda e, y=ybuf, db=dbuf: e.tensor_scalar(y, db, -0.5, None, ALU.is_le))
        S.do("vector", lambda e, y=ybuf, db=dbuf: e.scalar_tensor_tensor(db, db, 0.5, y,
                                                                         ALU.is_ge, ALU.subtract))
        qtok = S.do("vector", lambda e, q=qtm_ap, ib=ibuf, db=dbuf: e.tensor_tensor(q, db, ib, ALU.add),
                    inc="VE")
        S.do("scalar", lambda e, x=x_ap, d=dbuf: e.activation(d, x, AF.Square, bias=G(3),
                                                              scale=1.0, accum_out=H(3)),
             wait=[("VE", qtok)], inc="SE", drain=True)
        S.do("scalar", lambda e: e.activation(H(3), H(3), AF.Ln, bias=A(epsc), scale=1.0 / D),
             drain=True)
        se2 = S.do("scalar", lambda e: e.activation(H(3), H(3), AF.Exp, bias=0.0, scale=-0.5),
                   inc="SE", drain=True)
        S.do("vector", lambda e: e.tensor_tensor(H(0), H(0), H(3), ALU.mult),
             wait=[("SE", se2)], drain=True)
        S.do("vector", lambda e, lsc=ls_col: e.tensor_scalar(lsc, H(0), 1e-12, None, ALU.max),
             inc="VE", drain=True)
        return qtok, se2

    def emit_quant(z_ap, qtm_ap, wait_z_v, scale_col, ybuf=None, ibuf=None, dbuf=None):
        S.do("vector", lambda e, z=z_ap: e.reduce_max(A(c_e)[:, 0:1], z, axis=AX.X,
                                                      apply_absolute_value=True),
             wait=wait_z_v, drain=True)
        S.do("vector", lambda e: e.tensor_scalar(scale_col, A(c_e)[:, 0:1], 1e-12, None, ALU.max),
             drain=True)
        S.do("vector", lambda e: e.reciprocal(A(c_f)[:, 0:1], scale_col), drain=True)
        S.do("vector", lambda e: e.tensor_scalar(A(c_f)[:, 0:1], A(c_f)[:, 0:1], 127.0, None, ALU.mult),
             inc="VE", drain=True)
        vm = S.val("VE")
        S.do("scalar", lambda e, z=z_ap, y=ybuf: e.activation(y, z, AF.Identity, bias=0.0,
                                                      scale=A(c_f)[:, 0:1]),
             wait=[("VE", vm)], inc="SE")
        sm = S.val("SE")
        S.do("vector", lambda e, y=ybuf, ib=ibuf: e.tensor_copy(ib, y), wait=[("SE", sm)])
        S.do("vector", lambda e, y=ybuf, ib=ibuf, db=dbuf: e.tensor_tensor(db, y, ib, ALU.subtract))
        S.do("vector", lambda e, y=ybuf, db=dbuf: e.tensor_scalar(y, db, -0.5, None, ALU.is_le))
        S.do("vector", lambda e, y=ybuf, db=dbuf: e.scalar_tensor_tensor(db, db, 0.5, y,
                                                                         ALU.is_ge, ALU.subtract))
        tok = S.do("vector", lambda e, q=qtm_ap, ib=ibuf, db=dbuf: e.tensor_tensor(q, db, ib, ALU.add),
                   inc="VE")
        return tok

    S.dma("sync", A(mask), maskin, inc="lc")
    S.dma("sync", A(s_idf), idf, inc="lc")
    S.dma("sync", A(s_idb), idb, inc="lc")
    S.dma("sync", A(s_ones), ones_in, inc="lc")
    for (dst, src) in ((bq32c, bq_d), (bkc, bk_d), (bvc, bv_d), (boc, bo_d)):
        S.dma("sync", A(dst), src.rearrange("(c p) -> p c", p=128), slow=True, inc="lc")
    t_consts = S.val("lc")
    S.do("vector", lambda e: e.memset(A(epsc), 1e-5))
    S.do("vector", lambda e: e.tensor_scalar(A(bq32c), A(bq32c), 1.0 / 32.0, None, ALU.mult),
         wait=[("lc", t_consts)], inc="VE")

    def split_w(w_dram, hi_sb, lo_sb, prev_v):
        t = S.dma("sync", A(wtmp).rearrange("p (c d) -> p c d", c=8),
                  w_dram.rearrange("(c p) d -> p c d", p=128),
                  wait=([("VE", prev_v)] if prev_v else ()), inc="lw")
        S.do("vector", lambda e, h=hi_sb: e.tensor_copy(A(h), A(wtmp)), wait=[("lw", t)])
        tl = S.do("vector", lambda e, h=hi_sb, l=lo_sb: e.tensor_tensor(A(l), A(wtmp), A(h), ALU.subtract),
                  inc="VE")
        return tl

    v1 = split_w(wq_d, wq_hi, wq_lo, None)
    v2 = split_w(wk_d, wk_hi, wk_lo, v1)
    v3 = split_w(wv_d, wv_hi, wv_lo, v2)

    t_rhs = S.dma("sync", A(wtmp).rearrange("p (c d) -> p c d", c=8),
                  frhs.rearrange("(c p) d -> p c d", p=128),
                  wait=[("VE", v3)], inc="lw")

    def transpose_pack_f32(src_slices, dst_slices, bank, wait_first, evac_eng="vector"):
        for j in range(4):
            w = list(wait_first) if j == 0 else []
            S.do("tensor",
                 (lambda e, j=j, s=src_slices[j], bank=bank:
                  e.transpose(A(bank)[:, j * 128:(j + 1) * 128], s, A(s_idf))),
                 wait=w, inc=("TE" if j == 3 else None))
        tm = S.val("TE")
        tok = None
        for j in range(4):
            tok = S.do(evac_eng,
                       (lambda e, j=j, d=dst_slices[j], bank=bank:
                        e.tensor_copy(d, A(bank)[:, j * 128:(j + 1) * 128])),
                       wait=([("TE", tm)] if j == 0 else ()),
                       inc=("VE" if evac_eng == "vector" else "SE"))
        return tok

    last_ev = None
    for g in range(16):
        bank = pTf0 if g % 2 == 0 else pTf1
        wf = [("lw", t_rhs)]
        if last_ev is not None and g >= 2:
            wf.append(("VE", S.val("VE")))
        srcs, dsts = [], []
        for j in range(4):
            idx = g * 4 + j
            di, do = idx // 8, idx % 8
            srcs.append(A(wtmp)[:, di * 1024 + do * 128: di * 1024 + (do + 1) * 128])
            dsts.append(A(wtmp2)[:, do * 1024 + di * 128: do * 1024 + (di + 1) * 128])
        last_ev = transpose_pack_f32(srcs, dsts, bank, wf)

    if debug:
        S.dma("gpsimd", dbg["d_wt2"], A(wtmp2), wait=[("VE", last_ev)], inc="sdbg")
    for cdo in range(8):
        S.do("vector",
             (lambda e, cdo=cdo: e.reduce_max(A(c_g)[:, cdo:cdo + 1],
                                              A(wtmp2)[:, cdo * 1024:(cdo + 1) * 1024],
                                              axis=AX.X, apply_absolute_value=True)),
             wait=([("VE", last_ev)] if cdo == 0 else ()), drain=(cdo == 7))
    S.do("vector", lambda e: e.tensor_scalar(A(c_g), A(c_g), 1e-12, None, ALU.max), drain=True)
    for _rc in range(8):
        S.do("vector", (lambda e, c=_rc: e.reciprocal(A(c_h)[:, c:c + 1], A(c_g)[:, c:c + 1])),
             drain=(_rc == 7))
    S.do("vector", lambda e: e.tensor_scalar(A(c_h), A(c_h), 127.0, None, ALU.mult), inc="VE",
         drain=True)
    vq = S.val("VE")
    if debug:
        S.dma("gpsimd", dbg["d_cg"], A(c_g), wait=[("VE", vq)], inc="sdbg")
        S.dma("gpsimd", dbg["d_ch"], A(c_h), wait=[("VE", vq)], inc="sdbg")
    for cdo in range(8):
        S.do("scalar",
             (lambda e, cdo=cdo: e.activation(A(wtmp)[:, 0:1024],
                                              A(wtmp2)[:, cdo * 1024:(cdo + 1) * 1024],
                                              AF.Identity, bias=0.0, scale=A(c_h)[:, cdo:cdo + 1])),
             wait=([("VE", vq)] if cdo == 0 else [("VE", S.val("VE"))]), inc="SE")
        sm = S.val("SE")
        S.do("vector", lambda e: e.tensor_copy(A(a_i), A(wtmp)[:, 0:1024]), wait=[("SE", sm)])
        S.do("vector", lambda e: e.tensor_tensor(A(wtmp)[:, 2048:3072], A(wtmp)[:, 0:1024], A(a_i), ALU.subtract))
        S.do("vector", lambda e: e.tensor_scalar(A(wtmp)[:, 1024:2048], A(wtmp)[:, 2048:3072], -0.5, None, ALU.is_le))
        S.do("vector", lambda e: e.scalar_tensor_tensor(A(wtmp)[:, 2048:3072], A(wtmp)[:, 2048:3072], 0.5,
                                                        A(wtmp)[:, 1024:2048], ALU.is_ge, ALU.subtract))
        S.do("vector",
             (lambda e, cdo=cdo: e.tensor_tensor(A(qrT)[:, cdo * 1024:(cdo + 1) * 1024],
                                                 A(wtmp)[:, 2048:3072], A(a_i), ALU.add)),
             inc="VE")
    vq2 = S.val("VE")
    if debug:
        S.dma("gpsimd", dbg["d_qrT"], A(qrT), wait=[("VE", vq2)], inc="sdbg")
    last_ev = None
    for g in range(8):
        wf = [("VE", vq2)]
        if last_ev is not None and g >= 1:
            wf.append(("VE", S.val("VE")))
        for j in range(8):
            idx = g * 8 + j
            do, di = idx // 8, idx % 8
            do = g
            di = j
            S.do("tensor",
                 (lambda e, do=do, di=di, j=j:
                  e.transpose(A(pTb)[:, j * 128:(j + 1) * 128],
                              A(qrT)[:, do * 1024 + di * 128: do * 1024 + (di + 1) * 128],
                              A(s_idb))),
                 wait=(wf if j == 0 else ()), inc=("TE" if j == 7 else None))
        tm = S.val("TE")
        for j in range(8):
            di = j
            do = g
            last_ev = S.do("vector",
                           (lambda e, di=di, do=do, j=j:
                            e.tensor_copy(A(q_rhs)[:, di * 1024 + do * 128: di * 1024 + (do + 1) * 128],
                                          A(pTb)[:, j * 128:(j + 1) * 128])),
                           wait=([("TE", tm)] if j == 0 else ()), inc="VE")
    S.do("vector", lambda e: e.tensor_scalar(A(c_g), A(c_g), 1.0 / 127.0, None, ALU.mult),
         inc="VE")
    v_rs = S.val("VE")
    t_st = S.dma("gpsimd", rs_bounce.rearrange("c p -> p c"), A(c_g), wait=[("VE", v_rs)], inc="sb", slow=True)
    t_ld = S.dma("sync", A(wtmp)[0:1, 3072:4096],
                 rs_bounce.rearrange("c p -> (c p)").rearrange("(a b) -> a b", a=1),
                 wait=[("sb", t_st)], inc="lw")
    S.do("tensor", lambda e: e.matmul(A(pX), A(s_ones), A(wtmp)[0:1, 3072:3584], start=True, stop=True),
         wait=[("lw", t_ld), ("VE", S.val("VE"))])
    S.do("tensor", lambda e: e.matmul(A(pA0), A(s_ones), A(wtmp)[0:1, 3584:4096], start=True, stop=True),
         inc="TE")
    tm = S.val("TE")
    S.do("vector", lambda e: e.tensor_copy(A(rs_bc)[:, 0:512], A(pX)), wait=[("TE", tm)])
    v_rsbc = S.do("vector", lambda e: e.tensor_copy(A(rs_bc)[:, 512:1024], A(pA0)), inc="VE")
    if debug:
        S.dma("gpsimd", dbg["d_rsbc"], A(rs_bc), wait=[("VE", v_rsbc)], inc="sdbg")
        S.dma("gpsimd", dbg["d_qrhs"], A(q_rhs), wait=[("VE", S.val("VE"))], inc="sdbg")
        S.dma("gpsimd", dbg["d_qrhs"] if "d_qrhs" in dbg else xout, A(q_rhs), inc="st") if "d_qrhs" in dbg else None

    t_wo = S.dma("sync", A(wtmp).rearrange("p (c d) -> p c d", c=8),
                 wo_d.rearrange("(c p) d -> p c d", p=128),
                 wait=[("VE", S.val("VE")), ("SE", S.val("SE"))], inc="lw")
    for h in range(2):
        wv_h = [("lw", t_wo)] if h == 0 else [("swo", S.val("swo"))]
        S.do("vector",
             (lambda e, h=h: e.tensor_copy(A(qrT)[:, 0:4096], A(wtmp)[:, h * 4096:(h + 1) * 4096])),
             wait=wv_h)
        v_h = S.do("vector",
                   (lambda e, h=h: e.tensor_tensor(A(qrT)[:, 4096:8192],
                                                   A(wtmp)[:, h * 4096:(h + 1) * 4096],
                                                   A(qrT)[:, 0:4096], ALU.subtract)), inc="VE")
        S.dma("gpsimd", wo_scr[0][:, h * 4096:(h + 1) * 4096], A(qrT)[:, 0:4096],
              wait=[("VE", v_h)], inc="swo")
        t_wl = S.dma("gpsimd", wo_scr[1][:, h * 4096:(h + 1) * 4096], A(qrT)[:, 4096:8192],
                     inc="swo")
    v_wo = S.val("VE")
    for dh in range(2):
        po = pA0 if dh == 0 else pA1
        for dv in range(8):
            S.do("tensor",
                 (lambda e, dv=dv, dh=dh, po=po:
                  e.matmul(A(po)[0:1, :], A(bvc)[:, dv:dv + 1],
                           A(wtmp)[:, dv * 1024 + dh * 512: dv * 1024 + dh * 512 + 512],
                           start=(dv == 0), stop=(dv == 7))),
                 wait=([("VE", v_wo), ("TE", S.val("TE"))] if (dv == 0 and dh == 0) else ()),
                 inc=("TE" if dv == 7 else None))
    t_bop = S.val("TE")
    S.do("vector", lambda e: e.tensor_copy(A(wtmp)[0:1, 0:512], A(pA0)[0:1, :]), wait=[("TE", t_bop)])
    S.do("vector", lambda e: e.tensor_copy(A(wtmp)[0:1, 512:1024], A(pA1)[0:1, :]), inc="VE")
    v_bop = S.val("VE")
    t_bb = S.dma("gpsimd", bo_bounce.rearrange("(a b) -> a b", a=1), A(wtmp)[0:1, 0:1024],
                 wait=[("VE", v_bop)], inc="sb")
    t_bl = S.dma("sync", A(bopc), bo_bounce.rearrange("(c p) -> p c", p=128), wait=[("sb", t_bb)],
                 slow=True, inc="lw")
    v_boc = S.do("vector", lambda e: e.tensor_tensor(A(bopc), A(bopc), A(boc), ALU.add),
                 wait=[("lw", t_bl)], inc="VE")

    setup_VE = S.val("VE"); setup_SE = S.val("SE"); setup_TE = S.val("TE")

    x_cons_tok = {}
    nhi_cons_tok = None
    kv_store_tok = None
    q_store_tok = None
    x1_store_tok = None

    proj_bank_free = None
    bfree = None
    x_cons_tok = {}
    trf_done = {}
    nhi_cons_tok = None
    kv_store_tok = {"skv0": None, "skv1": None, "sq0": None, "sq1": None}
    q_store_tok = None
    x1_store_tok = None
    pb_state = {"k_se": {}, "k_ve": {}, "v_se": {}, "v_ve": {}, "q_se": {}, "q_ve": {}}

    P_state = {}

    P_lnq_se = {}

    def emit_P_front(t, c):
        g = t * 4 + c
        par = g % 2
        xap = A(a_x)[:, par * D:(par + 1) * D]
        w = [("VE", setup_VE), ("SE", setup_SE), ("TE", setup_TE)] if g == 0 else []
        if g >= 2 and (g - 2) in x_cons_tok:
            w.append(("VE", x_cons_tok[g - 2]))
        t_x = S.dma("sync", xap, xin[t * TILE + c * 128: t * TILE + (c + 1) * 128, :], wait=w,
                    inc=("lx0" if par == 0 else "lx1"))
        vtok, setok = emit_lnq(xap, A(a_qtm), A(c_e)[:, 1:2], par,
                               wait_x_v=[("lx0" if par == 0 else "lx1", t_x)],
                               ybuf=A(a_y), ibuf=A(a_i), dbuf=A(a_z2),
                               prev_tok=P_lnq_se.get(par))
        P_lnq_se[par] = setok
        P_state[(t, c)] = vtok

    def emit_P_back(t, c):
        nonlocal x1_store_tok
        g = t * 4 + c
        par = g % 2
        zpar = t % 2
        xap = A(a_x)[:, par * D:(par + 1) * D]
        vtok = P_state.pop((t, c))
        wtr = [("VE", vtok), ("VE", S.val("VE"))]
        for j in range(8):
            S.do("tensor",
                 (lambda e, j=j: e.transpose(A(pTb)[:, j * 128:(j + 1) * 128],
                                             A(a_qtm)[:, j * 128:(j + 1) * 128], A(s_idb))),
                 wait=(wtr if j == 0 else ()), inc=("TE" if j == 7 else None))
        tm = S.val("TE")
        qfm_tok = S.do("vector", lambda e: e.tensor_copy(A(a_qfm), A(pTb)), wait=[("TE", tm)],
                       inc="VE")
        for dh in range(2):
            pf = pF0 if dh == 0 else pF1
            for di in range(8):
                first = di == 0
                last = di == 7
                S.do("tensor",
                     (lambda e, di=di, dh=dh, pf=pf, first=first, last=last:
                      e.matmul(A(pf), A(a_qfm)[:, di * 128:(di + 1) * 128],
                               A(q_rhs)[:, di * 1024 + dh * 512: di * 1024 + dh * 512 + 512],
                               start=first, stop=last)),
                     wait=([("VE", qfm_tok)] if first else ()),
                     inc=("TE" if last else None))
        tf = S.val("TE")
        S.do("vector", lambda e: e.tensor_scalar(A(c_f)[:, 1:2], A(c_e)[:, 1:2], 0.5 / 127.0,
                                                 None, ALU.mult), drain=True)
        for dh in range(2):
            pf = pF0 if dh == 0 else pF1
            S.do("vector",
                 (lambda e, pf=pf, dh=dh:
                  e.scalar_tensor_tensor(A(a_y)[:, dh * 512:(dh + 1) * 512], A(pf),
                                         A(c_f)[:, 1:2], A(rs_bc)[:, dh * 512:(dh + 1) * 512],
                                         ALU.mult, ALU.mult)),
                 wait=([("TE", tf)] if dh == 0 else ()))
        xw = []
        if x1_store_tok is not None:
            xw.append(("sx1", x1_store_tok))
        xw.append(("SE", S.val("SE")))
        xe = S.do("vector",
                  (lambda e, xap=xap: e.tensor_tensor(A(a_x1), xap, A(a_y), ALU.add)),
                  wait=xw, inc="VE")
        x_cons_tok[g] = xe
        x1_store_tok = S.dma("gpsimd", x1s[t * 4 + c], A(a_x1), wait=[("VE", xe)], inc="sx1")
        if debug:
            x1_store_tok = S.dma("gpsimd", dbg["d_x1"][t * 4 + c], A(a_x1), inc="sx1")
        wz = [("VE", xe)]
        if (t - 2) in trf_done:
            wz.append(("TE", trf_done[t - 2]))
        emit_ln(A(a_x1), A(z2buf)[:, zpar * 4 * D + c * D: zpar * 4 * D + (c + 1) * D],
                wait_x_v=wz, sq=A(a_y))

    def emit_W_trf(t):
        nonlocal nhi_cons_tok
        zpar = t % 2
        bank_tok = {0: None, 1: None}
        for c in range(4):
            for gg in range(2):
                bank = pTf0 if gg == 0 else pTf1
                wf = [("SE", S.val("SE"))]
                if bank_tok[gg] is not None:
                    wf.append(("VE", bank_tok[gg]))
                if nhi_cons_tok is not None and c == 0 and gg == 0:
                    wf.append(("TE", nhi_cons_tok))
                if t == 0 and c == 0 and gg == 0:
                    wf.append(("swo", S.val("swo")))
                for j in range(4):
                    dv = gg * 4 + j
                    S.do("tensor",
                         (lambda e, dv=dv, j=j, bank=bank, c=c, zpar=zpar:
                          e.transpose(A(bank)[:, j * 128:(j + 1) * 128],
                                      A(z2buf)[:, zpar * 4 * D + c * D + dv * 128:
                                               zpar * 4 * D + c * D + (dv + 1) * 128],
                                      A(s_idf))),
                         wait=(wf if j == 0 else ()), inc=("TE" if j == 3 else None))
                tm2 = S.val("TE")
                dst_h = A(a_nhi).rearrange("p (dv t) -> p dv t", dv=NCH)[
                    :, gg * 4:(gg + 1) * 4, c * 128:(c + 1) * 128]
                dst_l = A(a_nlo).rearrange("p (dv t) -> p dv t", dv=NCH)[
                    :, gg * 4:(gg + 1) * 4, c * 128:(c + 1) * 128]
                S.do("vector", (lambda e, d=dst_h, bank=bank: e.tensor_copy(d, A(bank))),
                     wait=[("TE", tm2)])
                S.do("vector", (lambda e, d=dst_l, h=dst_h, bank=bank:
                                e.tensor_tensor(d, A(bank), h, ALU.subtract)), inc="VE")
                bank_tok[gg] = S.val("VE")
        trf_done[t] = S.val("TE")
        return S.val("VE")

    def emit_W_K(t, do_list, v_nsplit):
        st = pb_state
        for do in do_list:
            pa = pA0 if do % 2 == 0 else pA1
            wf = [("VE", v_nsplit)]
            if do >= 2:
                wf += [("SE", st["k_se"][do - 2]), ("VE", st["k_ve"][do - 2])]
            elif proj_bank_free is not None:
                wf += proj_bank_free[do % 2]
            pairs = [(A(wk_hi)[:, di * 1024 + do * 128: di * 1024 + (do + 1) * 128],
                      A(wk_lo)[:, di * 1024 + do * 128: di * 1024 + (do + 1) * 128],
                      A(a_nhi)[:, di * TILE:(di + 1) * TILE],
                      A(a_nlo)[:, di * TILE:(di + 1) * TILE]) for di in range(8)]
            mm3(A(pa), pairs, wait_first=wf, mark=True)
            tmk = S.val("TE")
            ssem = "skv0" if do % 2 == 0 else "skv1"
            slot = do % 2
            wk_ev = [("TE", tmk)]
            if kv_store_tok[ssem] is not None:
                wk_ev.append((ssem, kv_store_tok[ssem]))
            if kv_store_tok.get("sq%d" % slot) is not None:
                wk_ev.append(("sq%d" % slot, kv_store_tok["sq%d" % slot]))
            S.do("scalar",
                 (lambda e, pa=pa, do=do, slot=slot:
                  e.activation(A(a_kvhi)[:, slot * TILE:(slot + 1) * TILE], A(pa), AF.Identity,
                               bias=A(bkc)[:, do:do + 1], scale=1.0)),
                 wait=wk_ev, inc="SE")
            st["k_se"][do] = S.val("SE")
            S.do("vector",
                 (lambda e, pa=pa, do=do, slot=slot:
                  e.scalar_tensor_tensor(A(a_kvlo)[:, slot * TILE:(slot + 1) * TILE], A(pa),
                                         A(bkc)[:, do:do + 1],
                                         A(a_kvhi)[:, slot * TILE:(slot + 1) * TILE],
                                         ALU.add, ALU.subtract)),
                 wait=[("SE", st["k_se"][do])], inc="VE")
            st["k_ve"][do] = S.val("VE")
            S.dma("gpsimd",
                  contrib[t, 0:128, :].rearrange("p (qb dk k) -> p qb dk k", qb=4, dk=8)[:, :, do, :],
                  A(a_kvhi)[:, slot * TILE:(slot + 1) * TILE].rearrange("p (qb k) -> p qb k", qb=4),
                  wait=[("VE", st["k_ve"][do])], inc=ssem)
            kv_store_tok[ssem] = S.dma("gpsimd",
                  contrib[t, 128:256, :].rearrange("p (qb dk k) -> p qb dk k", qb=4, dk=8)[:, :, do, :],
                  A(a_kvlo)[:, slot * TILE:(slot + 1) * TILE].rearrange("p (qb k) -> p qb k", qb=4),
                  inc=ssem)

    def emit_W_V(t, tc_list, v_nsplit):
        st = pb_state
        for tc in tc_list:
            for dh in range(2):
                idx = tc * 2 + dh
                pa = pA0 if idx % 2 == 0 else pA1
                wf = []
                if idx >= 2:
                    wf += [("SE", st["v_se"][idx - 2]), ("VE", st["v_ve"][idx - 2])]
                else:
                    wf += [("SE", st["k_se"][6 + idx]), ("VE", st["k_ve"][6 + idx])]
                pairs = [(A(a_nhi)[:, di * TILE + tc * 128: di * TILE + (tc + 1) * 128],
                          A(a_nlo)[:, di * TILE + tc * 128: di * TILE + (tc + 1) * 128],
                          A(wv_hi)[:, di * 1024 + dh * 512: di * 1024 + dh * 512 + 512],
                          A(wv_lo)[:, di * 1024 + dh * 512: di * 1024 + dh * 512 + 512])
                         for di in range(8)]
                mm3(A(pa), pairs, wait_first=wf, mark=True)
                tmk = S.val("TE")
                ssem = "skv0" if idx % 2 == 0 else "skv1"
                slot = idx % 2
                wv_ev = [("TE", tmk)]
                if kv_store_tok[ssem] is not None:
                    wv_ev.append((ssem, kv_store_tok[ssem]))
                S.do("scalar",
                     (lambda e, pa=pa, slot=slot:
                      e.activation(A(a_kvhi)[:, slot * TILE:(slot + 1) * TILE], A(pa),
                                   AF.Identity, bias=0.0, scale=1.0)),
                     wait=wv_ev, inc="SE")
                st["v_se"][idx] = S.val("SE")
                S.do("vector",
                     (lambda e, pa=pa, slot=slot:
                      e.tensor_tensor(A(a_kvlo)[:, slot * TILE:(slot + 1) * TILE], A(pa),
                                      A(a_kvhi)[:, slot * TILE:(slot + 1) * TILE],
                                      ALU.subtract)),
                     wait=[("SE", st["v_se"][idx])], inc="VE")
                st["v_ve"][idx] = S.val("VE")
                S.dma("gpsimd", contrib[t, 256:384, tc * D + dh * 512: tc * D + dh * 512 + 512],
                      A(a_kvhi)[:, slot * TILE:(slot + 1) * TILE],
                      wait=[("VE", st["v_ve"][idx])], inc=ssem)
                kv_store_tok[ssem] = S.dma("gpsimd", contrib[t, 384:512, tc * D + dh * 512: tc * D + dh * 512 + 512],
                                           A(a_kvlo)[:, slot * TILE:(slot + 1) * TILE], inc=ssem)

    def emit_W_Q(t, v_nsplit):
        nonlocal nhi_cons_tok, q_store_tok, proj_bank_free
        st = pb_state
        for do in range(8):
            pa = pA0 if do % 2 == 0 else pA1
            wf = []
            if do >= 2:
                wf += [("SE", st["q_se"][do - 2]), ("VE", st["q_ve"][do - 2])]
            else:
                wf += [("SE", st["v_se"][6 + do]), ("VE", st["v_ve"][6 + do])]
            pairs = [(A(wq_hi)[:, di * 1024 + do * 128: di * 1024 + (do + 1) * 128],
                      A(wq_lo)[:, di * 1024 + do * 128: di * 1024 + (do + 1) * 128],
                      A(a_nhi)[:, di * TILE:(di + 1) * TILE],
                      A(a_nlo)[:, di * TILE:(di + 1) * TILE]) for di in range(8)]
            nhi_cons_tok = mm3(A(pa), pairs, wait_first=wf, mark=True)
            tmk = S.val("TE")
            ssem = "sq0" if do % 2 == 0 else "sq1"
            slot = do % 2
            wq_ev = [("TE", tmk), ("VE", S.val("VE"))]
            if kv_store_tok["skv%d" % slot] is not None:
                wq_ev.append(("skv%d" % slot, kv_store_tok["skv%d" % slot]))
            if kv_store_tok.get(ssem) is not None:
                wq_ev.append((ssem, kv_store_tok[ssem]))
            S.do("scalar",
                 (lambda e, do=do, pa=pa: e.activation(A(a_qf), A(pa), AF.Identity,
                                                       bias=A(bq32c)[:, do:do + 1], scale=1.0 / 32.0)),
                 wait=wq_ev, inc="SE")
            st["q_se"][do] = S.val("SE")
            sm = S.val("SE")
            S.do("vector", (lambda e, slot=slot: e.tensor_copy(A(a_kvhi)[:, slot * TILE:(slot + 1) * TILE],
                                                               A(a_qf))),
                 wait=[("SE", sm)])
            S.do("vector", (lambda e, slot=slot: e.tensor_tensor(A(a_kvlo)[:, slot * TILE:(slot + 1) * TILE],
                                                                 A(a_qf),
                                                                 A(a_kvhi)[:, slot * TILE:(slot + 1) * TILE],
                                                                 ALU.subtract)),
                 inc="VE")
            st["q_ve"][do] = S.val("VE")
            S.dma("gpsimd",
                  qx_hi[t].rearrange("p (cc dk j) -> p cc dk j", cc=4, dk=8)[:, :, do, :],
                  A(a_kvhi)[:, slot * TILE:(slot + 1) * TILE].rearrange("p (cc j) -> p cc j", cc=4),
                  wait=[("VE", st["q_ve"][do])], inc=ssem)
            kv_store_tok[ssem] = S.dma("gpsimd",
                  qx_lo[t].rearrange("p (cc dk j) -> p cc dk j", cc=4, dk=8)[:, :, do, :],
                  A(a_kvlo)[:, slot * TILE:(slot + 1) * TILE].rearrange("p (cc j) -> p cc j", cc=4),
                  inc=ssem)
        proj_bank_free = {0: [("SE", st["q_se"][6]), ("VE", st["q_ve"][6])],
                          1: [("SE", st["q_se"][7]), ("VE", st["q_ve"][7])]}

    REPL = [[0, 1, 2, 3], [4, 5, 6, 7]]
    cc_tok = {}
    for c in range(4):
        emit_P_front(0, c)
        emit_P_back(0, c)
    emit_P_front(1, 0) if NS > 1 else None
    for t in range(NS):
        pb_state = {"k_se": {}, "k_ve": {}, "v_se": {}, "v_ve": {}, "q_se": {}, "q_ve": {}}
        nxt = t + 1 < NS
        v_nsplit = emit_W_trf(t)
        if nxt:
            emit_P_back(t + 1, 0)
            emit_P_front(t + 1, 1)
        emit_W_K(t, [0, 1, 2, 3], v_nsplit)
        if nxt:
            emit_P_back(t + 1, 1)
            emit_P_front(t + 1, 2)
        emit_W_K(t, [4, 5, 6, 7], v_nsplit)
        if nxt:
            emit_P_back(t + 1, 2)
            emit_P_front(t + 1, 3)
        emit_W_V(t, [0, 1], v_nsplit)
        if nxt:
            emit_P_back(t + 1, 3)
        emit_W_V(t, [2, 3], v_nsplit)
        if t == NS - 1:
            bfree = [("TE", S.val("TE")), ("VE", S.val("VE")), ("SE", S.val("SE")),
                     ("sx1", S.val("sx1"))]
        if t + 2 < NS:
            emit_P_front(t + 2, 0)
        cc_tok[t] = S.do("gpsimd",
                         (lambda e, t=t: e.collective_compute(
                             "AllGather", ALU.bypass, replica_groups=REPL,
                             ins=[contrib[t]], outs=[gath[t]])),
                         wait=[("skv0", S.val("skv0")), ("skv1", S.val("skv1"))],
                         inc="cc")
        emit_W_Q(t, v_nsplit)

    A_TE = S.val("TE"); A_VE = S.val("VE"); A_SE = S.val("SE")

    bwait = [("TE", A_TE), ("VE", A_VE), ("SE", A_SE),
             ("skv0", S.val("skv0")), ("skv1", S.val("skv1")),
             ("sq0", S.val("sq0")), ("sq1", S.val("sq1")), ("sx1", S.val("sx1"))]
    sq_totals = [("sq0", S.val("sq0")), ("sq1", S.val("sq1")), ("sx1", S.val("sx1"))]

    kv_ld_tok = {}
    q_ld_tok = {}
    it_idx = 0
    ctx_ev_tok = {}
    exp_tok = {}
    pt_tok = {}
    kt_last_iter = {}
    ctx_store = {}

    def load_kv(kt, extra_wait=()):
        buf = kt % 3
        sg, cc_ = kt // 4, kt % 4
        w = list(bwait) if kt == 0 else list(extra_wait)
        w.append(("cc", cc_tok[sg]))
        ksem = ("lk0", "lk1", "lk2")[buf]
        if kt <= 2:
            vq, vsem = "sync", ("sv0", "sv1", "sv2")[kt]
        else:
            vq, vsem = "gpsimd", ("lv0", "lv1", "lv2")[buf]
        t3 = t4 = None
        for r in range(4):
            S.dma("gpsimd", A(b_khi)[:, buf * NCH * TILE + r * 1024: buf * NCH * TILE + (r + 1) * 1024],
                  gath[sg, r * 512 + 0:r * 512 + 128, cc_ * 1024:(cc_ + 1) * 1024],
                  wait=(w if r == 0 else ()), inc=ksem)
            t3 = S.dma("gpsimd", A(b_klo)[:, buf * NCH * TILE + r * 1024: buf * NCH * TILE + (r + 1) * 1024],
                       gath[sg, r * 512 + 128:r * 512 + 256, cc_ * 1024:(cc_ + 1) * 1024], inc=ksem)
            S.dma(vq, A(b_vhi)[:, buf * 4 * D + r * D:(buf * 4 + r + 1) * D],
                  gath[sg, r * 512 + 256:r * 512 + 384, cc_ * D:(cc_ + 1) * D],
                  wait=(w if r == 0 else ()), inc=vsem)
            t4 = S.dma(vq, A(b_vlo)[:, buf * 4 * D + r * D:(buf * 4 + r + 1) * D],
                       gath[sg, r * 512 + 384:r * 512 + 512, cc_ * D:(cc_ + 1) * D], inc=vsem)
        kv_ld_tok[kt] = [(ksem, t3), (vsem, t4)]

    iters = [(kt, k) for kt in range(nt) for k in range(kt, nt)]
    for kt in range(nt):
        kt_last_iter[kt] = max(i for i, (a, b) in enumerate(iters) if a == kt)

    load_kv(0)

    n_it = len(iters)
    t_qk_tok = {}
    t_tr_tok = {}
    t_pv_tok = {}

    def front(i):
        kt, k = iters[i]
        qpar = i % 2
        p4 = i % 4
        wq_ = []
        if i >= 2:
            wq_.append(("SE", exp_tok[i - 2]))
        else:
            wq_.append(("TE", trf_done[NS - 1]))
            wq_ += sq_totals
        qsem = "lq0" if qpar == 0 else "lq1"
        sgq, ccq = k // 4, k % 4
        S.dma("sync",
              A(b_qhi)[:, qpar * NCH * QCH:(qpar + 1) * NCH * QCH],
              qx_hi[sgq][:, ccq * 1024:(ccq + 1) * 1024],
              wait=wq_, inc=qsem)
        t2 = S.dma("sync",
                   A(b_qlo)[:, qpar * NCH * QCH:(qpar + 1) * NCH * QCH],
                   qx_lo[sgq][:, ccq * 1024:(ccq + 1) * 1024],
                   inc=qsem)
        q_ld_tok[i] = (qsem, t2)

        par = i % 2
        buf = kt % 3
        pS = pS0 if par == 0 else pS1
        wf = list(kv_ld_tok[kt]) + [q_ld_tok[i]]
        if i >= 2:
            wf.append(("SE", exp_tok[i - 2]))
        khv = A(b_khi)[:, buf * NCH * TILE:(buf + 1) * NCH * TILE].rearrange(
            "p (r di k) -> p r di k", r=4, di=NCH)
        klv = A(b_klo)[:, buf * NCH * TILE:(buf + 1) * NCH * TILE].rearrange(
            "p (r di k) -> p r di k", r=4, di=NCH)
        pairs = [(A(b_qhi)[:, qpar * NCH * QCH + di * 128: qpar * NCH * QCH + (di + 1) * 128],
                  A(b_qlo)[:, qpar * NCH * QCH + di * 128: qpar * NCH * QCH + (di + 1) * 128],
                  khv[:, :, di, :], klv[:, :, di, :])
                 for di in range(8)]
        mm3(A(pS), pairs, wait_first=wf, mark=True)
        t_qk_tok[i] = S.val("TE")

        wv_ = [("TE", t_qk_tok[i])]
        if kt == k:
            S.do("vector", lambda e, pS=pS: e.tensor_tensor(A(pS), A(pS), A(mask), ALU.add),
                 wait=wv_, inc="VE")
            wv_ = [("VE", S.val("VE"))]
        if i >= 1 and iters[i - 1][1] == k:
            wv_ = list(wv_) + [("SE", exp_tok[i - 1])]
        S.do("vector", lambda e, pS=pS: e.reduce_max(A(c_a)[:, 0:1], A(pS), axis=AX.X, negate=True),
             wait=wv_, drain=True)
        if kt == 0:
            S.do("vector", (lambda e, k=k: e.tensor_copy(A(nm_col)[:, k:k + 1], A(c_a)[:, 0:1])),
                 inc="VE", drain=True)
        else:
            S.do("vector", (lambda e, k=k: e.tensor_tensor(A(c_b)[:, 0:1], A(nm_col)[:, k:k + 1],
                                                           A(c_a)[:, 0:1], ALU.min)), drain=True)
            S.do("vector", (lambda e, k=k, p4=p4: e.tensor_tensor(A(c_c)[:, p4:p4 + 1], A(c_b)[:, 0:1],
                                                                  A(nm_col)[:, k:k + 1], ALU.subtract)))
            S.do("vector", (lambda e, k=k: e.tensor_copy(A(nm_col)[:, k:k + 1], A(c_b)[:, 0:1])),
                 inc="VE", drain=True)
        v_nm = S.val("VE")
        if kt > 0:
            S.do("scalar", (lambda e, p4=p4: e.activation(A(c_d)[:, p4:p4 + 1], A(c_c)[:, p4:p4 + 1],
                                                          AF.Exp, bias=0.0, scale=1.0)),
                 wait=[("VE", v_nm)], drain=True)
        exp_tok[i] = S.do("scalar",
                          (lambda e, pS=pS, k=k, par=par, p4=p4:
                           e.activation(A(b_p)[:, par * TILE:(par + 1) * TILE], A(pS), AF.Exp,
                                        bias=A(nm_col)[:, k:k + 1], scale=1.0,
                                        accum_out=A(c_e)[:, p4:p4 + 1])),
                          wait=[("VE", v_nm)], inc="SE", drain=True)

    def back_pre(i):
        kt, k = iters[i]
        par = i % 2
        p4 = i % 4
        pPT = pPT0 if par == 0 else pPT1
        wtr = [("SE", exp_tok[i])]
        if i >= 2:
            wtr.append(("VE", pt_tok[i - 2]))
        for j in range(4):
            S.do("tensor",
                 (lambda e, j=j, pPT=pPT, par=par:
                  e.transpose(A(pPT)[:, j * 128:(j + 1) * 128],
                              A(b_p)[:, par * TILE + j * 128: par * TILE + (j + 1) * 128], A(s_idf))),
                 wait=(wtr if j == 0 else ()), inc=("TE" if j == 3 else None))
        t_tr_tok[i] = S.val("TE")
        S.do("vector", lambda e, pPT=pPT: e.tensor_copy(A(b_pthi), A(pPT)),
             wait=[("TE", t_tr_tok[i])])
        pt_tok[i] = S.do("vector", lambda e, pPT=pPT: e.tensor_tensor(A(b_ptlo), A(pPT), A(b_pthi),
                                                                      ALU.subtract), inc="VE")
        if kt == 0:
            S.do("vector", (lambda e, k=k, p4=p4: e.tensor_copy(A(l_col)[:, k:k + 1],
                                                                A(c_e)[:, p4:p4 + 1])),
                 wait=[("SE", exp_tok[i])], inc="VE")
        else:
            S.do("vector",
                 (lambda e, k=k, p4=p4: e.scalar_tensor_tensor(A(l_col)[:, k:k + 1],
                                                               A(l_col)[:, k:k + 1],
                                                               A(c_d)[:, p4:p4 + 1],
                                                               A(c_e)[:, p4:p4 + 1],
                                                               ALU.mult, ALU.add)),
                 wait=[("SE", exp_tok[i])], inc="VE")

    def back_post(i):
        kt, k = iters[i]
        par = i % 2
        p4 = i % 4
        buf = kt % 3
        pC = pC0 if par == 0 else pC1
        wpv = [("VE", pt_tok[i])]
        if i >= 2:
            wpv.append(("VE", ctx_ev_tok[i - 2]))
        for dh in range(2):
            pairs = [(A(b_pthi)[:, kc * 128:(kc + 1) * 128],
                      A(b_ptlo)[:, kc * 128:(kc + 1) * 128],
                      A(b_vhi)[:, buf * 4 * D + kc * D + dh * 512: buf * 4 * D + kc * D + dh * 512 + 512],
                      A(b_vlo)[:, buf * 4 * D + kc * D + dh * 512: buf * 4 * D + kc * D + dh * 512 + 512])
                     for kc in range(4)]
            mm3(A(pC)[:, dh * 512:(dh + 1) * 512], pairs,
                wait_first=(wpv if dh == 0 else ()), mark=(dh == 1))
        t_pv_tok[i] = S.val("TE")
        if kt == 0:
            ctx_ev_tok[i] = S.do("vector",
                                 (lambda e, k=k, pC=pC: e.tensor_copy(A(b_ctx)[:, k * D:(k + 1) * D],
                                                                      A(pC))),
                                 wait=[("TE", t_pv_tok[i])], inc="VE")
        else:
            ctx_ev_tok[i] = S.do("vector",
                                 (lambda e, k=k, pC=pC, p4=p4:
                                  e.scalar_tensor_tensor(A(b_ctx)[:, k * D:(k + 1) * D],
                                                         A(b_ctx)[:, k * D:(k + 1) * D],
                                                         A(c_d)[:, p4:p4 + 1], A(pC),
                                                         ALU.mult, ALU.add)),
                                 wait=[("TE", t_pv_tok[i])], inc="VE")
        if kt == k:
            ctx_store[k] = S.dma("gpsimd", ctxs[k], A(b_ctx)[:, k * D:(k + 1) * D],
                                 wait=[("VE", ctx_ev_tok[i])], inc="sc")
            if debug:
                S.dma("gpsimd", dbg["d_ctx"][k], A(b_ctx)[:, k * D:(k + 1) * D], inc="sc")
        if i == kt_last_iter[kt] and kt + 3 < nt:
            load_kv(kt + 3, extra_wait=[("TE", t_pv_tok[i])])

    front(0)
    if nt > 1:
        load_kv(1)
    if nt > 2:
        load_kv(2)
    if n_it > 1:
        front(1)
    for j in range(n_it):
        back_pre(j)
        if j + 2 < n_it:
            front(j + 2)
        back_post(j)

    if debug:
        S.dma("gpsimd", dbg["d_l"], A(l_col)[:, 0:nt], wait=[("VE", S.val("VE"))], inc="sc")

    B_TE = S.val("TE"); B_VE = S.val("VE"); B_SE = S.val("SE")

    cwait = [("TE", B_TE), ("VE", B_VE), ("SE", B_SE), ("sc", S.val("sc"))]
    w_wo = [("sc", ctx_store[7])] if nt >= 8 else list(cwait)
    t_wo2 = S.dma("sync", A(wo_hi), wo_scr[0], wait=w_wo, inc="lw")
    t_wo3 = S.dma("sync", A(wo_lo), wo_scr[1], inc="lw")
    for _rc in range(nt):
        S.do("vector", (lambda e, c=_rc: e.reciprocal(A(li_col)[:, c:c + 1], A(l_col)[:, c:c + 1])),
             wait=(list(cwait) if _rc == 0 else ()), drain=(_rc == nt - 1))
    v_li = S.do("vector", lambda e: e.tensor_copy(A(c_a)[:, 0:1], A(li_col)[:, 0:1]), inc="VE")

    def pairs_o(do):
        return [(A(wo_hi)[:, dv * 1024 + do * 128: dv * 1024 + (do + 1) * 128],
                 A(wo_lo)[:, dv * 1024 + do * 128: dv * 1024 + (do + 1) * 128],
                 A(c_fhi)[:, dv * TILE:(dv + 1) * TILE],
                 A(c_flo)[:, dv * TILE:(dv + 1) * TILE]) for dv in range(8)]

    out_store = None
    ng = nt // 4
    C_so = {}
    C_x2ve = {}
    C_sy = {}
    C_ytrve = {}
    C2_state = {}

    def emit_C1(gi, part):
        par = gi % 2
        if part == 0:
            w0 = [("VE", v_li)] if gi == 0 else []
            if gi >= 2 and (gi - 2) in C_so:
                w0.append(("so", C_so[gi - 2]))
            tok_c = None
            for cc in range(4):
                k = gi * 4 + cc
                tok_c = S.dma("sync", A(c_ctx)[:, par * 4 * D + cc * D: par * 4 * D + (cc + 1) * D],
                              ctxs[k], wait=(w0 if cc == 0 else ()), inc="lg")
            for cc in range(4):
                k = gi * 4 + cc
                tok_c = S.dma("sync", A(c_x1g)[:, par * 4 * D + cc * D: par * 4 * D + (cc + 1) * D],
                              x1s[k], inc="lg")
            C2_state[("lg", gi)] = tok_c
        cc_list = ((0,), (1,), (2, 3))[part] if part < 3 else ()
        for cc in cc_list:
            k = gi * 4 + cc
            wv0 = [("lg", C2_state[("lg", gi)])]
            if cc == 0 and (gi - 1) in C_ytrve:
                wv0.append(("VE", C_ytrve[gi - 1]))
            v_n = S.do("vector",
                       (lambda e, cc=cc, k=k, par=par:
                        e.tensor_scalar(A(c_ctxn), A(c_ctx)[:, par * 4 * D + cc * D: par * 4 * D + (cc + 1) * D],
                                        A(li_col)[:, k:k + 1], None, ALU.mult)),
                       wait=wv0, inc="VE")
            for gg in range(2):
                bank = pCT0 if gg == 0 else pCT1
                for j in range(4):
                    dv = gg * 4 + j
                    S.do("tensor",
                         (lambda e, dv=dv, j=j, bank=bank:
                          e.transpose(A(bank)[:, j * 128:(j + 1) * 128],
                                      A(c_ctxn)[:, dv * 128:(dv + 1) * 128], A(s_idf))),
                         wait=([("VE", v_n), ("VE", S.val("VE"))] if j == 0 else ()),
                         inc=("TE" if j == 3 else None))
                tm2 = S.val("TE")
                dst_h = A(c_fhi).rearrange("p (dv t) -> p dv t", dv=NCH)[
                    :, gg * 4:(gg + 1) * 4, cc * 128:(cc + 1) * 128]
                dst_l = A(c_flo).rearrange("p (dv t) -> p dv t", dv=NCH)[
                    :, gg * 4:(gg + 1) * 4, cc * 128:(cc + 1) * 128]
                S.do("vector", (lambda e, d=dst_h, bank=bank: e.tensor_copy(d, A(bank))),
                     wait=[("TE", tm2)])
                S.do("vector", (lambda e, d=dst_l, h=dst_h, bank=bank:
                                e.tensor_tensor(d, A(bank), h, ALU.subtract)), inc="VE")
        if part == 3:
            par = gi % 2
            v_fsplit = S.val("VE")
            o_se = {}
            for do in range(8):
                po = pO0 if do % 2 == 0 else pO1
                wfo = [("VE", v_fsplit)] if do == 0 else []
                if do == 0 and gi == 0:
                    wfo.append(("lw", t_wo3))
                if do == 0 and (gi - 1) in C_sy:
                    wfo.append(("SE", C_sy[gi - 1]))
                if do >= 2:
                    wfo.append(("SE", o_se[do - 2]))
                mm3(A(po), pairs_o(do), wait_first=wfo, mark=True)
                tmk = S.val("TE")
                S.do("scalar",
                     (lambda e, do=do, po=po: e.activation(A(c_y)[:, do * TILE:(do + 1) * TILE], A(po),
                                                           AF.Identity, bias=A(bopc)[:, do:do + 1],
                                                           scale=1.0)),
                     wait=[("TE", tmk)], inc="SE")
                o_se[do] = S.val("SE")
            C_sy[gi] = S.val("SE")
            s_y = C_sy[gi]
            for cc in range(4):
                for gg in range(2):
                    bank = pCT0 if gg == 0 else pCT1
                    for j in range(4):
                        do = gg * 4 + j
                        S.do("tensor",
                             (lambda e, do=do, j=j, cc=cc, bank=bank:
                              e.transpose(A(bank)[:, j * 128:(j + 1) * 128],
                                          A(c_y)[:, do * TILE + cc * 128: do * TILE + (cc + 1) * 128],
                                          A(s_idf))),
                             wait=([("SE", s_y), ("VE", S.val("VE"))] if (j == 0) else ()),
                             inc=("TE" if j == 3 else None))
                    tm2 = S.val("TE")
                    wadd = [("TE", tm2)]
                    if gg == 0 and cc == 0 and (gi - 2) in C_so:
                        wadd.append(("so", C_so[gi - 2]))
                    x2v = A(c_x2)[:, par * 4 * D + cc * D + gg * 512:
                                  par * 4 * D + cc * D + (gg + 1) * 512]
                    x1v = A(c_x1g)[:, par * 4 * D + cc * D + gg * 512:
                                   par * 4 * D + cc * D + (gg + 1) * 512]
                    S.do("vector",
                         (lambda e, x2v=x2v, x1v=x1v, bank=bank:
                          e.tensor_tensor(x2v, A(bank), x1v, ALU.add)),
                         wait=wadd, inc="VE")
            C_x2ve[gi] = S.val("VE")
            C_ytrve[gi] = S.val("VE")

    C_lnq_se = {}

    def emit_C2f(gi, cc):
        par = gi % 2
        sp = cc % 2
        x2ap = A(c_x2)[:, par * 4 * D + cc * D: par * 4 * D + (cc + 1) * D]
        wx = [("VE", C_x2ve[gi])]
        if (gi, cc, "ffn_te") in C2_state:
            pass
        prev = (gi, cc - 2) if cc >= 2 else ((gi - 1, cc + 2) if gi > 0 else None)
        if prev is not None and ("b_te", prev[0], prev[1]) in C2_state:
            wx.append(("TE", C2_state[("b_te", prev[0], prev[1])]))
        vtok, setok = emit_lnq(x2ap, A(c_qtm)[:, sp * D:(sp + 1) * D],
                               A(c_e)[:, 2 + sp:3 + sp], sp, wait_x_v=wx,
                               ybuf=A(c_yq), ibuf=A(c_ii), dbuf=A(c_z),
                               prev_tok=C_lnq_se.get(sp))
        C_lnq_se[sp] = setok
        C2_state[(gi, cc)] = vtok

    def emit_C2b(gi, cc):
        nonlocal out_store
        par = gi % 2
        sp = cc % 2
        k = gi * 4 + cc
        x2ap = A(c_x2)[:, par * 4 * D + cc * D: par * 4 * D + (cc + 1) * D]
        vtok = C2_state.pop((gi, cc))
        wtr = [("VE", vtok), ("VE", S.val("VE"))]
        for j in range(8):
            S.do("tensor",
                 (lambda e, j=j, sp=sp: e.transpose(A(pCb)[:, sp * 1024 + j * 128: sp * 1024 + (j + 1) * 128],
                                                    A(c_qtm)[:, sp * D + j * 128: sp * D + (j + 1) * 128],
                                                    A(s_idb))),
                 wait=(wtr if j == 0 else ()), inc=("TE" if j == 7 else None))
        tm = S.val("TE")
        qfm_tok = S.do("vector", (lambda e, sp=sp: e.tensor_copy(A(c_qfm),
                                                                 A(pCb)[:, sp * 1024:(sp + 1) * 1024])),
                       wait=[("TE", tm)], inc="VE")
        wffn = [("VE", qfm_tok)]
        if ("epi_ve",) in C2_state:
            wffn.append(("VE", C2_state[("epi_ve",)]))
        for dh in range(2):
            for di in range(8):
                first = di == 0
                last = di == 7
                S.do("tensor",
                     (lambda e, di=di, dh=dh, first=first, last=last:
                      e.matmul(A(pCF0)[:, dh * 512:(dh + 1) * 512],
                               A(c_qfm)[:, di * 128:(di + 1) * 128],
                               A(q_rhs)[:, di * 1024 + dh * 512: di * 1024 + dh * 512 + 512],
                               start=first, stop=last)),
                     wait=(wffn if (first and dh == 0) else ()),
                     inc=("TE" if (last and dh == 1) else None))
        tf = S.val("TE")
        C2_state[("b_te", gi, cc)] = tf
        S.do("vector", (lambda e, sp=sp: e.tensor_scalar(A(c_f)[:, 1:2], A(c_e)[:, 2 + sp:3 + sp],
                                                         0.5 / 127.0, None, ALU.mult)), drain=True)
        for dh in range(2):
            S.do("vector",
                 (lambda e, dh=dh:
                  e.scalar_tensor_tensor(A(c_yq)[:, dh * 512:(dh + 1) * 512],
                                         A(pCF0)[:, dh * 512:(dh + 1) * 512],
                                         A(c_f)[:, 1:2], A(rs_bc)[:, dh * 512:(dh + 1) * 512],
                                         ALU.mult, ALU.mult)),
                 wait=([("TE", tf)] if dh == 0 else ()))
        v_o = S.do("vector",
                   (lambda e, x2ap=x2ap: e.tensor_tensor(x2ap, x2ap, A(c_yq), ALU.add)),
                   wait=[("SE", S.val("SE"))], inc="VE")
        C2_state[("epi_ve",)] = v_o
        out_store = S.dma("gpsimd", xout[k * 128:(k + 1) * 128, :], x2ap,
                          wait=[("VE", v_o)], inc="so")
        C_so[gi] = out_store

    for part in range(4):
        emit_C1(0, part)
    emit_C2f(0, 0)
    emit_C2f(0, 1)
    for gi in range(ng):
        for cc in range(4):
            emit_C2b(gi, cc)
            if gi + 1 < ng:
                emit_C1(gi + 1, cc)
            if cc < 2:
                emit_C2f(gi, cc + 2)
            elif gi + 1 < ng and cc == 3:
                pass
        if gi + 1 < ng:
            emit_C2f(gi + 1, 0)
            emit_C2f(gi + 1, 1)

    final_w = [(s, S.val(s)) for s in ("so", "sc", "skv0", "skv1", "sq0", "sq1", "sx1", "sb", "sdbg") if S.val(s) > 0]
    S.do("gpsimd", lambda e: None, wait=final_w)

    import contextlib
    with contextlib.ExitStack() as stack:
        sems = {}
        for name in sorted(S.cnt.keys()):
            sems[name] = stack.enter_context(nc.semaphore(name))
        if guard:
            for name in ("ovector", "oscalar"):
                sems[name] = stack.enter_context(nc.semaphore(name))
        block = stack.enter_context(nc.Block())

        @block.sync
        def _(eng):
            S.replay("sync", eng, sems, guard=guard)

        @block.tensor
        def _(eng):
            S.replay("tensor", eng, sems, guard=guard)

        @block.vector
        def _(eng):
            S.replay("vector", eng, sems, guard=guard)

        @block.scalar
        def _(eng):
            S.replay("scalar", eng, sems, guard=guard)

        @block.gpsimd
        def _(eng):
            S.replay("gpsimd", eng, sems, guard=guard)

    return nc


def _host_inputs(x_full, Wq, bq, Wk, bk, Wv, bv, Wo, bo, Frhs, nt):
    import ml_dtypes
    T = nt * TILE
    ident = np.eye(128, dtype=np.float32)
    ones = np.ones((1, 128), np.float32)
    in_maps = []
    for c in range(8):
        s, r = c // 4, c % 4
        rows = (np.arange(nt)[:, None] * TILE + r * 128 + np.arange(128)[None, :]).ravel()
        xs = np.ascontiguousarray(x_full[s][rows])
        m = np.full((QCH, TILE), NEG, np.float32)
        i_idx = np.arange(QCH)[:, None]
        j_idx = np.arange(QCH)[None, :]
        for j in range(4):
            if j < r:
                m[:, j * 128:(j + 1) * 128] = 0.0
            elif j == r:
                m[:, j * 128:(j + 1) * 128] = np.where(j_idx <= i_idx, 0.0, NEG)
        in_maps.append(dict(
            xin=xs, wq=Wq, wk=Wk, wv=Wv, wo=Wo, frhs=Frhs,
            bq=bq, bk=bk, bv=bv, bo=bo,
            maskin=m, idf=ident, idb=ident.astype(ml_dtypes.bfloat16), ones_col=ones,
        ))
    return in_maps


def _host_gather(results, nt):
    T = nt * TILE
    out = np.empty((2, T, D), np.float32)
    for c, r_ in enumerate(results):
        s, r = c // 4, c % 4
        own = np.asarray(r_["xout"])
        rows = (np.arange(nt)[:, None] * TILE + r * 128 + np.arange(128)[None, :]).ravel()
        out[s][rows] = own
    return out


def kernel(L0_attn_y_all, L0_attn_Wq, L0_attn_bq, L0_attn_Wk, L0_attn_bk,
           L0_attn_Wv, L0_attn_bv, L0_attn_Wo, L0_attn_bo, L0_ffn_rhs):
    nt = 16
    x = np.ascontiguousarray(np.asarray(L0_attn_y_all, dtype=np.float32))
    nc = build_core_program(nt)
    in_maps = _host_inputs(x, np.asarray(L0_attn_Wq, np.float32), np.asarray(L0_attn_bq, np.float32),
                           np.asarray(L0_attn_Wk, np.float32), np.asarray(L0_attn_bk, np.float32),
                           np.asarray(L0_attn_Wv, np.float32), np.asarray(L0_attn_bv, np.float32),
                           np.asarray(L0_attn_Wo, np.float32), np.asarray(L0_attn_bo, np.float32),
                           np.asarray(L0_ffn_rhs, np.float32), nt)
    res = run_bass_kernel_spmd(nc, in_maps, core_ids=list(range(8)))
    return _host_gather(res.results, nt)


# version 19
# speedup vs baseline: 1.0286x; 1.0286x over previous
import numpy as np
import concourse.bass as bass
import concourse.mybir as mybir
from concourse.bass_utils import run_bass_kernel_spmd

F32 = mybir.dt.float32
BF16 = mybir.dt.bfloat16
I32 = mybir.dt.int32
AF = mybir.ActivationFunctionType
ALU = mybir.AluOpType
AX = mybir.AxisListType

D = 1024
NCH = 8
TILE = 512
NROLE = 4
QCH = 128
NEG = -1.0e38

ENGINES = ("sync", "tensor", "vector", "scalar", "gpsimd")


class Sched:

    def __init__(self):
        self.ops = {e: [] for e in ENGINES}
        self.cnt = {}

    def do(self, eng, fn, wait=(), inc=None, amt=1, drain=False):
        tok = None
        if inc is not None:
            self.cnt[inc] = self.cnt.get(inc, 0) + amt
            tok = self.cnt[inc]
        self.ops[eng].append((tuple(wait), fn, inc, amt, drain))
        return tok

    def dma(self, eng, out, in_, wait=(), inc="ld", slow=False):
        if slow:
            return self.do(eng, (lambda e, o=out, i=in_:
                                 e.dma_start(out=o, in_=i, allow_slow_non_contiguous=True)),
                           wait=wait, inc=inc, amt=16)
        return self.do(eng, (lambda e, o=out, i=in_: e.dma_start(out=o, in_=i)),
                       wait=wait, inc=inc, amt=16)

    def val(self, sem):
        return self.cnt.get(sem, 0)

    def replay(self, eng_name, eng, sems, guard=False):
        chain = guard and eng_name in ("vector", "scalar")
        osem = "o" + eng_name if chain else None
        own = "VE" if eng_name == "vector" else ("SE" if eng_name == "scalar" else None)
        n = 0
        nown = 0
        for wait, fn, inc, amt, drain in self.ops[eng_name]:
            for (s, v) in wait:
                if v > 0:
                    eng.wait_ge(sems[s], v)
            if chain:
                if n > 0:
                    eng.wait_ge(sems[osem], n)
                if nown > 0:
                    eng.wait_ge(sems[own], nown)
            instr = fn(eng)
            if drain:
                instr = eng.drain()
            if inc is not None:
                assert instr is not None, f"inc {inc} on op with no instruction"
                instr.then_inc(sems[inc], amt)
                if inc == own:
                    nown += amt
            if chain and instr is not None and inc is None:
                instr.then_inc(sems[osem], 1)
                n += 1


def build_core_program(nt, debug=False, guard=False):
    nc = bass.Bass("TRN2", target_bir_lowering=False,
                   detect_race_conditions=guard)
    T = nt * TILE
    NOWN = nt * QCH

    def din(name, shape, dt=F32):
        return nc.dram_tensor(name, shape, dt, kind="ExternalInput").ap()

    def dout_(name, shape, dt=F32):
        return nc.dram_tensor(name, shape, dt, kind="ExternalOutput").ap()

    def dtmp(name, shape, dt=F32):
        return nc.dram_tensor(name, shape, dt).ap()

    xin = din("xin", [NOWN, D])
    wq_d = din("wq", [D, D]); wk_d = din("wk", [D, D]); wv_d = din("wv", [D, D])
    wo_d = din("wo", [D, D]); frhs = din("frhs", [D, D])
    bq_d = din("bq", [D]); bk_d = din("bk", [D]); bv_d = din("bv", [D]); bo_d = din("bo", [D])
    maskin = din("maskin", [QCH, TILE])
    idf = din("idf", [128, 128])
    idb = din("idb", [128, 128], BF16)
    ones_in = din("ones_col", [1, 128])

    xout = dout_("xout", [NOWN, D])

    NS = nt // 4
    contrib = dtmp("contrib", [NS, 512, NCH * TILE], BF16)
    gath = dtmp("gath", [NS, 4 * 512, NCH * TILE], BF16)
    qx_hi = dtmp("qx_hi", [NS, 128, NCH * TILE], BF16)
    qx_lo = dtmp("qx_lo", [NS, 128, NCH * TILE], BF16)
    x1s = dtmp("x1s", [nt, 128, D])
    ctxs = dtmp("ctxs", [nt, 128, D])
    rs_bounce = dtmp("rs_bounce", [8, 128])
    wo_scr = dtmp("wo_scr", [2, 128, NCH * D], BF16)
    bo_bounce = dtmp("bo_bounce", [D])

    dbg = {}
    if debug:
        dbg["d_x1"] = dout_("d_x1", [nt, 128, D])
        dbg["d_k"] = dout_("d_k", [nt, 128, NCH * TILE], BF16)
        dbg["d_v"] = dout_("d_v", [nt, 128, 4 * D], BF16)
        dbg["d_q"] = dout_("d_q", [nt, 128, NCH * QCH], BF16)
        dbg["d_ctx"] = dout_("d_ctx", [nt, 128, D])
        dbg["d_l"] = dout_("d_l", [128, nt])
        dbg["d_rsbc"] = dout_("d_rsbc", [128, D])
        dbg["d_qrhs"] = dout_("d_qrhs", [128, NCH * D], BF16)
        dbg["d_qtm"] = dout_("d_qtm", [128, D], BF16)
        dbg["d_y"] = dout_("d_y", [128, D])
        dbg["d_wt2"] = dout_("d_wt2", [128, NCH * D])
        dbg["d_qrT"] = dout_("d_qrT", [128, NCH * D], BF16)
        dbg["d_cg"] = dout_("d_cg", [128, 8])
        dbg["d_ch"] = dout_("d_ch", [128, 8])

    def sb(name, shape, dt=F32):
        return nc.alloc_sbuf_tensor(name, shape, dt)

    q_rhs = sb("q_rhs", [128, NCH * D], BF16)
    rs_bc = sb("rs_bc", [128, D])
    mask = sb("mask", [QCH, TILE])
    s_idf = sb("s_idf", [128, 128])
    s_idb = sb("s_idb", [128, 128], BF16)
    s_ones = sb("s_ones", [1, 128])
    bq32c = sb("bq32c", [128, 8]); bkc = sb("bkc", [128, 8])
    bvc = sb("bvc", [128, 8]); boc = sb("boc", [128, 8]); bopc = sb("bopc", [128, 8])
    epsc = sb("epsc", [128, 1])
    l_col = sb("l_col", [128, max(nt, 2)])
    nm_col = sb("nm_col", [128, max(nt, 2)])
    li_col = sb("li_col", [128, max(nt, 2)])
    c_a = sb("c_a", [128, 2]); c_b = sb("c_b", [128, 2])
    c_c = sb("c_c", [128, 4]); c_d = sb("c_d", [128, 4])
    c_e = sb("c_e", [128, 4]); c_f = sb("c_f", [128, 4])
    c_g = sb("c_g", [128, 8]); c_h = sb("c_h", [128, 8])

    base_R = nc.sbuf_base

    wq_hi = sb("wq_hi", [128, NCH * D], BF16); wq_lo = sb("wq_lo", [128, NCH * D], BF16)
    wk_hi = sb("wk_hi", [128, NCH * D], BF16); wk_lo = sb("wk_lo", [128, NCH * D], BF16)
    wv_hi = sb("wv_hi", [128, NCH * D], BF16); wv_lo = sb("wv_lo", [128, NCH * D], BF16)
    base_A2 = nc.sbuf_base
    wtmp = sb("wtmp", [128, NCH * D])
    wtmp2 = sb("wtmp2", [128, NCH * D])
    qrT = sb("qrT", [128, NCH * D], BF16)
    nc.sbuf_base = base_A2
    a_x = sb("a_x", [128, 2 * D])
    a_y = sb("a_y", [128, D])
    a_i = sb("a_i", [128, D], I32)
    a_qtm = sb("a_qtm", [128, D], BF16)
    a_qfm = sb("a_qfm", [128, D], BF16)
    a_qf = sb("a_qf", [128, TILE])
    a_x1 = sb("a_x1", [128, D])
    z2buf = sb("z2buf", [128, 2 * 4 * D])
    a_z2 = sb("a_z2", [128, D])
    a_nhi = sb("a_nhi", [128, NCH * TILE], BF16)
    a_nlo = sb("a_nlo", [128, NCH * TILE], BF16)
    a_kvhi = sb("a_kvhi", [128, 2 * TILE], BF16)
    a_kvlo = sb("a_kvlo", [128, 2 * TILE], BF16)
    endA = nc.sbuf_base
    assert endA - base_R <= 186 * 1024, f"A over: {(endA - base_R)/1024}KB"

    nc.sbuf_base = base_R
    b_ctx = sb("b_ctx", [128, nt * D])
    b_khi = sb("b_khi", [128, 3 * NCH * TILE], BF16)
    b_klo = sb("b_klo", [128, 3 * NCH * TILE], BF16)
    b_vhi = sb("b_vhi", [128, 3 * 4 * D], BF16)
    b_vlo = sb("b_vlo", [128, 3 * 4 * D], BF16)
    b_qhi = sb("b_qhi", [128, 2 * NCH * QCH], BF16)
    b_qlo = sb("b_qlo", [128, 2 * NCH * QCH], BF16)
    b_p = sb("b_p", [128, 2 * TILE])
    b_pthi = sb("b_pthi", [128, 4 * QCH], BF16)
    b_ptlo = sb("b_ptlo", [128, 4 * QCH], BF16)
    endB = nc.sbuf_base
    assert endB - base_R <= 186 * 1024, f"B over: {(endB - base_R)/1024}KB"

    nc.sbuf_base = base_R
    wo_hi = sb("wo_hi", [128, NCH * D], BF16); wo_lo = sb("wo_lo", [128, NCH * D], BF16)
    c_ctx = sb("c_ctx", [128, 2 * 4 * D])
    c_ctxn = sb("c_ctxn", [128, D])
    c_fhi = sb("c_fhi", [128, NCH * TILE], BF16)
    c_flo = sb("c_flo", [128, NCH * TILE], BF16)
    c_y = sb("c_y", [128, NCH * TILE])
    c_x1g = sb("c_x1g", [128, 2 * 4 * D])
    c_x2 = sb("c_x2", [128, 2 * 4 * D])
    c_z = sb("c_z", [128, D]); c_yq = sb("c_yq", [128, D])
    c_ii = sb("c_ii", [128, D], I32)
    c_qtm = sb("c_qtm", [128, 2 * D], BF16); c_qfm = sb("c_qfm", [128, D], BF16)
    endC = nc.sbuf_base
    assert endC - base_R <= 186 * 1024, f"C over: {(endC - base_R)/1024}KB"

    def psn(name, shape, dt=F32):
        return nc.alloc_psum_tensor(name, shape, dt)

    ps0 = nc.psum_base
    pF0 = psn("pF0", [128, 512]); pF1 = psn("pF1", [128, 512])
    pTb = psn("pTb", [128, 1024], BF16)
    pTf0 = psn("pTf0", [128, 512]); pTf1 = psn("pTf1", [128, 512])
    pA0 = psn("pA0", [128, 512]); pA1 = psn("pA1", [128, 512])
    pX = psn("pX", [128, 512])
    nc.psum_base = ps0
    pS0 = psn("pS0", [128, 512]); pS1 = psn("pS1", [128, 512])
    pPT0 = psn("pPT0", [128, 512]); pPT1 = psn("pPT1", [128, 512])
    pC0 = psn("pC0", [128, 1024]); pC1 = psn("pC1", [128, 1024])
    nc.psum_base = ps0
    pO0 = psn("pO0", [128, 512]); pO1 = psn("pO1", [128, 512])
    pCT0 = psn("pCT0", [128, 512]); pCT1 = psn("pCT1", [128, 512])
    pCF0 = psn("pCF0", [128, 1024]); pCb = psn("pCb", [128, 2048], BF16)

    S = Sched()
    A = lambda h: h.ap()

    def mm3(psum_ap, pairs, wait_first=(), mark=False):
        n = len(pairs)
        tok = None
        for i, (lh, ll, rh, rl) in enumerate(pairs):
            for j, (a_, b_) in enumerate(((lh, rh), (lh, rl), (ll, rh))):
                first = (i == 0 and j == 0)
                last = (i == n - 1 and j == 2)
                tok = S.do("tensor",
                           (lambda e, a_=a_, b_=b_, first=first, last=last, p=psum_ap:
                            e.matmul(p, a_, b_, start=first, stop=last)),
                           wait=(wait_first if first else ()),
                           inc=("TE" if (last and mark) else None))
        return tok

    def emit_ln(x_ap, z_ap, wait_x_v=(), wait_x_s=(), sq=None):
        S.do("vector", lambda e, x=x_ap: e.tensor_reduce(A(c_a)[:, 0:1], x, axis=AX.X, op=ALU.add),
             wait=wait_x_v, drain=True)
        S.do("vector", lambda e: e.tensor_scalar(A(c_a)[:, 0:1], A(c_a)[:, 0:1], -1.0 / D, None, ALU.mult),
             inc="VE", drain=True)
        vm = S.val("VE")
        S.do("scalar", lambda e, x=x_ap, q=sq: e.activation(q, x, AF.Square, bias=A(c_a)[:, 0:1],
                                                      scale=1.0, accum_out=A(c_b)[:, 0:1]),
             wait=tuple(wait_x_s) + (("VE", vm),), inc="SE", drain=True)
        S.do("scalar", lambda e: e.activation(A(c_c)[:, 0:1], A(c_b)[:, 0:1], AF.Ln,
                                              bias=A(epsc), scale=1.0 / D), drain=True)
        S.do("scalar", lambda e: e.activation(A(c_c)[:, 0:1], A(c_c)[:, 0:1], AF.Exp,
                                              bias=0.0, scale=-0.5), inc="SE")
        sm = S.val("SE")
        S.do("vector", lambda e: e.tensor_tensor(A(c_d)[:, 0:1], A(c_a)[:, 0:1], A(c_c)[:, 0:1], ALU.mult),
             wait=[("SE", sm)], inc="VE", drain=True)
        vm2 = S.val("VE")
        tok = S.do("scalar", lambda e, x=x_ap, z=z_ap: e.activation(z, x, AF.Identity,
                                                                    bias=A(c_d)[:, 0:1], scale=A(c_c)[:, 0:1]),
                   wait=[("VE", vm2)], inc="SE")
        return tok

    def emit_quant(z_ap, qtm_ap, wait_z_v, scale_col, ybuf=None, ibuf=None, dbuf=None):
        S.do("vector", lambda e, z=z_ap: e.reduce_max(A(c_e)[:, 0:1], z, axis=AX.X,
                                                      apply_absolute_value=True),
             wait=wait_z_v, drain=True)
        S.do("vector", lambda e: e.tensor_scalar(scale_col, A(c_e)[:, 0:1], 1e-12, None, ALU.max),
             drain=True)
        S.do("vector", lambda e: e.reciprocal(A(c_f)[:, 0:1], scale_col), drain=True)
        S.do("vector", lambda e: e.tensor_scalar(A(c_f)[:, 0:1], A(c_f)[:, 0:1], 127.0, None, ALU.mult),
             inc="VE", drain=True)
        vm = S.val("VE")
        S.do("scalar", lambda e, z=z_ap, y=ybuf: e.activation(y, z, AF.Identity, bias=0.0,
                                                      scale=A(c_f)[:, 0:1]),
             wait=[("VE", vm)], inc="SE")
        sm = S.val("SE")
        S.do("vector", lambda e, y=ybuf, ib=ibuf: e.tensor_copy(ib, y), wait=[("SE", sm)])
        S.do("vector", lambda e, y=ybuf, ib=ibuf, db=dbuf: e.tensor_tensor(db, y, ib, ALU.subtract))
        S.do("vector", lambda e, y=ybuf, db=dbuf: e.tensor_scalar(y, db, -0.5, None, ALU.is_le))
        S.do("vector", lambda e, y=ybuf, db=dbuf: e.scalar_tensor_tensor(db, db, 0.5, y,
                                                                         ALU.is_ge, ALU.subtract))
        tok = S.do("vector", lambda e, q=qtm_ap, ib=ibuf, db=dbuf: e.tensor_tensor(q, db, ib, ALU.add),
                   inc="VE")
        return tok

    S.dma("sync", A(mask), maskin, inc="lc")
    S.dma("sync", A(s_idf), idf, inc="lc")
    S.dma("sync", A(s_idb), idb, inc="lc")
    S.dma("sync", A(s_ones), ones_in, inc="lc")
    for (dst, src) in ((bq32c, bq_d), (bkc, bk_d), (bvc, bv_d), (boc, bo_d)):
        S.dma("sync", A(dst), src.rearrange("(c p) -> p c", p=128), slow=True, inc="lc")
    t_consts = S.val("lc")
    S.do("vector", lambda e: e.memset(A(epsc), 1e-5))
    S.do("vector", lambda e: e.tensor_scalar(A(bq32c), A(bq32c), 1.0 / 32.0, None, ALU.mult),
         wait=[("lc", t_consts)], inc="VE")

    def split_w(w_dram, hi_sb, lo_sb, prev_v, q="sync", buf=None, sem="lw"):
        b = buf if buf is not None else wtmp
        t = S.dma(q, A(b).rearrange("p (c d) -> p c d", c=8),
                  w_dram.rearrange("(c p) d -> p c d", p=128),
                  wait=([("VE", prev_v)] if prev_v else ()), inc=sem)
        S.do("vector", lambda e, h=hi_sb, b=b: e.tensor_copy(A(h), A(b)), wait=[(sem, t)])
        tl = S.do("vector", lambda e, h=hi_sb, l=lo_sb, b=b: e.tensor_tensor(A(l), A(b), A(h), ALU.subtract),
                  inc="VE")
        return tl

    v1 = split_w(wq_d, wq_hi, wq_lo, None)
    v2 = split_w(wk_d, wk_hi, wk_lo, None, q="gpsimd", buf=wtmp2, sem="lw2")
    v3 = split_w(wv_d, wv_hi, wv_lo, v2, q="gpsimd", buf=wtmp2, sem="lw2")

    t_rhs = S.dma("sync", A(wtmp).rearrange("p (c d) -> p c d", c=8),
                  frhs.rearrange("(c p) d -> p c d", p=128),
                  wait=[("VE", v1)], inc="lw")

    def transpose_pack_f32(src_slices, dst_slices, bank, wait_first, evac_eng="vector"):
        for j in range(4):
            w = list(wait_first) if j == 0 else []
            S.do("tensor",
                 (lambda e, j=j, s=src_slices[j], bank=bank:
                  e.transpose(A(bank)[:, j * 128:(j + 1) * 128], s, A(s_idf))),
                 wait=w, inc=("TE" if j == 3 else None))
        tm = S.val("TE")
        tok = None
        for j in range(4):
            tok = S.do(evac_eng,
                       (lambda e, j=j, d=dst_slices[j], bank=bank:
                        e.tensor_copy(d, A(bank)[:, j * 128:(j + 1) * 128])),
                       wait=([("TE", tm)] if j == 0 else ()),
                       inc=("VE" if evac_eng == "vector" else "SE"))
        return tok

    last_ev = None
    for g in range(16):
        bank = pTf0 if g % 2 == 0 else pTf1
        wf = [("lw", t_rhs)] if g > 0 else [("lw", t_rhs), ("VE", v3)]
        if last_ev is not None and g >= 2:
            wf.append(("VE", S.val("VE")))
        srcs, dsts = [], []
        for j in range(4):
            idx = g * 4 + j
            di, do = idx // 8, idx % 8
            srcs.append(A(wtmp)[:, di * 1024 + do * 128: di * 1024 + (do + 1) * 128])
            dsts.append(A(wtmp2)[:, do * 1024 + di * 128: do * 1024 + (di + 1) * 128])
        last_ev = transpose_pack_f32(srcs, dsts, bank, wf)

    if debug:
        S.dma("gpsimd", dbg["d_wt2"], A(wtmp2), wait=[("VE", last_ev)], inc="sdbg")
    for cdo in range(8):
        S.do("vector",
             (lambda e, cdo=cdo: e.reduce_max(A(c_g)[:, cdo:cdo + 1],
                                              A(wtmp2)[:, cdo * 1024:(cdo + 1) * 1024],
                                              axis=AX.X, apply_absolute_value=True)),
             wait=([("VE", last_ev)] if cdo == 0 else ()), drain=(cdo == 7))
    S.do("vector", lambda e: e.tensor_scalar(A(c_g), A(c_g), 1e-12, None, ALU.max), drain=True)
    for _rc in range(8):
        S.do("vector", (lambda e, c=_rc: e.reciprocal(A(c_h)[:, c:c + 1], A(c_g)[:, c:c + 1])),
             drain=(_rc == 7))
    S.do("vector", lambda e: e.tensor_scalar(A(c_h), A(c_h), 127.0, None, ALU.mult), inc="VE",
         drain=True)
    vq = S.val("VE")
    if debug:
        S.dma("gpsimd", dbg["d_cg"], A(c_g), wait=[("VE", vq)], inc="sdbg")
        S.dma("gpsimd", dbg["d_ch"], A(c_h), wait=[("VE", vq)], inc="sdbg")
    for cdo in range(8):
        S.do("scalar",
             (lambda e, cdo=cdo: e.activation(A(wtmp)[:, 0:1024],
                                              A(wtmp2)[:, cdo * 1024:(cdo + 1) * 1024],
                                              AF.Identity, bias=0.0, scale=A(c_h)[:, cdo:cdo + 1])),
             wait=([("VE", vq)] if cdo == 0 else [("VE", S.val("VE"))]), inc="SE")
        sm = S.val("SE")
        S.do("vector", lambda e: e.tensor_copy(A(a_i), A(wtmp)[:, 0:1024]), wait=[("SE", sm)])
        S.do("vector", lambda e: e.tensor_tensor(A(wtmp)[:, 2048:3072], A(wtmp)[:, 0:1024], A(a_i), ALU.subtract))
        S.do("vector", lambda e: e.tensor_scalar(A(wtmp)[:, 1024:2048], A(wtmp)[:, 2048:3072], -0.5, None, ALU.is_le))
        S.do("vector", lambda e: e.scalar_tensor_tensor(A(wtmp)[:, 2048:3072], A(wtmp)[:, 2048:3072], 0.5,
                                                        A(wtmp)[:, 1024:2048], ALU.is_ge, ALU.subtract))
        S.do("vector",
             (lambda e, cdo=cdo: e.tensor_tensor(A(qrT)[:, cdo * 1024:(cdo + 1) * 1024],
                                                 A(wtmp)[:, 2048:3072], A(a_i), ALU.add)),
             inc="VE")
    vq2 = S.val("VE")
    if debug:
        S.dma("gpsimd", dbg["d_qrT"], A(qrT), wait=[("VE", vq2)], inc="sdbg")
    last_ev = None
    for g in range(8):
        wf = [("VE", vq2)]
        if last_ev is not None and g >= 1:
            wf.append(("VE", S.val("VE")))
        for j in range(8):
            idx = g * 8 + j
            do, di = idx // 8, idx % 8
            do = g
            di = j
            S.do("tensor",
                 (lambda e, do=do, di=di, j=j:
                  e.transpose(A(pTb)[:, j * 128:(j + 1) * 128],
                              A(qrT)[:, do * 1024 + di * 128: do * 1024 + (di + 1) * 128],
                              A(s_idb))),
                 wait=(wf if j == 0 else ()), inc=("TE" if j == 7 else None))
        tm = S.val("TE")
        for j in range(8):
            di = j
            do = g
            last_ev = S.do("vector",
                           (lambda e, di=di, do=do, j=j:
                            e.tensor_copy(A(q_rhs)[:, di * 1024 + do * 128: di * 1024 + (do + 1) * 128],
                                          A(pTb)[:, j * 128:(j + 1) * 128])),
                           wait=([("TE", tm)] if j == 0 else ()), inc="VE")
    S.do("vector", lambda e: e.tensor_scalar(A(c_g), A(c_g), 1.0 / 127.0, None, ALU.mult),
         inc="VE")
    v_rs = S.val("VE")
    t_st = S.dma("gpsimd", rs_bounce.rearrange("c p -> p c"), A(c_g), wait=[("VE", v_rs)], inc="sb", slow=True)
    t_ld = S.dma("sync", A(wtmp)[0:1, 3072:4096],
                 rs_bounce.rearrange("c p -> (c p)").rearrange("(a b) -> a b", a=1),
                 wait=[("sb", t_st)], inc="lw")
    S.do("tensor", lambda e: e.matmul(A(pX), A(s_ones), A(wtmp)[0:1, 3072:3584], start=True, stop=True),
         wait=[("lw", t_ld), ("VE", S.val("VE"))])
    S.do("tensor", lambda e: e.matmul(A(pA0), A(s_ones), A(wtmp)[0:1, 3584:4096], start=True, stop=True),
         inc="TE")
    tm = S.val("TE")
    S.do("vector", lambda e: e.tensor_copy(A(rs_bc)[:, 0:512], A(pX)), wait=[("TE", tm)])
    v_rsbc = S.do("vector", lambda e: e.tensor_copy(A(rs_bc)[:, 512:1024], A(pA0)), inc="VE")
    if debug:
        S.dma("gpsimd", dbg["d_rsbc"], A(rs_bc), wait=[("VE", v_rsbc)], inc="sdbg")
        S.dma("gpsimd", dbg["d_qrhs"], A(q_rhs), wait=[("VE", S.val("VE"))], inc="sdbg")
        S.dma("gpsimd", dbg["d_qrhs"] if "d_qrhs" in dbg else xout, A(q_rhs), inc="st") if "d_qrhs" in dbg else None

    t_wo = S.dma("sync", A(wtmp).rearrange("p (c d) -> p c d", c=8),
                 wo_d.rearrange("(c p) d -> p c d", p=128),
                 wait=[("VE", S.val("VE")), ("SE", S.val("SE"))], inc="lw")
    for h in range(2):
        wv_h = [("lw", t_wo)] if h == 0 else [("swo", S.val("swo"))]
        S.do("vector",
             (lambda e, h=h: e.tensor_copy(A(qrT)[:, 0:4096], A(wtmp)[:, h * 4096:(h + 1) * 4096])),
             wait=wv_h)
        v_h = S.do("vector",
                   (lambda e, h=h: e.tensor_tensor(A(qrT)[:, 4096:8192],
                                                   A(wtmp)[:, h * 4096:(h + 1) * 4096],
                                                   A(qrT)[:, 0:4096], ALU.subtract)), inc="VE")
        S.dma("gpsimd", wo_scr[0][:, h * 4096:(h + 1) * 4096], A(qrT)[:, 0:4096],
              wait=[("VE", v_h)], inc="swo")
        t_wl = S.dma("gpsimd", wo_scr[1][:, h * 4096:(h + 1) * 4096], A(qrT)[:, 4096:8192],
                     inc="swo")
    v_wo = S.val("VE")
    for dh in range(2):
        po = pA0 if dh == 0 else pA1
        for dv in range(8):
            S.do("tensor",
                 (lambda e, dv=dv, dh=dh, po=po:
                  e.matmul(A(po)[0:1, :], A(bvc)[:, dv:dv + 1],
                           A(wtmp)[:, dv * 1024 + dh * 512: dv * 1024 + dh * 512 + 512],
                           start=(dv == 0), stop=(dv == 7))),
                 wait=([("VE", v_wo), ("TE", S.val("TE"))] if (dv == 0 and dh == 0) else ()),
                 inc=("TE" if dv == 7 else None))
    t_bop = S.val("TE")
    S.do("vector", lambda e: e.tensor_copy(A(wtmp)[0:1, 0:512], A(pA0)[0:1, :]), wait=[("TE", t_bop)])
    S.do("vector", lambda e: e.tensor_copy(A(wtmp)[0:1, 512:1024], A(pA1)[0:1, :]), inc="VE")
    v_bop = S.val("VE")
    t_bb = S.dma("gpsimd", bo_bounce.rearrange("(a b) -> a b", a=1), A(wtmp)[0:1, 0:1024],
                 wait=[("VE", v_bop)], inc="sb")
    t_bl = S.dma("sync", A(bopc), bo_bounce.rearrange("(c p) -> p c", p=128), wait=[("sb", t_bb)],
                 slow=True, inc="lw")
    v_boc = S.do("vector", lambda e: e.tensor_tensor(A(bopc), A(bopc), A(boc), ALU.add),
                 wait=[("lw", t_bl)], inc="VE")

    setup_VE = S.val("VE"); setup_SE = S.val("SE"); setup_TE = S.val("TE")

    x_cons_tok = {}
    nhi_cons_tok = None
    kv_store_tok = None
    q_store_tok = None
    x1_store_tok = None

    proj_bank_free = None
    bfree = None
    x_cons_tok = {}
    trf_done = {}
    nhi_cons_tok = None
    kv_store_tok = {"skv0": None, "skv1": None, "sq0": None, "sq1": None}
    q_store_tok = None
    x1_store_tok = None
    pb_state = {"k_se": {}, "k_ve": {}, "v_se": {}, "v_ve": {}, "q_se": {}, "q_ve": {}}

    P_state = {}

    def emit_P_front(t, c):
        g = t * 4 + c
        par = g % 2
        xap = A(a_x)[:, par * D:(par + 1) * D]
        w = [("VE", setup_VE), ("SE", setup_SE), ("TE", setup_TE)] if g == 0 else []
        if g >= 2 and (g - 2) in x_cons_tok:
            w.append(("VE", x_cons_tok[g - 2]))
        t_x = S.dma("sync", xap, xin[t * TILE + c * 128: t * TILE + (c + 1) * 128, :], wait=w,
                    inc=("lx0" if par == 0 else "lx1"))
        emit_ln(xap, A(a_z2), wait_x_v=[("lx0" if par == 0 else "lx1", t_x)], sq=A(a_y))
        wq2 = [("SE", S.val("SE"))]
        vtok = emit_quant(A(a_z2), A(a_qtm), wq2, A(c_e)[:, 1:2],
                          ybuf=A(a_y), ibuf=A(a_i), dbuf=A(a_z2))
        P_state[(t, c)] = vtok

    def emit_P_back(t, c):
        nonlocal x1_store_tok
        g = t * 4 + c
        par = g % 2
        zpar = t % 2
        xap = A(a_x)[:, par * D:(par + 1) * D]
        vtok = P_state.pop((t, c))
        wtr = [("VE", vtok), ("VE", S.val("VE"))]
        for j in range(8):
            S.do("tensor",
                 (lambda e, j=j: e.transpose(A(pTb)[:, j * 128:(j + 1) * 128],
                                             A(a_qtm)[:, j * 128:(j + 1) * 128], A(s_idb))),
                 wait=(wtr if j == 0 else ()), inc=("TE" if j == 7 else None))
        tm = S.val("TE")
        qfm_tok = S.do("vector", lambda e: e.tensor_copy(A(a_qfm), A(pTb)), wait=[("TE", tm)],
                       inc="VE")
        for dh in range(2):
            pf = pF0 if dh == 0 else pF1
            for di in range(8):
                first = di == 0
                last = di == 7
                S.do("tensor",
                     (lambda e, di=di, dh=dh, pf=pf, first=first, last=last:
                      e.matmul(A(pf), A(a_qfm)[:, di * 128:(di + 1) * 128],
                               A(q_rhs)[:, di * 1024 + dh * 512: di * 1024 + dh * 512 + 512],
                               start=first, stop=last)),
                     wait=([("VE", qfm_tok)] if first else ()),
                     inc=("TE" if last else None))
        tf = S.val("TE")
        S.do("vector", lambda e: e.tensor_scalar(A(c_f)[:, 1:2], A(c_e)[:, 1:2], 0.5 / 127.0,
                                                 None, ALU.mult), drain=True)
        for dh in range(2):
            pf = pF0 if dh == 0 else pF1
            S.do("vector",
                 (lambda e, pf=pf, dh=dh:
                  e.scalar_tensor_tensor(A(a_y)[:, dh * 512:(dh + 1) * 512], A(pf),
                                         A(c_f)[:, 1:2], A(rs_bc)[:, dh * 512:(dh + 1) * 512],
                                         ALU.mult, ALU.mult)),
                 wait=([("TE", tf)] if dh == 0 else ()))
        xw = []
        if x1_store_tok is not None:
            xw.append(("sx1", x1_store_tok))
        xw.append(("SE", S.val("SE")))
        xe = S.do("vector",
                  (lambda e, xap=xap: e.tensor_tensor(A(a_x1), xap, A(a_y), ALU.add)),
                  wait=xw, inc="VE")
        x_cons_tok[g] = xe
        x1_store_tok = S.dma("gpsimd", x1s[t * 4 + c], A(a_x1), wait=[("VE", xe)], inc="sx1")
        if debug:
            x1_store_tok = S.dma("gpsimd", dbg["d_x1"][t * 4 + c], A(a_x1), inc="sx1")
        wz = [("VE", xe)]
        if (t - 2) in trf_done:
            wz.append(("TE", trf_done[t - 2]))
        emit_ln(A(a_x1), A(z2buf)[:, zpar * 4 * D + c * D: zpar * 4 * D + (c + 1) * D],
                wait_x_v=wz, sq=A(a_y))

    def emit_W_trf(t):
        nonlocal nhi_cons_tok
        zpar = t % 2
        bank_tok = {0: None, 1: None}
        for c in range(4):
            for gg in range(2):
                bank = pTf0 if gg == 0 else pTf1
                wf = [("SE", S.val("SE"))]
                if bank_tok[gg] is not None:
                    wf.append(("VE", bank_tok[gg]))
                if nhi_cons_tok is not None and c == 0 and gg == 0:
                    wf.append(("TE", nhi_cons_tok))
                if t == 0 and c == 0 and gg == 0:
                    wf.append(("swo", S.val("swo")))
                for j in range(4):
                    dv = gg * 4 + j
                    S.do("tensor",
                         (lambda e, dv=dv, j=j, bank=bank, c=c, zpar=zpar:
                          e.transpose(A(bank)[:, j * 128:(j + 1) * 128],
                                      A(z2buf)[:, zpar * 4 * D + c * D + dv * 128:
                                               zpar * 4 * D + c * D + (dv + 1) * 128],
                                      A(s_idf))),
                         wait=(wf if j == 0 else ()), inc=("TE" if j == 3 else None))
                tm2 = S.val("TE")
                dst_h = A(a_nhi).rearrange("p (dv t) -> p dv t", dv=NCH)[
                    :, gg * 4:(gg + 1) * 4, c * 128:(c + 1) * 128]
                dst_l = A(a_nlo).rearrange("p (dv t) -> p dv t", dv=NCH)[
                    :, gg * 4:(gg + 1) * 4, c * 128:(c + 1) * 128]
                S.do("scalar", (lambda e, d=dst_h, bank=bank: e.activation(d, A(bank), AF.Identity,
                                                                            bias=0.0, scale=1.0)),
                     wait=[("TE", tm2)], inc="SE")
                sh2 = S.val("SE")
                S.do("vector", (lambda e, d=dst_l, h=dst_h, bank=bank:
                                e.tensor_tensor(d, A(bank), h, ALU.subtract)),
                     wait=[("SE", sh2)], inc="VE")
                bank_tok[gg] = S.val("VE")
        trf_done[t] = S.val("TE")
        return S.val("VE")

    def emit_W_K(t, do_list, v_nsplit):
        st = pb_state
        for do in do_list:
            pa = pA0 if do % 2 == 0 else pA1
            wf = [("VE", v_nsplit)]
            if do >= 2:
                wf += [("SE", st["k_se"][do - 2]), ("VE", st["k_ve"][do - 2])]
            elif proj_bank_free is not None:
                wf += proj_bank_free[do % 2]
            pairs = [(A(wk_hi)[:, di * 1024 + do * 128: di * 1024 + (do + 1) * 128],
                      A(wk_lo)[:, di * 1024 + do * 128: di * 1024 + (do + 1) * 128],
                      A(a_nhi)[:, di * TILE:(di + 1) * TILE],
                      A(a_nlo)[:, di * TILE:(di + 1) * TILE]) for di in range(8)]
            mm3(A(pa), pairs, wait_first=wf, mark=True)
            tmk = S.val("TE")
            ssem = "skv0" if do % 2 == 0 else "skv1"
            slot = do % 2
            wk_ev = [("TE", tmk)]
            if kv_store_tok[ssem] is not None:
                wk_ev.append((ssem, kv_store_tok[ssem]))
            if kv_store_tok.get("sq%d" % slot) is not None:
                wk_ev.append(("sq%d" % slot, kv_store_tok["sq%d" % slot]))
            S.do("scalar",
                 (lambda e, pa=pa, do=do, slot=slot:
                  e.activation(A(a_kvhi)[:, slot * TILE:(slot + 1) * TILE], A(pa), AF.Identity,
                               bias=A(bkc)[:, do:do + 1], scale=1.0)),
                 wait=wk_ev, inc="SE")
            st["k_se"][do] = S.val("SE")
            S.do("vector",
                 (lambda e, pa=pa, do=do, slot=slot:
                  e.scalar_tensor_tensor(A(a_kvlo)[:, slot * TILE:(slot + 1) * TILE], A(pa),
                                         A(bkc)[:, do:do + 1],
                                         A(a_kvhi)[:, slot * TILE:(slot + 1) * TILE],
                                         ALU.add, ALU.subtract)),
                 wait=[("SE", st["k_se"][do])], inc="VE")
            st["k_ve"][do] = S.val("VE")
            S.dma("gpsimd",
                  contrib[t, 0:128, :].rearrange("p (qb dk k) -> p qb dk k", qb=4, dk=8)[:, :, do, :],
                  A(a_kvhi)[:, slot * TILE:(slot + 1) * TILE].rearrange("p (qb k) -> p qb k", qb=4),
                  wait=[("VE", st["k_ve"][do])], inc=ssem)
            kv_store_tok[ssem] = S.dma("gpsimd",
                  contrib[t, 128:256, :].rearrange("p (qb dk k) -> p qb dk k", qb=4, dk=8)[:, :, do, :],
                  A(a_kvlo)[:, slot * TILE:(slot + 1) * TILE].rearrange("p (qb k) -> p qb k", qb=4),
                  inc=ssem)

    def emit_W_V(t, tc_list, v_nsplit):
        st = pb_state
        for tc in tc_list:
            for dh in range(2):
                idx = tc * 2 + dh
                pa = pA0 if idx % 2 == 0 else pA1
                wf = []
                if idx >= 2:
                    wf += [("SE", st["v_se"][idx - 2]), ("VE", st["v_ve"][idx - 2])]
                else:
                    wf += [("SE", st["k_se"][6 + idx]), ("VE", st["k_ve"][6 + idx])]
                pairs = [(A(a_nhi)[:, di * TILE + tc * 128: di * TILE + (tc + 1) * 128],
                          A(a_nlo)[:, di * TILE + tc * 128: di * TILE + (tc + 1) * 128],
                          A(wv_hi)[:, di * 1024 + dh * 512: di * 1024 + dh * 512 + 512],
                          A(wv_lo)[:, di * 1024 + dh * 512: di * 1024 + dh * 512 + 512])
                         for di in range(8)]
                mm3(A(pa), pairs, wait_first=wf, mark=True)
                tmk = S.val("TE")
                ssem = "skv0" if idx % 2 == 0 else "skv1"
                slot = idx % 2
                wv_ev = [("TE", tmk)]
                if kv_store_tok[ssem] is not None:
                    wv_ev.append((ssem, kv_store_tok[ssem]))
                S.do("scalar",
                     (lambda e, pa=pa, slot=slot:
                      e.activation(A(a_kvhi)[:, slot * TILE:(slot + 1) * TILE], A(pa),
                                   AF.Identity, bias=0.0, scale=1.0)),
                     wait=wv_ev, inc="SE")
                st["v_se"][idx] = S.val("SE")
                S.do("vector",
                     (lambda e, pa=pa, slot=slot:
                      e.tensor_tensor(A(a_kvlo)[:, slot * TILE:(slot + 1) * TILE], A(pa),
                                      A(a_kvhi)[:, slot * TILE:(slot + 1) * TILE],
                                      ALU.subtract)),
                     wait=[("SE", st["v_se"][idx])], inc="VE")
                st["v_ve"][idx] = S.val("VE")
                S.dma("gpsimd", contrib[t, 256:384, tc * D + dh * 512: tc * D + dh * 512 + 512],
                      A(a_kvhi)[:, slot * TILE:(slot + 1) * TILE],
                      wait=[("VE", st["v_ve"][idx])], inc=ssem)
                kv_store_tok[ssem] = S.dma("gpsimd", contrib[t, 384:512, tc * D + dh * 512: tc * D + dh * 512 + 512],
                                           A(a_kvlo)[:, slot * TILE:(slot + 1) * TILE], inc=ssem)

    def emit_W_Q(t, v_nsplit):
        nonlocal nhi_cons_tok, q_store_tok, proj_bank_free
        st = pb_state
        for do in range(8):
            pa = pA0 if do % 2 == 0 else pA1
            wf = []
            if do >= 2:
                wf += [("SE", st["q_se"][do - 2]), ("VE", st["q_ve"][do - 2])]
            else:
                wf += [("SE", st["v_se"][6 + do]), ("VE", st["v_ve"][6 + do])]
            pairs = [(A(wq_hi)[:, di * 1024 + do * 128: di * 1024 + (do + 1) * 128],
                      A(wq_lo)[:, di * 1024 + do * 128: di * 1024 + (do + 1) * 128],
                      A(a_nhi)[:, di * TILE:(di + 1) * TILE],
                      A(a_nlo)[:, di * TILE:(di + 1) * TILE]) for di in range(8)]
            nhi_cons_tok = mm3(A(pa), pairs, wait_first=wf, mark=True)
            tmk = S.val("TE")
            ssem = "sq0" if do % 2 == 0 else "sq1"
            slot = do % 2
            wq_ev = [("TE", tmk), ("VE", S.val("VE"))]
            if kv_store_tok["skv%d" % slot] is not None:
                wq_ev.append(("skv%d" % slot, kv_store_tok["skv%d" % slot]))
            if kv_store_tok.get(ssem) is not None:
                wq_ev.append((ssem, kv_store_tok[ssem]))
            S.do("scalar",
                 (lambda e, do=do, pa=pa: e.activation(A(a_qf), A(pa), AF.Identity,
                                                       bias=A(bq32c)[:, do:do + 1], scale=1.0 / 32.0)),
                 wait=wq_ev, inc="SE")
            st["q_se"][do] = S.val("SE")
            sm = S.val("SE")
            S.do("vector", (lambda e, slot=slot: e.tensor_copy(A(a_kvhi)[:, slot * TILE:(slot + 1) * TILE],
                                                               A(a_qf))),
                 wait=[("SE", sm)])
            S.do("vector", (lambda e, slot=slot: e.tensor_tensor(A(a_kvlo)[:, slot * TILE:(slot + 1) * TILE],
                                                                 A(a_qf),
                                                                 A(a_kvhi)[:, slot * TILE:(slot + 1) * TILE],
                                                                 ALU.subtract)),
                 inc="VE")
            st["q_ve"][do] = S.val("VE")
            S.dma("gpsimd",
                  qx_hi[t].rearrange("p (cc dk j) -> p cc dk j", cc=4, dk=8)[:, :, do, :],
                  A(a_kvhi)[:, slot * TILE:(slot + 1) * TILE].rearrange("p (cc j) -> p cc j", cc=4),
                  wait=[("VE", st["q_ve"][do])], inc=ssem)
            kv_store_tok[ssem] = S.dma("gpsimd",
                  qx_lo[t].rearrange("p (cc dk j) -> p cc dk j", cc=4, dk=8)[:, :, do, :],
                  A(a_kvlo)[:, slot * TILE:(slot + 1) * TILE].rearrange("p (cc j) -> p cc j", cc=4),
                  inc=ssem)
        proj_bank_free = {0: [("SE", st["q_se"][6]), ("VE", st["q_ve"][6])],
                          1: [("SE", st["q_se"][7]), ("VE", st["q_ve"][7])]}

    REPL = [[0, 1, 2, 3], [4, 5, 6, 7]]
    cc_tok = {}
    for c in range(4):
        emit_P_front(0, c)
        emit_P_back(0, c)
    emit_P_front(1, 0) if NS > 1 else None
    for t in range(NS):
        pb_state = {"k_se": {}, "k_ve": {}, "v_se": {}, "v_ve": {}, "q_se": {}, "q_ve": {}}
        nxt = t + 1 < NS
        v_nsplit = emit_W_trf(t)
        if nxt:
            emit_P_back(t + 1, 0)
            emit_P_front(t + 1, 1)
        emit_W_K(t, [0, 1, 2, 3], v_nsplit)
        if nxt:
            emit_P_back(t + 1, 1)
            emit_P_front(t + 1, 2)
        emit_W_K(t, [4, 5, 6, 7], v_nsplit)
        if nxt:
            emit_P_back(t + 1, 2)
            emit_P_front(t + 1, 3)
        emit_W_V(t, [0, 1], v_nsplit)
        if nxt:
            emit_P_back(t + 1, 3)
        emit_W_V(t, [2, 3], v_nsplit)
        if t == NS - 1:
            bfree = [("TE", S.val("TE")), ("VE", S.val("VE")), ("SE", S.val("SE")),
                     ("sx1", S.val("sx1"))]
        if t + 2 < NS:
            emit_P_front(t + 2, 0)
        cc_tok[t] = S.do("gpsimd",
                         (lambda e, t=t: e.collective_compute(
                             "AllGather", ALU.bypass, replica_groups=REPL,
                             ins=[contrib[t]], outs=[gath[t]])),
                         wait=[("skv0", S.val("skv0")), ("skv1", S.val("skv1"))],
                         inc="cc")
        emit_W_Q(t, v_nsplit)

    A_TE = S.val("TE"); A_VE = S.val("VE"); A_SE = S.val("SE")

    bwait = [("TE", A_TE), ("VE", A_VE), ("SE", A_SE),
             ("skv0", S.val("skv0")), ("skv1", S.val("skv1")),
             ("sq0", S.val("sq0")), ("sq1", S.val("sq1")), ("sx1", S.val("sx1"))]
    sq_totals = [("sq0", S.val("sq0")), ("sq1", S.val("sq1")), ("sx1", S.val("sx1"))]

    kv_ld_tok = {}
    q_ld_tok = {}
    it_idx = 0
    ctx_ev_tok = {}
    exp_tok = {}
    pt_tok = {}
    kt_last_iter = {}
    ctx_store = {}

    def load_kv(kt, extra_wait=()):
        buf = kt % 3
        sg, cc_ = kt // 4, kt % 4
        w = list(bwait) if kt == 0 else list(extra_wait)
        w.append(("cc", cc_tok[sg]))
        sem = ("lk0", "lk1", "lk2")[buf]
        t4 = None
        for r in range(4):
            S.dma("gpsimd", A(b_khi)[:, buf * NCH * TILE + r * 1024: buf * NCH * TILE + (r + 1) * 1024],
                  gath[sg, r * 512 + 0:r * 512 + 128, cc_ * 1024:(cc_ + 1) * 1024],
                  wait=(w if r == 0 else ()), inc=sem)
            S.dma("gpsimd", A(b_klo)[:, buf * NCH * TILE + r * 1024: buf * NCH * TILE + (r + 1) * 1024],
                  gath[sg, r * 512 + 128:r * 512 + 256, cc_ * 1024:(cc_ + 1) * 1024], inc=sem)
            S.dma("gpsimd", A(b_vhi)[:, buf * 4 * D + r * D:(buf * 4 + r + 1) * D],
                  gath[sg, r * 512 + 256:r * 512 + 384, cc_ * D:(cc_ + 1) * D], inc=sem)
            t4 = S.dma("gpsimd", A(b_vlo)[:, buf * 4 * D + r * D:(buf * 4 + r + 1) * D],
                       gath[sg, r * 512 + 384:r * 512 + 512, cc_ * D:(cc_ + 1) * D], inc=sem)
        kv_ld_tok[kt] = (sem, t4)

    iters = [(kt, k) for kt in range(nt) for k in range(kt, nt)]
    for kt in range(nt):
        kt_last_iter[kt] = max(i for i, (a, b) in enumerate(iters) if a == kt)

    load_kv(0)

    n_it = len(iters)
    t_qk_tok = {}
    t_tr_tok = {}
    t_pv_tok = {}

    def front(i):
        kt, k = iters[i]
        qpar = i % 2
        p4 = i % 4
        wq_ = []
        if i >= 2:
            wq_.append(("SE", exp_tok[i - 2]))
        else:
            wq_.append(("TE", trf_done[NS - 1]))
            wq_ += sq_totals
        qsem = "lq0" if qpar == 0 else "lq1"
        sgq, ccq = k // 4, k % 4
        S.dma("sync",
              A(b_qhi)[:, qpar * NCH * QCH:(qpar + 1) * NCH * QCH],
              qx_hi[sgq][:, ccq * 1024:(ccq + 1) * 1024],
              wait=wq_, inc=qsem)
        t2 = S.dma("sync",
                   A(b_qlo)[:, qpar * NCH * QCH:(qpar + 1) * NCH * QCH],
                   qx_lo[sgq][:, ccq * 1024:(ccq + 1) * 1024],
                   inc=qsem)
        q_ld_tok[i] = (qsem, t2)

        par = i % 2
        buf = kt % 3
        pS = pS0 if par == 0 else pS1
        wf = [kv_ld_tok[kt], q_ld_tok[i]]
        if i >= 2:
            wf.append(("SE", exp_tok[i - 2]))
        khv = A(b_khi)[:, buf * NCH * TILE:(buf + 1) * NCH * TILE].rearrange(
            "p (r di k) -> p r di k", r=4, di=NCH)
        klv = A(b_klo)[:, buf * NCH * TILE:(buf + 1) * NCH * TILE].rearrange(
            "p (r di k) -> p r di k", r=4, di=NCH)
        pairs = [(A(b_qhi)[:, qpar * NCH * QCH + di * 128: qpar * NCH * QCH + (di + 1) * 128],
                  A(b_qlo)[:, qpar * NCH * QCH + di * 128: qpar * NCH * QCH + (di + 1) * 128],
                  khv[:, :, di, :], klv[:, :, di, :])
                 for di in range(8)]
        mm3(A(pS), pairs, wait_first=wf, mark=True)
        t_qk_tok[i] = S.val("TE")

        wv_ = [("TE", t_qk_tok[i])]
        if kt == k:
            S.do("vector", lambda e, pS=pS: e.tensor_tensor(A(pS), A(pS), A(mask), ALU.add),
                 wait=wv_, inc="VE")
            wv_ = [("VE", S.val("VE"))]
        if i >= 1 and iters[i - 1][1] == k:
            wv_ = list(wv_) + [("SE", exp_tok[i - 1])]
        S.do("vector", lambda e, pS=pS: e.reduce_max(A(c_a)[:, 0:1], A(pS), axis=AX.X, negate=True),
             wait=wv_, drain=True)
        if kt == 0:
            S.do("vector", (lambda e, k=k: e.tensor_copy(A(nm_col)[:, k:k + 1], A(c_a)[:, 0:1])),
                 inc="VE", drain=True)
        else:
            S.do("vector", (lambda e, k=k: e.tensor_tensor(A(c_b)[:, 0:1], A(nm_col)[:, k:k + 1],
                                                           A(c_a)[:, 0:1], ALU.min)), drain=True)
            S.do("vector", (lambda e, k=k, p4=p4: e.tensor_tensor(A(c_c)[:, p4:p4 + 1], A(c_b)[:, 0:1],
                                                                  A(nm_col)[:, k:k + 1], ALU.subtract)))
            S.do("vector", (lambda e, k=k: e.tensor_copy(A(nm_col)[:, k:k + 1], A(c_b)[:, 0:1])),
                 inc="VE", drain=True)
        v_nm = S.val("VE")
        if kt > 0:
            S.do("scalar", (lambda e, p4=p4: e.activation(A(c_d)[:, p4:p4 + 1], A(c_c)[:, p4:p4 + 1],
                                                          AF.Exp, bias=0.0, scale=1.0)),
                 wait=[("VE", v_nm)], drain=True)
        exp_tok[i] = S.do("scalar",
                          (lambda e, pS=pS, k=k, par=par, p4=p4:
                           e.activation(A(b_p)[:, par * TILE:(par + 1) * TILE], A(pS), AF.Exp,
                                        bias=A(nm_col)[:, k:k + 1], scale=1.0,
                                        accum_out=A(c_e)[:, p4:p4 + 1])),
                          wait=[("VE", v_nm)], inc="SE", drain=True)

    def back_pre(i):
        kt, k = iters[i]
        par = i % 2
        p4 = i % 4
        pPT = pPT0 if par == 0 else pPT1
        wtr = [("SE", exp_tok[i])]
        if i >= 2:
            wtr.append(("VE", pt_tok[i - 2]))
        for j in range(4):
            S.do("tensor",
                 (lambda e, j=j, pPT=pPT, par=par:
                  e.transpose(A(pPT)[:, j * 128:(j + 1) * 128],
                              A(b_p)[:, par * TILE + j * 128: par * TILE + (j + 1) * 128], A(s_idf))),
                 wait=(wtr if j == 0 else ()), inc=("TE" if j == 3 else None))
        t_tr_tok[i] = S.val("TE")
        S.do("vector", lambda e, pPT=pPT: e.tensor_copy(A(b_pthi), A(pPT)),
             wait=[("TE", t_tr_tok[i])])
        pt_tok[i] = S.do("vector", lambda e, pPT=pPT: e.tensor_tensor(A(b_ptlo), A(pPT), A(b_pthi),
                                                                      ALU.subtract), inc="VE")
        if kt == 0:
            S.do("vector", (lambda e, k=k, p4=p4: e.tensor_copy(A(l_col)[:, k:k + 1],
                                                                A(c_e)[:, p4:p4 + 1])),
                 wait=[("SE", exp_tok[i])], inc="VE")
        else:
            S.do("vector",
                 (lambda e, k=k, p4=p4: e.scalar_tensor_tensor(A(l_col)[:, k:k + 1],
                                                               A(l_col)[:, k:k + 1],
                                                               A(c_d)[:, p4:p4 + 1],
                                                               A(c_e)[:, p4:p4 + 1],
                                                               ALU.mult, ALU.add)),
                 wait=[("SE", exp_tok[i])], inc="VE")

    def back_post(i):
        kt, k = iters[i]
        par = i % 2
        p4 = i % 4
        buf = kt % 3
        pC = pC0 if par == 0 else pC1
        wpv = [("VE", pt_tok[i])]
        if i >= 2:
            wpv.append(("VE", ctx_ev_tok[i - 2]))
        for dh in range(2):
            pairs = [(A(b_pthi)[:, kc * 128:(kc + 1) * 128],
                      A(b_ptlo)[:, kc * 128:(kc + 1) * 128],
                      A(b_vhi)[:, buf * 4 * D + kc * D + dh * 512: buf * 4 * D + kc * D + dh * 512 + 512],
                      A(b_vlo)[:, buf * 4 * D + kc * D + dh * 512: buf * 4 * D + kc * D + dh * 512 + 512])
                     for kc in range(4)]
            mm3(A(pC)[:, dh * 512:(dh + 1) * 512], pairs,
                wait_first=(wpv if dh == 0 else ()), mark=(dh == 1))
        t_pv_tok[i] = S.val("TE")
        if kt == 0:
            ctx_ev_tok[i] = S.do("vector",
                                 (lambda e, k=k, pC=pC: e.tensor_copy(A(b_ctx)[:, k * D:(k + 1) * D],
                                                                      A(pC))),
                                 wait=[("TE", t_pv_tok[i])], inc="VE")
        else:
            ctx_ev_tok[i] = S.do("vector",
                                 (lambda e, k=k, pC=pC, p4=p4:
                                  e.scalar_tensor_tensor(A(b_ctx)[:, k * D:(k + 1) * D],
                                                         A(b_ctx)[:, k * D:(k + 1) * D],
                                                         A(c_d)[:, p4:p4 + 1], A(pC),
                                                         ALU.mult, ALU.add)),
                                 wait=[("TE", t_pv_tok[i])], inc="VE")
        if kt == k:
            ctx_store[k] = S.dma("gpsimd", ctxs[k], A(b_ctx)[:, k * D:(k + 1) * D],
                                 wait=[("VE", ctx_ev_tok[i])], inc="sc")
            if debug:
                S.dma("gpsimd", dbg["d_ctx"][k], A(b_ctx)[:, k * D:(k + 1) * D], inc="sc")
        if i == kt_last_iter[kt] and kt + 3 < nt:
            load_kv(kt + 3, extra_wait=[("TE", t_pv_tok[i])])

    front(0)
    if nt > 1:
        load_kv(1)
    if nt > 2:
        load_kv(2)
    if n_it > 1:
        front(1)
    for j in range(n_it):
        back_pre(j)
        if j + 2 < n_it:
            front(j + 2)
        back_post(j)

    if debug:
        S.dma("gpsimd", dbg["d_l"], A(l_col)[:, 0:nt], wait=[("VE", S.val("VE"))], inc="sc")

    B_TE = S.val("TE"); B_VE = S.val("VE"); B_SE = S.val("SE")

    cwait = [("TE", B_TE), ("VE", B_VE), ("SE", B_SE), ("sc", S.val("sc"))]
    w_wo = [("sc", ctx_store[7])] if nt >= 8 else list(cwait)
    t_wo2 = S.dma("sync", A(wo_hi), wo_scr[0], wait=w_wo, inc="lw")
    t_wo3 = S.dma("sync", A(wo_lo), wo_scr[1], inc="lw")
    for _rc in range(nt):
        S.do("vector", (lambda e, c=_rc: e.reciprocal(A(li_col)[:, c:c + 1], A(l_col)[:, c:c + 1])),
             wait=(list(cwait) if _rc == 0 else ()), drain=(_rc == nt - 1))
    v_li = S.do("vector", lambda e: e.tensor_copy(A(c_a)[:, 0:1], A(li_col)[:, 0:1]), inc="VE")

    def pairs_o(do):
        return [(A(wo_hi)[:, dv * 1024 + do * 128: dv * 1024 + (do + 1) * 128],
                 A(wo_lo)[:, dv * 1024 + do * 128: dv * 1024 + (do + 1) * 128],
                 A(c_fhi)[:, dv * TILE:(dv + 1) * TILE],
                 A(c_flo)[:, dv * TILE:(dv + 1) * TILE]) for dv in range(8)]

    out_store = None
    ng = nt // 4
    C_so = {}
    C_x2ve = {}
    C_sy = {}
    C_ytrve = {}
    C2_state = {}

    def emit_C1(gi, part):
        par = gi % 2
        if part == 0:
            w0 = [("VE", v_li)] if gi == 0 else []
            if gi >= 2 and (gi - 2) in C_so:
                w0.append(("so", C_so[gi - 2]))
            tok_c = None
            for cc in range(4):
                k = gi * 4 + cc
                tok_c = S.dma("sync", A(c_ctx)[:, par * 4 * D + cc * D: par * 4 * D + (cc + 1) * D],
                              ctxs[k], wait=(w0 if cc == 0 else ()), inc="lg")
            for cc in range(4):
                k = gi * 4 + cc
                tok_c = S.dma("sync", A(c_x1g)[:, par * 4 * D + cc * D: par * 4 * D + (cc + 1) * D],
                              x1s[k], inc="lg")
            C2_state[("lg", gi)] = tok_c
        cc_list = ((0,), (1,), (2, 3))[part] if part < 3 else ()
        for cc in cc_list:
            k = gi * 4 + cc
            wv0 = [("lg", C2_state[("lg", gi)])]
            if cc == 0 and (gi - 1) in C_ytrve:
                wv0.append(("VE", C_ytrve[gi - 1]))
            v_n = S.do("vector",
                       (lambda e, cc=cc, k=k, par=par:
                        e.tensor_scalar(A(c_ctxn), A(c_ctx)[:, par * 4 * D + cc * D: par * 4 * D + (cc + 1) * D],
                                        A(li_col)[:, k:k + 1], None, ALU.mult)),
                       wait=wv0, inc="VE")
            for gg in range(2):
                bank = pCT0 if gg == 0 else pCT1
                for j in range(4):
                    dv = gg * 4 + j
                    S.do("tensor",
                         (lambda e, dv=dv, j=j, bank=bank:
                          e.transpose(A(bank)[:, j * 128:(j + 1) * 128],
                                      A(c_ctxn)[:, dv * 128:(dv + 1) * 128], A(s_idf))),
                         wait=([("VE", v_n), ("VE", S.val("VE"))] if j == 0 else ()),
                         inc=("TE" if j == 3 else None))
                tm2 = S.val("TE")
                dst_h = A(c_fhi).rearrange("p (dv t) -> p dv t", dv=NCH)[
                    :, gg * 4:(gg + 1) * 4, cc * 128:(cc + 1) * 128]
                dst_l = A(c_flo).rearrange("p (dv t) -> p dv t", dv=NCH)[
                    :, gg * 4:(gg + 1) * 4, cc * 128:(cc + 1) * 128]
                S.do("vector", (lambda e, d=dst_h, bank=bank: e.tensor_copy(d, A(bank))),
                     wait=[("TE", tm2)])
                S.do("vector", (lambda e, d=dst_l, h=dst_h, bank=bank:
                                e.tensor_tensor(d, A(bank), h, ALU.subtract)), inc="VE")
        if part == 3:
            par = gi % 2
            v_fsplit = S.val("VE")
            o_se = {}
            for do in range(8):
                po = pO0 if do % 2 == 0 else pO1
                wfo = [("VE", v_fsplit)] if do == 0 else []
                if do == 0 and gi == 0:
                    wfo.append(("lw", t_wo3))
                if do == 0 and (gi - 1) in C_sy:
                    wfo.append(("SE", C_sy[gi - 1]))
                if do >= 2:
                    wfo.append(("SE", o_se[do - 2]))
                mm3(A(po), pairs_o(do), wait_first=wfo, mark=True)
                tmk = S.val("TE")
                S.do("scalar",
                     (lambda e, do=do, po=po: e.activation(A(c_y)[:, do * TILE:(do + 1) * TILE], A(po),
                                                           AF.Identity, bias=A(bopc)[:, do:do + 1],
                                                           scale=1.0)),
                     wait=[("TE", tmk)], inc="SE")
                o_se[do] = S.val("SE")
            C_sy[gi] = S.val("SE")
            s_y = C_sy[gi]
            for cc in range(4):
                for gg in range(2):
                    bank = pCT0 if gg == 0 else pCT1
                    for j in range(4):
                        do = gg * 4 + j
                        S.do("tensor",
                             (lambda e, do=do, j=j, cc=cc, bank=bank:
                              e.transpose(A(bank)[:, j * 128:(j + 1) * 128],
                                          A(c_y)[:, do * TILE + cc * 128: do * TILE + (cc + 1) * 128],
                                          A(s_idf))),
                             wait=([("SE", s_y), ("VE", S.val("VE"))] if (j == 0) else ()),
                             inc=("TE" if j == 3 else None))
                    tm2 = S.val("TE")
                    wadd = [("TE", tm2)]
                    if gg == 0 and cc == 0 and (gi - 2) in C_so:
                        wadd.append(("so", C_so[gi - 2]))
                    x2v = A(c_x2)[:, par * 4 * D + cc * D + gg * 512:
                                  par * 4 * D + cc * D + (gg + 1) * 512]
                    x1v = A(c_x1g)[:, par * 4 * D + cc * D + gg * 512:
                                   par * 4 * D + cc * D + (gg + 1) * 512]
                    S.do("vector",
                         (lambda e, x2v=x2v, x1v=x1v, bank=bank:
                          e.tensor_tensor(x2v, A(bank), x1v, ALU.add)),
                         wait=wadd, inc="VE")
                C_x2ve[(gi, cc)] = S.val("VE")
            C_x2ve[gi] = S.val("VE")
            C_ytrve[gi] = S.val("VE")

    def emit_C2f(gi, cc):
        par = gi % 2
        sp = cc % 2
        x2ap = A(c_x2)[:, par * 4 * D + cc * D: par * 4 * D + (cc + 1) * D]
        wx = [("VE", C_x2ve[(gi, cc)])]
        if (gi, cc, "ffn_te") in C2_state:
            pass
        prev = (gi, cc - 2) if cc >= 2 else ((gi - 1, cc + 2) if gi > 0 else None)
        if prev is not None and ("b_te", prev[0], prev[1]) in C2_state:
            wx.append(("TE", C2_state[("b_te", prev[0], prev[1])]))
        z3t = emit_ln(x2ap, A(c_z), wait_x_v=wx, sq=A(c_yq))
        vtok = emit_quant(A(c_z), A(c_qtm)[:, sp * D:(sp + 1) * D],
                          [("SE", z3t)], A(c_e)[:, 2 + sp:3 + sp],
                          ybuf=A(c_yq), ibuf=A(c_ii), dbuf=A(c_z))
        C2_state[(gi, cc)] = vtok

    def emit_C2b(gi, cc):
        nonlocal out_store
        par = gi % 2
        sp = cc % 2
        k = gi * 4 + cc
        x2ap = A(c_x2)[:, par * 4 * D + cc * D: par * 4 * D + (cc + 1) * D]
        vtok = C2_state.pop((gi, cc))
        wtr = [("VE", vtok), ("VE", S.val("VE"))]
        for j in range(8):
            S.do("tensor",
                 (lambda e, j=j, sp=sp: e.transpose(A(pCb)[:, sp * 1024 + j * 128: sp * 1024 + (j + 1) * 128],
                                                    A(c_qtm)[:, sp * D + j * 128: sp * D + (j + 1) * 128],
                                                    A(s_idb))),
                 wait=(wtr if j == 0 else ()), inc=("TE" if j == 7 else None))
        tm = S.val("TE")
        qfm_tok = S.do("vector", (lambda e, sp=sp: e.tensor_copy(A(c_qfm),
                                                                 A(pCb)[:, sp * 1024:(sp + 1) * 1024])),
                       wait=[("TE", tm)], inc="VE")
        wffn = [("VE", qfm_tok)]
        if ("epi_ve",) in C2_state:
            wffn.append(("VE", C2_state[("epi_ve",)]))
        for dh in range(2):
            for di in range(8):
                first = di == 0
                last = di == 7
                S.do("tensor",
                     (lambda e, di=di, dh=dh, first=first, last=last:
                      e.matmul(A(pCF0)[:, dh * 512:(dh + 1) * 512],
                               A(c_qfm)[:, di * 128:(di + 1) * 128],
                               A(q_rhs)[:, di * 1024 + dh * 512: di * 1024 + dh * 512 + 512],
                               start=first, stop=last)),
                     wait=(wffn if (first and dh == 0) else ()),
                     inc=("TE" if (last and dh == 1) else None))
        tf = S.val("TE")
        C2_state[("b_te", gi, cc)] = tf
        S.do("vector", (lambda e, sp=sp: e.tensor_scalar(A(c_f)[:, 1:2], A(c_e)[:, 2 + sp:3 + sp],
                                                         0.5 / 127.0, None, ALU.mult)), drain=True)
        for dh in range(2):
            S.do("vector",
                 (lambda e, dh=dh:
                  e.scalar_tensor_tensor(A(c_yq)[:, dh * 512:(dh + 1) * 512],
                                         A(pCF0)[:, dh * 512:(dh + 1) * 512],
                                         A(c_f)[:, 1:2], A(rs_bc)[:, dh * 512:(dh + 1) * 512],
                                         ALU.mult, ALU.mult)),
                 wait=([("TE", tf)] if dh == 0 else ()))
        v_o = S.do("vector",
                   (lambda e, x2ap=x2ap: e.tensor_tensor(x2ap, x2ap, A(c_yq), ALU.add)),
                   wait=[("SE", S.val("SE"))], inc="VE")
        C2_state[("epi_ve",)] = v_o
        out_store = S.dma("gpsimd", xout[k * 128:(k + 1) * 128, :], x2ap,
                          wait=[("VE", v_o)], inc="so")
        C_so[gi] = out_store

    for part in range(4):
        emit_C1(0, part)
    emit_C2f(0, 0)
    emit_C2f(0, 1)
    for gi in range(ng):
        for cc in range(4):
            emit_C2b(gi, cc)
            if gi + 1 < ng:
                emit_C1(gi + 1, cc)
            if cc < 2:
                emit_C2f(gi, cc + 2)
            elif gi + 1 < ng and cc == 3:
                pass
        if gi + 1 < ng:
            emit_C2f(gi + 1, 0)
            emit_C2f(gi + 1, 1)

    final_w = [(s, S.val(s)) for s in ("so", "sc", "skv0", "skv1", "sq0", "sq1", "sx1", "sb", "sdbg") if S.val(s) > 0]
    S.do("gpsimd", lambda e: None, wait=final_w)

    import contextlib
    with contextlib.ExitStack() as stack:
        sems = {}
        for name in sorted(S.cnt.keys()):
            sems[name] = stack.enter_context(nc.semaphore(name))
        if guard:
            for name in ("ovector", "oscalar"):
                sems[name] = stack.enter_context(nc.semaphore(name))
        block = stack.enter_context(nc.Block())

        @block.sync
        def _(eng):
            S.replay("sync", eng, sems, guard=guard)

        @block.tensor
        def _(eng):
            S.replay("tensor", eng, sems, guard=guard)

        @block.vector
        def _(eng):
            S.replay("vector", eng, sems, guard=guard)

        @block.scalar
        def _(eng):
            S.replay("scalar", eng, sems, guard=guard)

        @block.gpsimd
        def _(eng):
            S.replay("gpsimd", eng, sems, guard=guard)

    return nc


def _host_inputs(x_full, Wq, bq, Wk, bk, Wv, bv, Wo, bo, Frhs, nt):
    import ml_dtypes
    T = nt * TILE
    ident = np.eye(128, dtype=np.float32)
    ones = np.ones((1, 128), np.float32)
    in_maps = []
    for c in range(8):
        s, r = c // 4, c % 4
        rows = (np.arange(nt)[:, None] * TILE + r * 128 + np.arange(128)[None, :]).ravel()
        xs = np.ascontiguousarray(x_full[s][rows])
        m = np.full((QCH, TILE), NEG, np.float32)
        i_idx = np.arange(QCH)[:, None]
        j_idx = np.arange(QCH)[None, :]
        for j in range(4):
            if j < r:
                m[:, j * 128:(j + 1) * 128] = 0.0
            elif j == r:
                m[:, j * 128:(j + 1) * 128] = np.where(j_idx <= i_idx, 0.0, NEG)
        in_maps.append(dict(
            xin=xs, wq=Wq, wk=Wk, wv=Wv, wo=Wo, frhs=Frhs,
            bq=bq, bk=bk, bv=bv, bo=bo,
            maskin=m, idf=ident, idb=ident.astype(ml_dtypes.bfloat16), ones_col=ones,
        ))
    return in_maps


def _host_gather(results, nt):
    T = nt * TILE
    out = np.empty((2, T, D), np.float32)
    for c, r_ in enumerate(results):
        s, r = c // 4, c % 4
        own = np.asarray(r_["xout"])
        rows = (np.arange(nt)[:, None] * TILE + r * 128 + np.arange(128)[None, :]).ravel()
        out[s][rows] = own
    return out


def kernel(L0_attn_y_all, L0_attn_Wq, L0_attn_bq, L0_attn_Wk, L0_attn_bk,
           L0_attn_Wv, L0_attn_bv, L0_attn_Wo, L0_attn_bo, L0_ffn_rhs):
    nt = 16
    x = np.ascontiguousarray(np.asarray(L0_attn_y_all, dtype=np.float32))
    nc = build_core_program(nt)
    in_maps = _host_inputs(x, np.asarray(L0_attn_Wq, np.float32), np.asarray(L0_attn_bq, np.float32),
                           np.asarray(L0_attn_Wk, np.float32), np.asarray(L0_attn_bk, np.float32),
                           np.asarray(L0_attn_Wv, np.float32), np.asarray(L0_attn_bv, np.float32),
                           np.asarray(L0_attn_Wo, np.float32), np.asarray(L0_attn_bo, np.float32),
                           np.asarray(L0_ffn_rhs, np.float32), nt)
    res = run_bass_kernel_spmd(nc, in_maps, core_ids=list(range(8)))
    return _host_gather(res.results, nt)


# version 20
# speedup vs baseline: 1.0479x; 1.0188x over previous
import numpy as np
import concourse.bass as bass
import concourse.mybir as mybir
from concourse.bass_utils import run_bass_kernel_spmd

F32 = mybir.dt.float32
BF16 = mybir.dt.bfloat16
I32 = mybir.dt.int32
AF = mybir.ActivationFunctionType
ALU = mybir.AluOpType
AX = mybir.AxisListType

D = 1024
NCH = 8
TILE = 512
NROLE = 4
QCH = 128
NEG = -1.0e38

ENGINES = ("sync", "tensor", "vector", "scalar", "gpsimd")


class Sched:

    def __init__(self):
        self.ops = {e: [] for e in ENGINES}
        self.cnt = {}

    def do(self, eng, fn, wait=(), inc=None, amt=1, drain=False):
        tok = None
        if inc is not None:
            self.cnt[inc] = self.cnt.get(inc, 0) + amt
            tok = self.cnt[inc]
        self.ops[eng].append((tuple(wait), fn, inc, amt, drain))
        return tok

    def dma(self, eng, out, in_, wait=(), inc="ld", slow=False):
        if slow:
            return self.do(eng, (lambda e, o=out, i=in_:
                                 e.dma_start(out=o, in_=i, allow_slow_non_contiguous=True)),
                           wait=wait, inc=inc, amt=16)
        return self.do(eng, (lambda e, o=out, i=in_: e.dma_start(out=o, in_=i)),
                       wait=wait, inc=inc, amt=16)

    def val(self, sem):
        return self.cnt.get(sem, 0)

    def replay(self, eng_name, eng, sems, guard=False):
        chain = guard and eng_name in ("vector", "scalar")
        osem = "o" + eng_name if chain else None
        own = "VE" if eng_name == "vector" else ("SE" if eng_name == "scalar" else None)
        n = 0
        nown = 0
        for wait, fn, inc, amt, drain in self.ops[eng_name]:
            for (s, v) in wait:
                if v > 0:
                    eng.wait_ge(sems[s], v)
            if chain:
                if n > 0:
                    eng.wait_ge(sems[osem], n)
                if nown > 0:
                    eng.wait_ge(sems[own], nown)
            instr = fn(eng)
            if drain:
                instr = eng.drain()
            if inc is not None:
                assert instr is not None, f"inc {inc} on op with no instruction"
                instr.then_inc(sems[inc], amt)
                if inc == own:
                    nown += amt
            if chain and instr is not None and inc is None:
                instr.then_inc(sems[osem], 1)
                n += 1


def build_core_program(nt, debug=False, guard=False):
    nc = bass.Bass("TRN2", target_bir_lowering=False,
                   detect_race_conditions=guard)
    T = nt * TILE
    NOWN = nt * QCH

    def din(name, shape, dt=F32):
        return nc.dram_tensor(name, shape, dt, kind="ExternalInput").ap()

    def dout_(name, shape, dt=F32):
        return nc.dram_tensor(name, shape, dt, kind="ExternalOutput").ap()

    def dtmp(name, shape, dt=F32):
        return nc.dram_tensor(name, shape, dt).ap()

    xin = din("xin", [NOWN, D])
    wq_d = din("wq", [D, D]); wk_d = din("wk", [D, D]); wv_d = din("wv", [D, D])
    wo_d = din("wo", [D, D]); frhs = din("frhs", [D, D])
    bq_d = din("bq", [D]); bk_d = din("bk", [D]); bv_d = din("bv", [D]); bo_d = din("bo", [D])
    maskin = din("maskin", [QCH, TILE])
    idf = din("idf", [128, 128])
    idb = din("idb", [128, 128], BF16)
    ones_in = din("ones_col", [1, 128])

    xout = dout_("xout", [NOWN, D])

    NS = nt // 4
    contrib = dtmp("contrib", [NS, 512, NCH * TILE], BF16)
    gath = dtmp("gath", [NS, 4 * 512, NCH * TILE], BF16)
    qx_hi = dtmp("qx_hi", [NS, 128, NCH * TILE], BF16)
    qx_lo = dtmp("qx_lo", [NS, 128, NCH * TILE], BF16)
    x1s = dtmp("x1s", [nt, 128, D])
    ctxs = dtmp("ctxs", [nt, 128, D])
    rs_bounce = dtmp("rs_bounce", [8, 128])
    wo_scr = dtmp("wo_scr", [2, 128, NCH * D], BF16)
    bo_bounce = dtmp("bo_bounce", [D])

    dbg = {}
    if debug:
        dbg["d_x1"] = dout_("d_x1", [nt, 128, D])
        dbg["d_k"] = dout_("d_k", [nt, 128, NCH * TILE], BF16)
        dbg["d_v"] = dout_("d_v", [nt, 128, 4 * D], BF16)
        dbg["d_q"] = dout_("d_q", [nt, 128, NCH * QCH], BF16)
        dbg["d_ctx"] = dout_("d_ctx", [nt, 128, D])
        dbg["d_l"] = dout_("d_l", [128, nt])
        dbg["d_rsbc"] = dout_("d_rsbc", [128, D])
        dbg["d_qrhs"] = dout_("d_qrhs", [128, NCH * D], BF16)
        dbg["d_qtm"] = dout_("d_qtm", [128, D], BF16)
        dbg["d_y"] = dout_("d_y", [128, D])
        dbg["d_wt2"] = dout_("d_wt2", [128, NCH * D])
        dbg["d_qrT"] = dout_("d_qrT", [128, NCH * D], BF16)
        dbg["d_cg"] = dout_("d_cg", [128, 8])
        dbg["d_ch"] = dout_("d_ch", [128, 8])

    def sb(name, shape, dt=F32):
        return nc.alloc_sbuf_tensor(name, shape, dt)

    q_rhs = sb("q_rhs", [128, NCH * D], BF16)
    rs_bc = sb("rs_bc", [128, D])
    mask = sb("mask", [QCH, TILE])
    s_idf = sb("s_idf", [128, 128])
    s_idb = sb("s_idb", [128, 128], BF16)
    s_ones = sb("s_ones", [1, 128])
    bq32c = sb("bq32c", [128, 8]); bkc = sb("bkc", [128, 8])
    bvc = sb("bvc", [128, 8]); boc = sb("boc", [128, 8]); bopc = sb("bopc", [128, 8])
    epsc = sb("epsc", [128, 1])
    l_col = sb("l_col", [128, max(nt, 2)])
    nm_col = sb("nm_col", [128, max(nt, 2)])
    li_col = sb("li_col", [128, max(nt, 2)])
    c_a = sb("c_a", [128, 2]); c_b = sb("c_b", [128, 2])
    c_c = sb("c_c", [128, 4]); c_d = sb("c_d", [128, 4])
    c_e = sb("c_e", [128, 4]); c_f = sb("c_f", [128, 4])
    c_g = sb("c_g", [128, 8]); c_h = sb("c_h", [128, 8])

    base_R = nc.sbuf_base

    wq_hi = sb("wq_hi", [128, NCH * D], BF16); wq_lo = sb("wq_lo", [128, NCH * D], BF16)
    wk_hi = sb("wk_hi", [128, NCH * D], BF16); wk_lo = sb("wk_lo", [128, NCH * D], BF16)
    wv_hi = sb("wv_hi", [128, NCH * D], BF16); wv_lo = sb("wv_lo", [128, NCH * D], BF16)
    base_A2 = nc.sbuf_base
    wtmp = sb("wtmp", [128, NCH * D])
    wtmp2 = sb("wtmp2", [128, NCH * D])
    qrT = sb("qrT", [128, NCH * D], BF16)
    nc.sbuf_base = base_A2
    a_x = sb("a_x", [128, 2 * D])
    a_y = sb("a_y", [128, D])
    a_i = sb("a_i", [128, D], I32)
    a_qtm = sb("a_qtm", [128, D], BF16)
    a_qfm = sb("a_qfm", [128, D], BF16)
    a_qf = sb("a_qf", [128, TILE])
    a_x1 = sb("a_x1", [128, D])
    z2buf = sb("z2buf", [128, 2 * 4 * D])
    a_z2 = sb("a_z2", [128, D])
    a_nhi = sb("a_nhi", [128, NCH * TILE], BF16)
    a_nlo = sb("a_nlo", [128, NCH * TILE], BF16)
    a_kvhi = sb("a_kvhi", [128, 2 * TILE], BF16)
    a_kvlo = sb("a_kvlo", [128, 2 * TILE], BF16)
    endA = nc.sbuf_base
    assert endA - base_R <= 186 * 1024, f"A over: {(endA - base_R)/1024}KB"

    nc.sbuf_base = base_R
    b_ctx = sb("b_ctx", [128, nt * D])
    b_khi = sb("b_khi", [128, 3 * NCH * TILE], BF16)
    b_klo = sb("b_klo", [128, 3 * NCH * TILE], BF16)
    b_vhi = sb("b_vhi", [128, 3 * 4 * D], BF16)
    b_vlo = sb("b_vlo", [128, 3 * 4 * D], BF16)
    b_qhi = sb("b_qhi", [128, 2 * NCH * QCH], BF16)
    b_qlo = sb("b_qlo", [128, 2 * NCH * QCH], BF16)
    b_p = sb("b_p", [128, 2 * TILE])
    b_pthi = sb("b_pthi", [128, 4 * QCH], BF16)
    b_ptlo = sb("b_ptlo", [128, 4 * QCH], BF16)
    endB = nc.sbuf_base
    assert endB - base_R <= 186 * 1024, f"B over: {(endB - base_R)/1024}KB"

    nc.sbuf_base = base_R
    wo_hi = sb("wo_hi", [128, NCH * D], BF16); wo_lo = sb("wo_lo", [128, NCH * D], BF16)
    c_ctx = sb("c_ctx", [128, 2 * 4 * D])
    c_ctxn = sb("c_ctxn", [128, D])
    c_fhi = sb("c_fhi", [128, NCH * TILE], BF16)
    c_flo = sb("c_flo", [128, NCH * TILE], BF16)
    c_y = sb("c_y", [128, NCH * TILE])
    c_x1g = sb("c_x1g", [128, 2 * 4 * D])
    c_x2 = sb("c_x2", [128, 2 * 4 * D])
    c_z = sb("c_z", [128, D]); c_yq = sb("c_yq", [128, D])
    c_ii = sb("c_ii", [128, D], I32)
    c_qtm = sb("c_qtm", [128, 2 * D], BF16); c_qfm = sb("c_qfm", [128, D], BF16)
    endC = nc.sbuf_base
    assert endC - base_R <= 186 * 1024, f"C over: {(endC - base_R)/1024}KB"

    def psn(name, shape, dt=F32):
        return nc.alloc_psum_tensor(name, shape, dt)

    ps0 = nc.psum_base
    pF0 = psn("pF0", [128, 512]); pF1 = psn("pF1", [128, 512])
    pTb = psn("pTb", [128, 1024], BF16)
    pTf0 = psn("pTf0", [128, 512]); pTf1 = psn("pTf1", [128, 512])
    pA0 = psn("pA0", [128, 512]); pA1 = psn("pA1", [128, 512])
    pX = psn("pX", [128, 512])
    nc.psum_base = ps0
    pS0 = psn("pS0", [128, 512]); pS1 = psn("pS1", [128, 512])
    pPT0 = psn("pPT0", [128, 512]); pPT1 = psn("pPT1", [128, 512])
    pC0 = psn("pC0", [128, 1024]); pC1 = psn("pC1", [128, 1024])
    nc.psum_base = ps0
    pO0 = psn("pO0", [128, 512]); pO1 = psn("pO1", [128, 512])
    pCT0 = psn("pCT0", [128, 512]); pCT1 = psn("pCT1", [128, 512])
    pCF0 = psn("pCF0", [128, 1024]); pCb = psn("pCb", [128, 2048], BF16)

    S = Sched()
    A = lambda h: h.ap()

    def mm3(psum_ap, pairs, wait_first=(), mark=False):
        n = len(pairs)
        tok = None
        for i, (lh, ll, rh, rl) in enumerate(pairs):
            for j, (a_, b_) in enumerate(((lh, rh), (lh, rl), (ll, rh))):
                first = (i == 0 and j == 0)
                last = (i == n - 1 and j == 2)
                tok = S.do("tensor",
                           (lambda e, a_=a_, b_=b_, first=first, last=last, p=psum_ap:
                            e.matmul(p, a_, b_, start=first, stop=last)),
                           wait=(wait_first if first else ()),
                           inc=("TE" if (last and mark) else None))
        return tok

    def emit_ln(x_ap, z_ap, wait_x_v=(), wait_x_s=(), sq=None):
        S.do("vector", lambda e, x=x_ap: e.tensor_reduce(A(c_a)[:, 0:1], x, axis=AX.X, op=ALU.add),
             wait=wait_x_v, drain=True)
        S.do("vector", lambda e: e.tensor_scalar(A(c_a)[:, 0:1], A(c_a)[:, 0:1], -1.0 / D, None, ALU.mult),
             inc="VE", drain=True)
        vm = S.val("VE")
        S.do("scalar", lambda e, x=x_ap, q=sq: e.activation(q, x, AF.Square, bias=A(c_a)[:, 0:1],
                                                      scale=1.0, accum_out=A(c_b)[:, 0:1]),
             wait=tuple(wait_x_s) + (("VE", vm),), inc="SE", drain=True)
        S.do("scalar", lambda e: e.activation(A(c_c)[:, 0:1], A(c_b)[:, 0:1], AF.Ln,
                                              bias=A(epsc), scale=1.0 / D), drain=True)
        S.do("scalar", lambda e: e.activation(A(c_c)[:, 0:1], A(c_c)[:, 0:1], AF.Exp,
                                              bias=0.0, scale=-0.5), inc="SE")
        sm = S.val("SE")
        S.do("vector", lambda e: e.tensor_tensor(A(c_d)[:, 0:1], A(c_a)[:, 0:1], A(c_c)[:, 0:1], ALU.mult),
             wait=[("SE", sm)], inc="VE", drain=True)
        vm2 = S.val("VE")
        tok = S.do("scalar", lambda e, x=x_ap, z=z_ap: e.activation(z, x, AF.Identity,
                                                                    bias=A(c_d)[:, 0:1], scale=A(c_c)[:, 0:1]),
                   wait=[("VE", vm2)], inc="SE")
        return tok

    def emit_quant(z_ap, qtm_ap, wait_z_v, scale_col, ybuf=None, ibuf=None, dbuf=None):
        S.do("vector", lambda e, z=z_ap: e.reduce_max(A(c_e)[:, 0:1], z, axis=AX.X,
                                                      apply_absolute_value=True),
             wait=wait_z_v, drain=True)
        S.do("vector", lambda e: e.tensor_scalar(scale_col, A(c_e)[:, 0:1], 1e-12, None, ALU.max),
             drain=True)
        S.do("vector", lambda e: e.reciprocal(A(c_f)[:, 0:1], scale_col), drain=True)
        S.do("vector", lambda e: e.tensor_scalar(A(c_f)[:, 0:1], A(c_f)[:, 0:1], 127.0, None, ALU.mult),
             inc="VE", drain=True)
        vm = S.val("VE")
        S.do("scalar", lambda e, z=z_ap, y=ybuf: e.activation(y, z, AF.Identity, bias=0.0,
                                                      scale=A(c_f)[:, 0:1]),
             wait=[("VE", vm)], inc="SE")
        sm = S.val("SE")
        S.do("vector", lambda e, y=ybuf, ib=ibuf: e.tensor_copy(ib, y), wait=[("SE", sm)])
        S.do("vector", lambda e, y=ybuf, ib=ibuf, db=dbuf: e.tensor_tensor(db, y, ib, ALU.subtract))
        S.do("vector", lambda e, y=ybuf, db=dbuf: e.tensor_scalar(y, db, -0.5, None, ALU.is_le))
        S.do("vector", lambda e, y=ybuf, db=dbuf: e.scalar_tensor_tensor(db, db, 0.5, y,
                                                                         ALU.is_ge, ALU.subtract))
        tok = S.do("vector", lambda e, q=qtm_ap, ib=ibuf, db=dbuf: e.tensor_tensor(q, db, ib, ALU.add),
                   inc="VE")
        return tok

    S.dma("sync", A(mask), maskin, inc="lc")
    S.dma("sync", A(s_idf), idf, inc="lc")
    S.dma("sync", A(s_idb), idb, inc="lc")
    S.dma("sync", A(s_ones), ones_in, inc="lc")
    for (dst, src) in ((bq32c, bq_d), (bkc, bk_d), (bvc, bv_d), (boc, bo_d)):
        S.dma("sync", A(dst), src.rearrange("(c p) -> p c", p=128), slow=True, inc="lc")
    t_consts = S.val("lc")
    S.do("vector", lambda e: e.memset(A(epsc), 1e-5))
    S.do("vector", lambda e: e.tensor_scalar(A(bq32c), A(bq32c), 1.0 / 32.0, None, ALU.mult),
         wait=[("lc", t_consts)], inc="VE")

    def split_w(w_dram, hi_sb, lo_sb, prev_v, q="sync", buf=None, sem="lw"):
        b = buf if buf is not None else wtmp
        t = S.dma(q, A(b).rearrange("p (c d) -> p c d", c=8),
                  w_dram.rearrange("(c p) d -> p c d", p=128),
                  wait=([("VE", prev_v)] if prev_v else ()), inc=sem)
        S.do("vector", lambda e, h=hi_sb, b=b: e.tensor_copy(A(h), A(b)), wait=[(sem, t)])
        tl = S.do("vector", lambda e, h=hi_sb, l=lo_sb, b=b: e.tensor_tensor(A(l), A(b), A(h), ALU.subtract),
                  inc="VE")
        return tl

    v1 = split_w(wq_d, wq_hi, wq_lo, None)
    v2 = split_w(wk_d, wk_hi, wk_lo, None, q="gpsimd", buf=wtmp2, sem="lw2")
    v3 = split_w(wv_d, wv_hi, wv_lo, v2, q="gpsimd", buf=wtmp2, sem="lw2")

    t_rhs = S.dma("sync", A(wtmp).rearrange("p (c d) -> p c d", c=8),
                  frhs.rearrange("(c p) d -> p c d", p=128),
                  wait=[("VE", v1)], inc="lw")

    def transpose_pack_f32(src_slices, dst_slices, bank, wait_first, evac_eng="vector"):
        for j in range(4):
            w = list(wait_first) if j == 0 else []
            S.do("tensor",
                 (lambda e, j=j, s=src_slices[j], bank=bank:
                  e.transpose(A(bank)[:, j * 128:(j + 1) * 128], s, A(s_idf))),
                 wait=w, inc=("TE" if j == 3 else None))
        tm = S.val("TE")
        tok = None
        for j in range(4):
            tok = S.do(evac_eng,
                       (lambda e, j=j, d=dst_slices[j], bank=bank:
                        e.tensor_copy(d, A(bank)[:, j * 128:(j + 1) * 128])),
                       wait=([("TE", tm)] if j == 0 else ()),
                       inc=("VE" if evac_eng == "vector" else "SE"))
        return tok

    last_ev = None
    for g in range(16):
        bank = pTf0 if g % 2 == 0 else pTf1
        wf = [("lw", t_rhs)] if g > 0 else [("lw", t_rhs), ("VE", v3)]
        if last_ev is not None and g >= 2:
            wf.append(("VE", S.val("VE")))
        srcs, dsts = [], []
        for j in range(4):
            idx = g * 4 + j
            di, do = idx // 8, idx % 8
            srcs.append(A(wtmp)[:, di * 1024 + do * 128: di * 1024 + (do + 1) * 128])
            dsts.append(A(wtmp2)[:, do * 1024 + di * 128: do * 1024 + (di + 1) * 128])
        last_ev = transpose_pack_f32(srcs, dsts, bank, wf)

    if debug:
        S.dma("gpsimd", dbg["d_wt2"], A(wtmp2), wait=[("VE", last_ev)], inc="sdbg")
    for cdo in range(8):
        S.do("vector",
             (lambda e, cdo=cdo: e.reduce_max(A(c_g)[:, cdo:cdo + 1],
                                              A(wtmp2)[:, cdo * 1024:(cdo + 1) * 1024],
                                              axis=AX.X, apply_absolute_value=True)),
             wait=([("VE", last_ev)] if cdo == 0 else ()), drain=(cdo == 7))
    S.do("vector", lambda e: e.tensor_scalar(A(c_g), A(c_g), 1e-12, None, ALU.max), drain=True)
    for _rc in range(8):
        S.do("vector", (lambda e, c=_rc: e.reciprocal(A(c_h)[:, c:c + 1], A(c_g)[:, c:c + 1])),
             drain=(_rc == 7))
    S.do("vector", lambda e: e.tensor_scalar(A(c_h), A(c_h), 127.0, None, ALU.mult), inc="VE",
         drain=True)
    vq = S.val("VE")
    if debug:
        S.dma("gpsimd", dbg["d_cg"], A(c_g), wait=[("VE", vq)], inc="sdbg")
        S.dma("gpsimd", dbg["d_ch"], A(c_h), wait=[("VE", vq)], inc="sdbg")
    for cdo in range(8):
        S.do("scalar",
             (lambda e, cdo=cdo: e.activation(A(wtmp)[:, 0:1024],
                                              A(wtmp2)[:, cdo * 1024:(cdo + 1) * 1024],
                                              AF.Identity, bias=0.0, scale=A(c_h)[:, cdo:cdo + 1])),
             wait=([("VE", vq)] if cdo == 0 else [("VE", S.val("VE"))]), inc="SE")
        sm = S.val("SE")
        S.do("vector", lambda e: e.tensor_copy(A(a_i), A(wtmp)[:, 0:1024]), wait=[("SE", sm)])
        S.do("vector", lambda e: e.tensor_tensor(A(wtmp)[:, 2048:3072], A(wtmp)[:, 0:1024], A(a_i), ALU.subtract))
        S.do("vector", lambda e: e.tensor_scalar(A(wtmp)[:, 1024:2048], A(wtmp)[:, 2048:3072], -0.5, None, ALU.is_le))
        S.do("vector", lambda e: e.scalar_tensor_tensor(A(wtmp)[:, 2048:3072], A(wtmp)[:, 2048:3072], 0.5,
                                                        A(wtmp)[:, 1024:2048], ALU.is_ge, ALU.subtract))
        S.do("vector",
             (lambda e, cdo=cdo: e.tensor_tensor(A(qrT)[:, cdo * 1024:(cdo + 1) * 1024],
                                                 A(wtmp)[:, 2048:3072], A(a_i), ALU.add)),
             inc="VE")
    vq2 = S.val("VE")
    if debug:
        S.dma("gpsimd", dbg["d_qrT"], A(qrT), wait=[("VE", vq2)], inc="sdbg")
    last_ev = None
    for g in range(8):
        wf = [("VE", vq2)]
        if last_ev is not None and g >= 1:
            wf.append(("VE", S.val("VE")))
        for j in range(8):
            idx = g * 8 + j
            do, di = idx // 8, idx % 8
            do = g
            di = j
            S.do("tensor",
                 (lambda e, do=do, di=di, j=j:
                  e.transpose(A(pTb)[:, j * 128:(j + 1) * 128],
                              A(qrT)[:, do * 1024 + di * 128: do * 1024 + (di + 1) * 128],
                              A(s_idb))),
                 wait=(wf if j == 0 else ()), inc=("TE" if j == 7 else None))
        tm = S.val("TE")
        for j in range(8):
            di = j
            do = g
            last_ev = S.do("vector",
                           (lambda e, di=di, do=do, j=j:
                            e.tensor_copy(A(q_rhs)[:, di * 1024 + do * 128: di * 1024 + (do + 1) * 128],
                                          A(pTb)[:, j * 128:(j + 1) * 128])),
                           wait=([("TE", tm)] if j == 0 else ()), inc="VE")
    S.do("vector", lambda e: e.tensor_scalar(A(c_g), A(c_g), 1.0 / 127.0, None, ALU.mult),
         inc="VE")
    v_rs = S.val("VE")
    t_st = S.dma("gpsimd", rs_bounce.rearrange("c p -> p c"), A(c_g), wait=[("VE", v_rs)], inc="sb", slow=True)
    t_ld = S.dma("sync", A(wtmp)[0:1, 3072:4096],
                 rs_bounce.rearrange("c p -> (c p)").rearrange("(a b) -> a b", a=1),
                 wait=[("sb", t_st)], inc="lw")
    S.do("tensor", lambda e: e.matmul(A(pX), A(s_ones), A(wtmp)[0:1, 3072:3584], start=True, stop=True),
         wait=[("lw", t_ld), ("VE", S.val("VE"))])
    S.do("tensor", lambda e: e.matmul(A(pA0), A(s_ones), A(wtmp)[0:1, 3584:4096], start=True, stop=True),
         inc="TE")
    tm = S.val("TE")
    S.do("vector", lambda e: e.tensor_copy(A(rs_bc)[:, 0:512], A(pX)), wait=[("TE", tm)])
    v_rsbc = S.do("vector", lambda e: e.tensor_copy(A(rs_bc)[:, 512:1024], A(pA0)), inc="VE")
    if debug:
        S.dma("gpsimd", dbg["d_rsbc"], A(rs_bc), wait=[("VE", v_rsbc)], inc="sdbg")
        S.dma("gpsimd", dbg["d_qrhs"], A(q_rhs), wait=[("VE", S.val("VE"))], inc="sdbg")
        S.dma("gpsimd", dbg["d_qrhs"] if "d_qrhs" in dbg else xout, A(q_rhs), inc="st") if "d_qrhs" in dbg else None

    t_wo = S.dma("sync", A(wtmp).rearrange("p (c d) -> p c d", c=8),
                 wo_d.rearrange("(c p) d -> p c d", p=128),
                 wait=[("VE", S.val("VE")), ("SE", S.val("SE"))], inc="lw")
    for h in range(2):
        wv_h = [("lw", t_wo)] if h == 0 else [("swo", S.val("swo"))]
        S.do("vector",
             (lambda e, h=h: e.tensor_copy(A(qrT)[:, 0:4096], A(wtmp)[:, h * 4096:(h + 1) * 4096])),
             wait=wv_h)
        v_h = S.do("vector",
                   (lambda e, h=h: e.tensor_tensor(A(qrT)[:, 4096:8192],
                                                   A(wtmp)[:, h * 4096:(h + 1) * 4096],
                                                   A(qrT)[:, 0:4096], ALU.subtract)), inc="VE")
        S.dma("gpsimd", wo_scr[0][:, h * 4096:(h + 1) * 4096], A(qrT)[:, 0:4096],
              wait=[("VE", v_h)], inc="swo")
        t_wl = S.dma("gpsimd", wo_scr[1][:, h * 4096:(h + 1) * 4096], A(qrT)[:, 4096:8192],
                     inc="swo")
    v_wo = S.val("VE")
    for dh in range(2):
        po = pA0 if dh == 0 else pA1
        for dv in range(8):
            S.do("tensor",
                 (lambda e, dv=dv, dh=dh, po=po:
                  e.matmul(A(po)[0:1, :], A(bvc)[:, dv:dv + 1],
                           A(wtmp)[:, dv * 1024 + dh * 512: dv * 1024 + dh * 512 + 512],
                           start=(dv == 0), stop=(dv == 7))),
                 wait=([("VE", v_wo), ("TE", S.val("TE"))] if (dv == 0 and dh == 0) else ()),
                 inc=("TE" if dv == 7 else None))
    t_bop = S.val("TE")
    S.do("vector", lambda e: e.tensor_copy(A(wtmp)[0:1, 0:512], A(pA0)[0:1, :]), wait=[("TE", t_bop)])
    S.do("vector", lambda e: e.tensor_copy(A(wtmp)[0:1, 512:1024], A(pA1)[0:1, :]), inc="VE")
    v_bop = S.val("VE")
    t_bb = S.dma("gpsimd", bo_bounce.rearrange("(a b) -> a b", a=1), A(wtmp)[0:1, 0:1024],
                 wait=[("VE", v_bop)], inc="sb")
    t_bl = S.dma("sync", A(bopc), bo_bounce.rearrange("(c p) -> p c", p=128), wait=[("sb", t_bb)],
                 slow=True, inc="lw")
    v_boc = S.do("vector", lambda e: e.tensor_tensor(A(bopc), A(bopc), A(boc), ALU.add),
                 wait=[("lw", t_bl)], inc="VE")

    setup_VE = S.val("VE"); setup_SE = S.val("SE"); setup_TE = S.val("TE")

    x_cons_tok = {}
    nhi_cons_tok = None
    kv_store_tok = None
    q_store_tok = None
    x1_store_tok = None

    proj_bank_free = None
    bfree = None
    x_cons_tok = {}
    trf_done = {}
    nhi_cons_tok = None
    kv_store_tok = {"skv0": None, "skv1": None, "sq0": None, "sq1": None}
    q_store_tok = None
    x1_store_tok = None
    pb_state = {"k_se": {}, "k_ve": {}, "v_se": {}, "v_ve": {}, "q_se": {}, "q_ve": {}}

    P_state = {}

    def emit_P_front(t, c):
        g = t * 4 + c
        par = g % 2
        xap = A(a_x)[:, par * D:(par + 1) * D]
        w = [("VE", setup_VE), ("SE", setup_SE), ("TE", setup_TE)] if g == 0 else []
        if g >= 2 and (g - 2) in x_cons_tok:
            w.append(("VE", x_cons_tok[g - 2]))
        t_x = S.dma("sync", xap, xin[t * TILE + c * 128: t * TILE + (c + 1) * 128, :], wait=w,
                    inc=("lx0" if par == 0 else "lx1"))
        emit_ln(xap, A(a_z2), wait_x_v=[("lx0" if par == 0 else "lx1", t_x)], sq=A(a_y))
        wq2 = [("SE", S.val("SE"))]
        vtok = emit_quant(A(a_z2), A(a_qtm), wq2, A(c_e)[:, 1:2],
                          ybuf=A(a_y), ibuf=A(a_i), dbuf=A(a_z2))
        P_state[(t, c)] = vtok

    def emit_P_back(t, c):
        nonlocal x1_store_tok
        g = t * 4 + c
        par = g % 2
        zpar = t % 2
        xap = A(a_x)[:, par * D:(par + 1) * D]
        vtok = P_state.pop((t, c))
        wtr = [("VE", vtok), ("VE", S.val("VE"))]
        for j in range(8):
            S.do("tensor",
                 (lambda e, j=j: e.transpose(A(pTb)[:, j * 128:(j + 1) * 128],
                                             A(a_qtm)[:, j * 128:(j + 1) * 128], A(s_idb))),
                 wait=(wtr if j == 0 else ()), inc=("TE" if j == 7 else None))
        tm = S.val("TE")
        qfm_tok = S.do("vector", lambda e: e.tensor_copy(A(a_qfm), A(pTb)), wait=[("TE", tm)],
                       inc="VE")
        for dh in range(2):
            pf = pF0 if dh == 0 else pF1
            for di in range(8):
                first = di == 0
                last = di == 7
                S.do("tensor",
                     (lambda e, di=di, dh=dh, pf=pf, first=first, last=last:
                      e.matmul(A(pf), A(a_qfm)[:, di * 128:(di + 1) * 128],
                               A(q_rhs)[:, di * 1024 + dh * 512: di * 1024 + dh * 512 + 512],
                               start=first, stop=last)),
                     wait=([("VE", qfm_tok)] if first else ()),
                     inc=("TE" if last else None))
        tf = S.val("TE")
        S.do("vector", lambda e: e.tensor_scalar(A(c_f)[:, 1:2], A(c_e)[:, 1:2], 0.5 / 127.0,
                                                 None, ALU.mult), drain=True)
        for dh in range(2):
            pf = pF0 if dh == 0 else pF1
            S.do("vector",
                 (lambda e, pf=pf, dh=dh:
                  e.scalar_tensor_tensor(A(a_y)[:, dh * 512:(dh + 1) * 512], A(pf),
                                         A(c_f)[:, 1:2], A(rs_bc)[:, dh * 512:(dh + 1) * 512],
                                         ALU.mult, ALU.mult)),
                 wait=([("TE", tf)] if dh == 0 else ()))
        xw = []
        if x1_store_tok is not None:
            xw.append(("sx1", x1_store_tok))
        xw.append(("SE", S.val("SE")))
        xe = S.do("vector",
                  (lambda e, xap=xap: e.tensor_tensor(A(a_x1), xap, A(a_y), ALU.add)),
                  wait=xw, inc="VE")
        x_cons_tok[g] = xe
        x1_store_tok = S.dma("gpsimd", x1s[t * 4 + c], A(a_x1), wait=[("VE", xe)], inc="sx1")
        if debug:
            x1_store_tok = S.dma("gpsimd", dbg["d_x1"][t * 4 + c], A(a_x1), inc="sx1")
        wz = [("VE", xe)]
        if (t - 2) in trf_done:
            wz.append(("TE", trf_done[t - 2]))
        emit_ln(A(a_x1), A(z2buf)[:, zpar * 4 * D + c * D: zpar * 4 * D + (c + 1) * D],
                wait_x_v=wz, sq=A(a_y))

    def emit_W_trf(t):
        nonlocal nhi_cons_tok
        zpar = t % 2
        bank_tok = {0: None, 1: None}
        for c in range(4):
            for gg in range(2):
                bank = pTf0 if gg == 0 else pTf1
                wf = [("SE", S.val("SE"))]
                if bank_tok[gg] is not None:
                    wf.append(("VE", bank_tok[gg]))
                if nhi_cons_tok is not None and c == 0 and gg == 0:
                    wf.append(("TE", nhi_cons_tok))
                if t == 0 and c == 0 and gg == 0:
                    wf.append(("swo", S.val("swo")))
                for j in range(4):
                    dv = gg * 4 + j
                    S.do("tensor",
                         (lambda e, dv=dv, j=j, bank=bank, c=c, zpar=zpar:
                          e.transpose(A(bank)[:, j * 128:(j + 1) * 128],
                                      A(z2buf)[:, zpar * 4 * D + c * D + dv * 128:
                                               zpar * 4 * D + c * D + (dv + 1) * 128],
                                      A(s_idf))),
                         wait=(wf if j == 0 else ()), inc=("TE" if j == 3 else None))
                tm2 = S.val("TE")
                dst_h = A(a_nhi).rearrange("p (dv t) -> p dv t", dv=NCH)[
                    :, gg * 4:(gg + 1) * 4, c * 128:(c + 1) * 128]
                dst_l = A(a_nlo).rearrange("p (dv t) -> p dv t", dv=NCH)[
                    :, gg * 4:(gg + 1) * 4, c * 128:(c + 1) * 128]
                S.do("scalar", (lambda e, d=dst_h, bank=bank: e.activation(d, A(bank), AF.Identity,
                                                                            bias=0.0, scale=1.0)),
                     wait=[("TE", tm2)], inc="SE")
                sh2 = S.val("SE")
                S.do("vector", (lambda e, d=dst_l, h=dst_h, bank=bank:
                                e.tensor_tensor(d, A(bank), h, ALU.subtract)),
                     wait=[("SE", sh2)], inc="VE")
                bank_tok[gg] = S.val("VE")
        trf_done[t] = S.val("TE")
        return S.val("VE")

    def emit_W_K(t, do_list, v_nsplit):
        st = pb_state
        for do in do_list:
            pa = pA0 if do % 2 == 0 else pA1
            wf = [("VE", v_nsplit)]
            if do >= 2:
                wf += [("SE", st["k_se"][do - 2]), ("VE", st["k_ve"][do - 2])]
            elif proj_bank_free is not None:
                wf += proj_bank_free[do % 2]
            pairs = [(A(wk_hi)[:, di * 1024 + do * 128: di * 1024 + (do + 1) * 128],
                      A(wk_lo)[:, di * 1024 + do * 128: di * 1024 + (do + 1) * 128],
                      A(a_nhi)[:, di * TILE:(di + 1) * TILE],
                      A(a_nlo)[:, di * TILE:(di + 1) * TILE]) for di in range(8)]
            mm3(A(pa), pairs, wait_first=wf, mark=True)
            tmk = S.val("TE")
            ssem = "skv0" if do % 2 == 0 else "skv1"
            slot = do % 2
            wk_ev = [("TE", tmk)]
            if kv_store_tok[ssem] is not None:
                wk_ev.append((ssem, kv_store_tok[ssem]))
            if kv_store_tok.get("sq%d" % slot) is not None:
                wk_ev.append(("sq%d" % slot, kv_store_tok["sq%d" % slot]))
            S.do("scalar",
                 (lambda e, pa=pa, do=do, slot=slot:
                  e.activation(A(a_kvhi)[:, slot * TILE:(slot + 1) * TILE], A(pa), AF.Identity,
                               bias=A(bkc)[:, do:do + 1], scale=1.0)),
                 wait=wk_ev, inc="SE")
            st["k_se"][do] = S.val("SE")
            S.do("vector",
                 (lambda e, pa=pa, do=do, slot=slot:
                  e.scalar_tensor_tensor(A(a_kvlo)[:, slot * TILE:(slot + 1) * TILE], A(pa),
                                         A(bkc)[:, do:do + 1],
                                         A(a_kvhi)[:, slot * TILE:(slot + 1) * TILE],
                                         ALU.add, ALU.subtract)),
                 wait=[("SE", st["k_se"][do])], inc="VE")
            st["k_ve"][do] = S.val("VE")
            S.dma("gpsimd",
                  contrib[t, 0:128, :].rearrange("p (qb dk k) -> p qb dk k", qb=4, dk=8)[:, :, do, :],
                  A(a_kvhi)[:, slot * TILE:(slot + 1) * TILE].rearrange("p (qb k) -> p qb k", qb=4),
                  wait=[("VE", st["k_ve"][do])], inc=ssem)
            kv_store_tok[ssem] = S.dma("gpsimd",
                  contrib[t, 128:256, :].rearrange("p (qb dk k) -> p qb dk k", qb=4, dk=8)[:, :, do, :],
                  A(a_kvlo)[:, slot * TILE:(slot + 1) * TILE].rearrange("p (qb k) -> p qb k", qb=4),
                  inc=ssem)

    def emit_W_V(t, tc_list, v_nsplit):
        st = pb_state
        for tc in tc_list:
            for dh in range(2):
                idx = tc * 2 + dh
                pa = pA0 if idx % 2 == 0 else pA1
                wf = []
                if idx >= 2:
                    wf += [("SE", st["v_se"][idx - 2]), ("VE", st["v_ve"][idx - 2])]
                else:
                    wf += [("SE", st["k_se"][6 + idx]), ("VE", st["k_ve"][6 + idx])]
                pairs = [(A(a_nhi)[:, di * TILE + tc * 128: di * TILE + (tc + 1) * 128],
                          A(a_nlo)[:, di * TILE + tc * 128: di * TILE + (tc + 1) * 128],
                          A(wv_hi)[:, di * 1024 + dh * 512: di * 1024 + dh * 512 + 512],
                          A(wv_lo)[:, di * 1024 + dh * 512: di * 1024 + dh * 512 + 512])
                         for di in range(8)]
                mm3(A(pa), pairs, wait_first=wf, mark=True)
                tmk = S.val("TE")
                ssem = "skv0" if idx % 2 == 0 else "skv1"
                slot = idx % 2
                wv_ev = [("TE", tmk)]
                if kv_store_tok[ssem] is not None:
                    wv_ev.append((ssem, kv_store_tok[ssem]))
                S.do("scalar",
                     (lambda e, pa=pa, slot=slot:
                      e.activation(A(a_kvhi)[:, slot * TILE:(slot + 1) * TILE], A(pa),
                                   AF.Identity, bias=0.0, scale=1.0)),
                     wait=wv_ev, inc="SE")
                st["v_se"][idx] = S.val("SE")
                S.do("vector",
                     (lambda e, pa=pa, slot=slot:
                      e.tensor_tensor(A(a_kvlo)[:, slot * TILE:(slot + 1) * TILE], A(pa),
                                      A(a_kvhi)[:, slot * TILE:(slot + 1) * TILE],
                                      ALU.subtract)),
                     wait=[("SE", st["v_se"][idx])], inc="VE")
                st["v_ve"][idx] = S.val("VE")
                S.dma("gpsimd", contrib[t, 256:384, tc * D + dh * 512: tc * D + dh * 512 + 512],
                      A(a_kvhi)[:, slot * TILE:(slot + 1) * TILE],
                      wait=[("VE", st["v_ve"][idx])], inc=ssem)
                kv_store_tok[ssem] = S.dma("gpsimd", contrib[t, 384:512, tc * D + dh * 512: tc * D + dh * 512 + 512],
                                           A(a_kvlo)[:, slot * TILE:(slot + 1) * TILE], inc=ssem)

    def emit_W_Q(t, v_nsplit):
        nonlocal nhi_cons_tok, q_store_tok, proj_bank_free
        st = pb_state
        for do in range(8):
            pa = pA0 if do % 2 == 0 else pA1
            wf = []
            if do >= 2:
                wf += [("SE", st["q_se"][do - 2]), ("VE", st["q_ve"][do - 2])]
            else:
                wf += [("SE", st["v_se"][6 + do]), ("VE", st["v_ve"][6 + do])]
            pairs = [(A(wq_hi)[:, di * 1024 + do * 128: di * 1024 + (do + 1) * 128],
                      A(wq_lo)[:, di * 1024 + do * 128: di * 1024 + (do + 1) * 128],
                      A(a_nhi)[:, di * TILE:(di + 1) * TILE],
                      A(a_nlo)[:, di * TILE:(di + 1) * TILE]) for di in range(8)]
            nhi_cons_tok = mm3(A(pa), pairs, wait_first=wf, mark=True)
            tmk = S.val("TE")
            ssem = "sq0" if do % 2 == 0 else "sq1"
            slot = do % 2
            wq_ev = [("TE", tmk), ("VE", S.val("VE"))]
            if kv_store_tok["skv%d" % slot] is not None:
                wq_ev.append(("skv%d" % slot, kv_store_tok["skv%d" % slot]))
            if kv_store_tok.get(ssem) is not None:
                wq_ev.append((ssem, kv_store_tok[ssem]))
            S.do("scalar",
                 (lambda e, do=do, pa=pa: e.activation(A(a_qf), A(pa), AF.Identity,
                                                       bias=A(bq32c)[:, do:do + 1], scale=1.0 / 32.0)),
                 wait=wq_ev, inc="SE")
            st["q_se"][do] = S.val("SE")
            sm = S.val("SE")
            S.do("vector", (lambda e, slot=slot: e.tensor_copy(A(a_kvhi)[:, slot * TILE:(slot + 1) * TILE],
                                                               A(a_qf))),
                 wait=[("SE", sm)])
            S.do("vector", (lambda e, slot=slot: e.tensor_tensor(A(a_kvlo)[:, slot * TILE:(slot + 1) * TILE],
                                                                 A(a_qf),
                                                                 A(a_kvhi)[:, slot * TILE:(slot + 1) * TILE],
                                                                 ALU.subtract)),
                 inc="VE")
            st["q_ve"][do] = S.val("VE")
            S.dma("gpsimd",
                  qx_hi[t].rearrange("p (cc dk j) -> p cc dk j", cc=4, dk=8)[:, :, do, :],
                  A(a_kvhi)[:, slot * TILE:(slot + 1) * TILE].rearrange("p (cc j) -> p cc j", cc=4),
                  wait=[("VE", st["q_ve"][do])], inc=ssem)
            kv_store_tok[ssem] = S.dma("gpsimd",
                  qx_lo[t].rearrange("p (cc dk j) -> p cc dk j", cc=4, dk=8)[:, :, do, :],
                  A(a_kvlo)[:, slot * TILE:(slot + 1) * TILE].rearrange("p (cc j) -> p cc j", cc=4),
                  inc=ssem)
        proj_bank_free = {0: [("SE", st["q_se"][6]), ("VE", st["q_ve"][6])],
                          1: [("SE", st["q_se"][7]), ("VE", st["q_ve"][7])]}

    REPL = [[0, 1, 2, 3], [4, 5, 6, 7]]
    cc_tok = {}
    for c in range(4):
        emit_P_front(0, c)
        emit_P_back(0, c)
    emit_P_front(1, 0) if NS > 1 else None
    for t in range(NS):
        pb_state = {"k_se": {}, "k_ve": {}, "v_se": {}, "v_ve": {}, "q_se": {}, "q_ve": {}}
        nxt = t + 1 < NS
        v_nsplit = emit_W_trf(t)
        if nxt:
            emit_P_back(t + 1, 0)
            emit_P_front(t + 1, 1)
        emit_W_K(t, [0, 1, 2, 3], v_nsplit)
        if nxt:
            emit_P_back(t + 1, 1)
            emit_P_front(t + 1, 2)
        emit_W_K(t, [4, 5, 6, 7], v_nsplit)
        if nxt:
            emit_P_back(t + 1, 2)
            emit_P_front(t + 1, 3)
        emit_W_V(t, [0, 1], v_nsplit)
        if nxt:
            emit_P_back(t + 1, 3)
        emit_W_V(t, [2, 3], v_nsplit)
        if t == NS - 1:
            bfree = [("TE", S.val("TE")), ("VE", S.val("VE")), ("SE", S.val("SE")),
                     ("sx1", S.val("sx1"))]
        if t + 2 < NS:
            emit_P_front(t + 2, 0)
        cc_tok[t] = S.do("gpsimd",
                         (lambda e, t=t: e.collective_compute(
                             "AllGather", ALU.bypass, replica_groups=REPL,
                             ins=[contrib[t]], outs=[gath[t]])),
                         wait=[("skv0", S.val("skv0")), ("skv1", S.val("skv1"))],
                         inc="cc")
        emit_W_Q(t, v_nsplit)

    A_TE = S.val("TE"); A_VE = S.val("VE"); A_SE = S.val("SE")

    bwait = [("TE", A_TE), ("VE", A_VE), ("SE", A_SE),
             ("skv0", S.val("skv0")), ("skv1", S.val("skv1")),
             ("sq0", S.val("sq0")), ("sq1", S.val("sq1")), ("sx1", S.val("sx1"))]
    sq_totals = [("sq0", S.val("sq0")), ("sq1", S.val("sq1")), ("sx1", S.val("sx1"))]

    kv_ld_tok = {}
    q_ld_tok = {}
    it_idx = 0
    ctx_ev_tok = {}
    exp_tok = {}
    pt_tok = {}
    kt_last_iter = {}
    ctx_store = {}

    def load_kv(kt, extra_wait=()):
        buf = kt % 3
        sg, cc_ = kt // 4, kt % 4
        w = list(bwait) if kt == 0 else list(extra_wait)
        w.append(("cc", cc_tok[sg]))
        sem = ("lk0", "lk1", "lk2")[buf]
        t4 = None
        for r in range(4):
            S.dma("gpsimd", A(b_khi)[:, buf * NCH * TILE + r * 1024: buf * NCH * TILE + (r + 1) * 1024],
                  gath[sg, r * 512 + 0:r * 512 + 128, cc_ * 1024:(cc_ + 1) * 1024],
                  wait=(w if r == 0 else ()), inc=sem)
            S.dma("gpsimd", A(b_klo)[:, buf * NCH * TILE + r * 1024: buf * NCH * TILE + (r + 1) * 1024],
                  gath[sg, r * 512 + 128:r * 512 + 256, cc_ * 1024:(cc_ + 1) * 1024], inc=sem)
            S.dma("gpsimd", A(b_vhi)[:, buf * 4 * D + r * D:(buf * 4 + r + 1) * D],
                  gath[sg, r * 512 + 256:r * 512 + 384, cc_ * D:(cc_ + 1) * D], inc=sem)
            t4 = S.dma("gpsimd", A(b_vlo)[:, buf * 4 * D + r * D:(buf * 4 + r + 1) * D],
                       gath[sg, r * 512 + 384:r * 512 + 512, cc_ * D:(cc_ + 1) * D], inc=sem)
        kv_ld_tok[kt] = (sem, t4)

    iters = [(kt, k) for kt in range(nt) for k in range(kt, nt)]
    for kt in range(nt):
        kt_last_iter[kt] = max(i for i, (a, b) in enumerate(iters) if a == kt)

    load_kv(0)

    n_it = len(iters)
    t_qk_tok = {}
    t_tr_tok = {}
    t_pv_tok = {}

    def front(i):
        kt, k = iters[i]
        qpar = i % 2
        p4 = i % 4
        wq_ = []
        if i >= 2:
            wq_.append(("SE", exp_tok[i - 2]))
        else:
            wq_.append(("TE", trf_done[NS - 1]))
            wq_ += sq_totals
        qsem = "lq0" if qpar == 0 else "lq1"
        sgq, ccq = k // 4, k % 4
        S.dma("sync",
              A(b_qhi)[:, qpar * NCH * QCH:(qpar + 1) * NCH * QCH],
              qx_hi[sgq][:, ccq * 1024:(ccq + 1) * 1024],
              wait=wq_, inc=qsem)
        t2 = S.dma("sync",
                   A(b_qlo)[:, qpar * NCH * QCH:(qpar + 1) * NCH * QCH],
                   qx_lo[sgq][:, ccq * 1024:(ccq + 1) * 1024],
                   inc=qsem)
        q_ld_tok[i] = (qsem, t2)

        par = i % 2
        buf = kt % 3
        pS = pS0 if par == 0 else pS1
        wf = [kv_ld_tok[kt], q_ld_tok[i]]
        if i >= 2:
            wf.append(("SE", exp_tok[i - 2]))
        khv = A(b_khi)[:, buf * NCH * TILE:(buf + 1) * NCH * TILE].rearrange(
            "p (r di k) -> p r di k", r=4, di=NCH)
        klv = A(b_klo)[:, buf * NCH * TILE:(buf + 1) * NCH * TILE].rearrange(
            "p (r di k) -> p r di k", r=4, di=NCH)
        pairs = [(A(b_qhi)[:, qpar * NCH * QCH + di * 128: qpar * NCH * QCH + (di + 1) * 128],
                  A(b_qlo)[:, qpar * NCH * QCH + di * 128: qpar * NCH * QCH + (di + 1) * 128],
                  khv[:, :, di, :], klv[:, :, di, :])
                 for di in range(8)]
        mm3(A(pS), pairs, wait_first=wf, mark=True)
        t_qk_tok[i] = S.val("TE")

        wv_ = [("TE", t_qk_tok[i])]
        if kt == k:
            S.do("vector", lambda e, pS=pS: e.tensor_tensor(A(pS), A(pS), A(mask), ALU.add),
                 wait=wv_, inc="VE")
            wv_ = [("VE", S.val("VE"))]
        if i >= 1 and iters[i - 1][1] == k:
            wv_ = list(wv_) + [("SE", exp_tok[i - 1])]
        S.do("vector", lambda e, pS=pS: e.reduce_max(A(c_a)[:, 0:1], A(pS), axis=AX.X, negate=True),
             wait=wv_, drain=True)
        if kt == 0:
            S.do("vector", (lambda e, k=k: e.tensor_copy(A(nm_col)[:, k:k + 1], A(c_a)[:, 0:1])),
                 inc="VE", drain=True)
        else:
            S.do("vector", (lambda e, k=k: e.tensor_tensor(A(c_b)[:, 0:1], A(nm_col)[:, k:k + 1],
                                                           A(c_a)[:, 0:1], ALU.min)), drain=True)
            S.do("vector", (lambda e, k=k, p4=p4: e.tensor_tensor(A(c_c)[:, p4:p4 + 1], A(c_b)[:, 0:1],
                                                                  A(nm_col)[:, k:k + 1], ALU.subtract)))
            S.do("vector", (lambda e, k=k: e.tensor_copy(A(nm_col)[:, k:k + 1], A(c_b)[:, 0:1])),
                 inc="VE", drain=True)
        v_nm = S.val("VE")
        if kt > 0:
            S.do("scalar", (lambda e, p4=p4: e.activation(A(c_d)[:, p4:p4 + 1], A(c_c)[:, p4:p4 + 1],
                                                          AF.Exp, bias=0.0, scale=1.0)),
                 wait=[("VE", v_nm)], drain=True)
        exp_tok[i] = S.do("scalar",
                          (lambda e, pS=pS, k=k, par=par, p4=p4:
                           e.activation(A(b_p)[:, par * TILE:(par + 1) * TILE], A(pS), AF.Exp,
                                        bias=A(nm_col)[:, k:k + 1], scale=1.0,
                                        accum_out=A(c_e)[:, p4:p4 + 1])),
                          wait=[("VE", v_nm)], inc="SE", drain=True)

    def back_pre(i):
        kt, k = iters[i]
        par = i % 2
        p4 = i % 4
        pPT = pPT0 if par == 0 else pPT1
        wtr = [("SE", exp_tok[i])]
        if i >= 2:
            wtr.append(("VE", pt_tok[i - 2]))
        for j in range(4):
            S.do("tensor",
                 (lambda e, j=j, pPT=pPT, par=par:
                  e.transpose(A(pPT)[:, j * 128:(j + 1) * 128],
                              A(b_p)[:, par * TILE + j * 128: par * TILE + (j + 1) * 128], A(s_idf))),
                 wait=(wtr if j == 0 else ()), inc=("TE" if j == 3 else None))
        t_tr_tok[i] = S.val("TE")
        S.do("vector", lambda e, pPT=pPT: e.tensor_copy(A(b_pthi), A(pPT)),
             wait=[("TE", t_tr_tok[i])])
        pt_tok[i] = S.do("vector", lambda e, pPT=pPT: e.tensor_tensor(A(b_ptlo), A(pPT), A(b_pthi),
                                                                      ALU.subtract), inc="VE")
        if kt == 0:
            S.do("vector", (lambda e, k=k, p4=p4: e.tensor_copy(A(l_col)[:, k:k + 1],
                                                                A(c_e)[:, p4:p4 + 1])),
                 wait=[("SE", exp_tok[i])], inc="VE")
        else:
            S.do("vector",
                 (lambda e, k=k, p4=p4: e.scalar_tensor_tensor(A(l_col)[:, k:k + 1],
                                                               A(l_col)[:, k:k + 1],
                                                               A(c_d)[:, p4:p4 + 1],
                                                               A(c_e)[:, p4:p4 + 1],
                                                               ALU.mult, ALU.add)),
                 wait=[("SE", exp_tok[i])], inc="VE")

    def back_post(i):
        kt, k = iters[i]
        par = i % 2
        p4 = i % 4
        buf = kt % 3
        pC = pC0 if par == 0 else pC1
        wpv = [("VE", pt_tok[i])]
        if i >= 2:
            wpv.append(("VE", ctx_ev_tok[i - 2]))
        for dh in range(2):
            pairs = [(A(b_pthi)[:, kc * 128:(kc + 1) * 128],
                      A(b_ptlo)[:, kc * 128:(kc + 1) * 128],
                      A(b_vhi)[:, buf * 4 * D + kc * D + dh * 512: buf * 4 * D + kc * D + dh * 512 + 512],
                      A(b_vlo)[:, buf * 4 * D + kc * D + dh * 512: buf * 4 * D + kc * D + dh * 512 + 512])
                     for kc in range(4)]
            mm3(A(pC)[:, dh * 512:(dh + 1) * 512], pairs,
                wait_first=(wpv if dh == 0 else ()), mark=(dh == 1))
        t_pv_tok[i] = S.val("TE")
        if kt == 0:
            ctx_ev_tok[i] = S.do("vector",
                                 (lambda e, k=k, pC=pC: e.tensor_copy(A(b_ctx)[:, k * D:(k + 1) * D],
                                                                      A(pC))),
                                 wait=[("TE", t_pv_tok[i])], inc="VE")
        else:
            ctx_ev_tok[i] = S.do("vector",
                                 (lambda e, k=k, pC=pC, p4=p4:
                                  e.scalar_tensor_tensor(A(b_ctx)[:, k * D:(k + 1) * D],
                                                         A(b_ctx)[:, k * D:(k + 1) * D],
                                                         A(c_d)[:, p4:p4 + 1], A(pC),
                                                         ALU.mult, ALU.add)),
                                 wait=[("TE", t_pv_tok[i])], inc="VE")
        if kt == k:
            ctx_store[k] = S.dma("gpsimd", ctxs[k], A(b_ctx)[:, k * D:(k + 1) * D],
                                 wait=[("VE", ctx_ev_tok[i])], inc="sc")
            if debug:
                S.dma("gpsimd", dbg["d_ctx"][k], A(b_ctx)[:, k * D:(k + 1) * D], inc="sc")
        if i == kt_last_iter[kt] and kt + 3 < nt:
            load_kv(kt + 3, extra_wait=[("TE", t_pv_tok[i])])

    front(0)
    if nt > 1:
        load_kv(1)
    if nt > 2:
        load_kv(2)
    if n_it > 1:
        front(1)
    for j in range(n_it):
        back_pre(j)
        if j + 2 < n_it:
            front(j + 2)
        back_post(j)

    if debug:
        S.dma("gpsimd", dbg["d_l"], A(l_col)[:, 0:nt], wait=[("VE", S.val("VE"))], inc="sc")

    B_TE = S.val("TE"); B_VE = S.val("VE"); B_SE = S.val("SE")

    cwait = [("TE", B_TE), ("VE", B_VE), ("SE", B_SE), ("sc", S.val("sc"))]
    w_wo = [("sc", ctx_store[7])] if nt >= 8 else list(cwait)
    t_wo2 = S.dma("sync", A(wo_hi), wo_scr[0], wait=w_wo, inc="lw")
    t_wo3 = S.dma("sync", A(wo_lo), wo_scr[1], inc="lw")
    for _rc in range(nt):
        S.do("vector", (lambda e, c=_rc: e.reciprocal(A(li_col)[:, c:c + 1], A(l_col)[:, c:c + 1])),
             wait=(list(cwait) if _rc == 0 else ()), drain=(_rc == nt - 1))
    v_li = S.do("vector", lambda e: e.tensor_copy(A(c_a)[:, 0:1], A(li_col)[:, 0:1]), inc="VE")

    def pairs_o(do):
        return [(A(wo_hi)[:, dv * 1024 + do * 128: dv * 1024 + (do + 1) * 128],
                 A(wo_lo)[:, dv * 1024 + do * 128: dv * 1024 + (do + 1) * 128],
                 A(c_fhi)[:, dv * TILE:(dv + 1) * TILE],
                 A(c_flo)[:, dv * TILE:(dv + 1) * TILE]) for dv in range(8)]

    out_store = None
    ng = nt // 4
    C_so = {}
    C_x2ve = {}
    C_sy = {}
    C_ytrve = {}
    C2_state = {}

    def _ytr(gi, ccs):
        par = gi % 2
        s_y = C_sy[gi]
        for cc in ccs:
            for gg in range(2):
                bank = pCT0 if gg == 0 else pCT1
                prev = C2_state.get(("ytr", gi, gg))
                wv0 = [("SE", s_y)] + ([("VE", prev)] if prev is not None
                                       else [("VE", S.val("VE"))])
                for j in range(4):
                    do = gg * 4 + j
                    S.do("tensor",
                         (lambda e, do=do, j=j, cc=cc, bank=bank:
                          e.transpose(A(bank)[:, j * 128:(j + 1) * 128],
                                      A(c_y)[:, do * TILE + cc * 128: do * TILE + (cc + 1) * 128],
                                      A(s_idf))),
                         wait=(wv0 if j == 0 else ()), inc=("TE" if j == 3 else None))
                tm2 = S.val("TE")
                wadd = [("TE", tm2)]
                if gg == 0 and cc == 0 and (gi - 2) in C_so:
                    wadd.append(("so", C_so[gi - 2]))
                x2v = A(c_x2)[:, par * 4 * D + cc * D + gg * 512:
                              par * 4 * D + cc * D + (gg + 1) * 512]
                x1v = A(c_x1g)[:, par * 4 * D + cc * D + gg * 512:
                               par * 4 * D + cc * D + (gg + 1) * 512]
                C2_state[("ytr", gi, gg)] = S.do("vector",
                     (lambda e, x2v=x2v, x1v=x1v, bank=bank:
                      e.tensor_tensor(x2v, A(bank), x1v, ALU.add)),
                     wait=wadd, inc="VE")
            C_x2ve[(gi, cc)] = S.val("VE")

    def emit_C1(gi, part):
        par = gi % 2
        if part == 0:
            w0 = [("VE", v_li)] if gi == 0 else []
            if gi >= 2 and (gi - 2) in C_so:
                w0.append(("so", C_so[gi - 2]))
            tok_c = None
            for cc in range(4):
                k = gi * 4 + cc
                tok_c = S.dma("sync", A(c_ctx)[:, par * 4 * D + cc * D: par * 4 * D + (cc + 1) * D],
                              ctxs[k], wait=(w0 if cc == 0 else ()), inc="lg")
            for cc in range(4):
                k = gi * 4 + cc
                tok_c = S.dma("sync", A(c_x1g)[:, par * 4 * D + cc * D: par * 4 * D + (cc + 1) * D],
                              x1s[k], inc="lg")
            C2_state[("lg", gi)] = tok_c
        cc_list = ((0,), (1,), (2, 3))[part] if part in (0, 1, 2) else ()
        for cc in cc_list:
            k = gi * 4 + cc
            wv0 = [("lg", C2_state[("lg", gi)])]
            if cc == 0 and (gi - 1) in C_ytrve:
                wv0.append(("VE", C_ytrve[gi - 1]))
            v_n = S.do("vector",
                       (lambda e, cc=cc, k=k, par=par:
                        e.tensor_scalar(A(c_ctxn), A(c_ctx)[:, par * 4 * D + cc * D: par * 4 * D + (cc + 1) * D],
                                        A(li_col)[:, k:k + 1], None, ALU.mult)),
                       wait=wv0, inc="VE")
            for gg in range(2):
                bank = pCT0 if gg == 0 else pCT1
                for j in range(4):
                    dv = gg * 4 + j
                    S.do("tensor",
                         (lambda e, dv=dv, j=j, bank=bank:
                          e.transpose(A(bank)[:, j * 128:(j + 1) * 128],
                                      A(c_ctxn)[:, dv * 128:(dv + 1) * 128], A(s_idf))),
                         wait=([("VE", v_n), ("VE", S.val("VE"))] if j == 0 else ()),
                         inc=("TE" if j == 3 else None))
                tm2 = S.val("TE")
                dst_h = A(c_fhi).rearrange("p (dv t) -> p dv t", dv=NCH)[
                    :, gg * 4:(gg + 1) * 4, cc * 128:(cc + 1) * 128]
                dst_l = A(c_flo).rearrange("p (dv t) -> p dv t", dv=NCH)[
                    :, gg * 4:(gg + 1) * 4, cc * 128:(cc + 1) * 128]
                S.do("vector", (lambda e, d=dst_h, bank=bank: e.tensor_copy(d, A(bank))),
                     wait=[("TE", tm2)])
                S.do("vector", (lambda e, d=dst_l, h=dst_h, bank=bank:
                                e.tensor_tensor(d, A(bank), h, ALU.subtract)), inc="VE")
        if part == "3a":
            par = gi % 2
            v_fsplit = S.val("VE")
            o_se = {}
            for do in range(8):
                po = pO0 if do % 2 == 0 else pO1
                wfo = [("VE", v_fsplit)] if do == 0 else []
                if do == 0 and gi == 0:
                    wfo.append(("lw", t_wo3))
                if do == 0 and (gi - 1) in C_sy:
                    wfo.append(("SE", C_sy[gi - 1]))
                if do >= 2:
                    wfo.append(("SE", o_se[do - 2]))
                mm3(A(po), pairs_o(do), wait_first=wfo, mark=True)
                tmk = S.val("TE")
                S.do("scalar",
                     (lambda e, do=do, po=po: e.activation(A(c_y)[:, do * TILE:(do + 1) * TILE], A(po),
                                                           AF.Identity, bias=A(bopc)[:, do:do + 1],
                                                           scale=1.0)),
                     wait=[("TE", tmk)], inc="SE")
                o_se[do] = S.val("SE")
            C_sy[gi] = S.val("SE")
            _ytr(gi, (0,))
        if part == "3b":
            _ytr(gi, (1,))
        if part == "3c":
            _ytr(gi, (2, 3))
            C_x2ve[gi] = S.val("VE")
            C_ytrve[gi] = S.val("VE")

    def emit_C2f(gi, cc):
        par = gi % 2
        sp = cc % 2
        x2ap = A(c_x2)[:, par * 4 * D + cc * D: par * 4 * D + (cc + 1) * D]
        wx = [("VE", C_x2ve[(gi, cc)])]
        if (gi, cc, "ffn_te") in C2_state:
            pass
        prev = (gi, cc - 2) if cc >= 2 else ((gi - 1, cc + 2) if gi > 0 else None)
        if prev is not None and ("b_te", prev[0], prev[1]) in C2_state:
            wx.append(("TE", C2_state[("b_te", prev[0], prev[1])]))
        z3t = emit_ln(x2ap, A(c_z), wait_x_v=wx, sq=A(c_yq))
        vtok = emit_quant(A(c_z), A(c_qtm)[:, sp * D:(sp + 1) * D],
                          [("SE", z3t)], A(c_e)[:, 2 + sp:3 + sp],
                          ybuf=A(c_yq), ibuf=A(c_ii), dbuf=A(c_z))
        C2_state[(gi, cc)] = vtok

    def emit_C2b(gi, cc):
        nonlocal out_store
        par = gi % 2
        sp = cc % 2
        k = gi * 4 + cc
        x2ap = A(c_x2)[:, par * 4 * D + cc * D: par * 4 * D + (cc + 1) * D]
        vtok = C2_state.pop((gi, cc))
        wtr = [("VE", vtok), ("VE", S.val("VE"))]
        for j in range(8):
            S.do("tensor",
                 (lambda e, j=j, sp=sp: e.transpose(A(pCb)[:, sp * 1024 + j * 128: sp * 1024 + (j + 1) * 128],
                                                    A(c_qtm)[:, sp * D + j * 128: sp * D + (j + 1) * 128],
                                                    A(s_idb))),
                 wait=(wtr if j == 0 else ()), inc=("TE" if j == 7 else None))
        tm = S.val("TE")
        qfm_tok = S.do("vector", (lambda e, sp=sp: e.tensor_copy(A(c_qfm),
                                                                 A(pCb)[:, sp * 1024:(sp + 1) * 1024])),
                       wait=[("TE", tm)], inc="VE")
        wffn = [("VE", qfm_tok)]
        if ("epi_ve",) in C2_state:
            wffn.append(("VE", C2_state[("epi_ve",)]))
        for dh in range(2):
            for di in range(8):
                first = di == 0
                last = di == 7
                S.do("tensor",
                     (lambda e, di=di, dh=dh, first=first, last=last:
                      e.matmul(A(pCF0)[:, dh * 512:(dh + 1) * 512],
                               A(c_qfm)[:, di * 128:(di + 1) * 128],
                               A(q_rhs)[:, di * 1024 + dh * 512: di * 1024 + dh * 512 + 512],
                               start=first, stop=last)),
                     wait=(wffn if (first and dh == 0) else ()),
                     inc=("TE" if (last and dh == 1) else None))
        tf = S.val("TE")
        C2_state[("b_te", gi, cc)] = tf
        S.do("vector", (lambda e, sp=sp: e.tensor_scalar(A(c_f)[:, 1:2], A(c_e)[:, 2 + sp:3 + sp],
                                                         0.5 / 127.0, None, ALU.mult)), drain=True)
        for dh in range(2):
            S.do("vector",
                 (lambda e, dh=dh:
                  e.scalar_tensor_tensor(A(c_yq)[:, dh * 512:(dh + 1) * 512],
                                         A(pCF0)[:, dh * 512:(dh + 1) * 512],
                                         A(c_f)[:, 1:2], A(rs_bc)[:, dh * 512:(dh + 1) * 512],
                                         ALU.mult, ALU.mult)),
                 wait=([("TE", tf)] if dh == 0 else ()))
        v_o = S.do("vector",
                   (lambda e, x2ap=x2ap: e.tensor_tensor(x2ap, x2ap, A(c_yq), ALU.add)),
                   wait=[("SE", S.val("SE"))], inc="VE")
        C2_state[("epi_ve",)] = v_o
        out_store = S.dma("gpsimd", xout[k * 128:(k + 1) * 128, :], x2ap,
                          wait=[("VE", v_o)], inc="so")
        C_so[gi] = out_store

    for part in (0, 1, 2, "3a"):
        emit_C1(0, part)
    emit_C2f(0, 0)
    emit_C1(0, "3b")
    emit_C2f(0, 1)
    emit_C1(0, "3c")
    for gi in range(ng):
        for cc in range(4):
            emit_C2b(gi, cc)
            if gi + 1 < ng:
                emit_C1(gi + 1, cc if cc < 3 else "3a")
            if cc < 2:
                emit_C2f(gi, cc + 2)
        if gi + 1 < ng:
            emit_C2f(gi + 1, 0)
            emit_C1(gi + 1, "3b")
            emit_C2f(gi + 1, 1)
            emit_C1(gi + 1, "3c")

    final_w = [(s, S.val(s)) for s in ("so", "sc", "skv0", "skv1", "sq0", "sq1", "sx1", "sb", "sdbg") if S.val(s) > 0]
    S.do("gpsimd", lambda e: None, wait=final_w)

    import contextlib
    with contextlib.ExitStack() as stack:
        sems = {}
        for name in sorted(S.cnt.keys()):
            sems[name] = stack.enter_context(nc.semaphore(name))
        if guard:
            for name in ("ovector", "oscalar"):
                sems[name] = stack.enter_context(nc.semaphore(name))
        block = stack.enter_context(nc.Block())

        @block.sync
        def _(eng):
            S.replay("sync", eng, sems, guard=guard)

        @block.tensor
        def _(eng):
            S.replay("tensor", eng, sems, guard=guard)

        @block.vector
        def _(eng):
            S.replay("vector", eng, sems, guard=guard)

        @block.scalar
        def _(eng):
            S.replay("scalar", eng, sems, guard=guard)

        @block.gpsimd
        def _(eng):
            S.replay("gpsimd", eng, sems, guard=guard)

    return nc


def _host_inputs(x_full, Wq, bq, Wk, bk, Wv, bv, Wo, bo, Frhs, nt):
    import ml_dtypes
    T = nt * TILE
    ident = np.eye(128, dtype=np.float32)
    ones = np.ones((1, 128), np.float32)
    in_maps = []
    for c in range(8):
        s, r = c // 4, c % 4
        rows = (np.arange(nt)[:, None] * TILE + r * 128 + np.arange(128)[None, :]).ravel()
        xs = np.ascontiguousarray(x_full[s][rows])
        m = np.full((QCH, TILE), NEG, np.float32)
        i_idx = np.arange(QCH)[:, None]
        j_idx = np.arange(QCH)[None, :]
        for j in range(4):
            if j < r:
                m[:, j * 128:(j + 1) * 128] = 0.0
            elif j == r:
                m[:, j * 128:(j + 1) * 128] = np.where(j_idx <= i_idx, 0.0, NEG)
        in_maps.append(dict(
            xin=xs, wq=Wq, wk=Wk, wv=Wv, wo=Wo, frhs=Frhs,
            bq=bq, bk=bk, bv=bv, bo=bo,
            maskin=m, idf=ident, idb=ident.astype(ml_dtypes.bfloat16), ones_col=ones,
        ))
    return in_maps


def _host_gather(results, nt):
    T = nt * TILE
    out = np.empty((2, T, D), np.float32)
    for c, r_ in enumerate(results):
        s, r = c // 4, c % 4
        own = np.asarray(r_["xout"])
        rows = (np.arange(nt)[:, None] * TILE + r * 128 + np.arange(128)[None, :]).ravel()
        out[s][rows] = own
    return out


def kernel(L0_attn_y_all, L0_attn_Wq, L0_attn_bq, L0_attn_Wk, L0_attn_bk,
           L0_attn_Wv, L0_attn_bv, L0_attn_Wo, L0_attn_bo, L0_ffn_rhs):
    nt = 16
    x = np.ascontiguousarray(np.asarray(L0_attn_y_all, dtype=np.float32))
    nc = build_core_program(nt)
    in_maps = _host_inputs(x, np.asarray(L0_attn_Wq, np.float32), np.asarray(L0_attn_bq, np.float32),
                           np.asarray(L0_attn_Wk, np.float32), np.asarray(L0_attn_bk, np.float32),
                           np.asarray(L0_attn_Wv, np.float32), np.asarray(L0_attn_bv, np.float32),
                           np.asarray(L0_attn_Wo, np.float32), np.asarray(L0_attn_bo, np.float32),
                           np.asarray(L0_ffn_rhs, np.float32), nt)
    res = run_bass_kernel_spmd(nc, in_maps, core_ids=list(range(8)))
    return _host_gather(res.results, nt)
